# version 01
# baseline (speedup 1.0000x reference)
import numpy as np
import ml_dtypes
import concourse.bass as bass
import concourse.mybir as mybir
from concourse.bass_utils import run_bass_kernel_spmd

F32 = mybir.dt.float32
BF16 = mybir.dt.bfloat16
I32 = mybir.dt.int32
AF = mybir.ActivationFunctionType
OP = mybir.AluOpType
AX = mybir.AxisListType
BF = ml_dtypes.bfloat16

B, L, C = 4, 4096, 1024
H, D = 16, 64
NT = L // 128
NTO = 2048 // 128
FI = 2816
NI = FI // 128
NC = C // 128
QMAX = 127.0
EPS = 1e-5

ENGINES = ("sync", "tensor", "vector", "scalar", "pool")
_OWN = {"sync": "ld", "tensor": "tt", "vector": "tv", "scalar": "ta", "pool": "tp"}
SEMS = ("ld", "tt", "tv", "ta", "tp", "cc", "rd")


class Prog:
    def __init__(self):
        self.events = []
        self.cnt = {k: 0 for k in SEMS}
        self._gates = {}

    def now(self, sem):
        return self.cnt[sem]

    def _gate(self, owner, sem, v):
        key = (sem, v)
        if key in self._gates:
            return self._gates[key]
        full = self.cnt[sem]
        self.events.append((owner, lambda e: e.nop(), [(sem, full)], "rd", 1))
        self.cnt["rd"] += 1
        self._gates[key] = self.cnt["rd"]
        return self.cnt["rd"]

    def emit(self, eng, fn, waits=(), inc=None, inc_by=1):
        w = [(s, v) for (s, v) in waits if v > 0]
        w2 = []
        for s, v in w:
            if s == "ld" and eng != "sync":
                w2.append(("rd", self._gate("sync", "ld", v)))
            elif s == "tp" and eng != "pool":
                w2.append(("rd", self._gate("pool", "tp", v)))
            elif s == "ld" and eng == "sync":
                w2.append(("ld", self.cnt["ld"]))
            elif s == "tp" and eng == "pool":
                w2.append(("tp", self.cnt["tp"]))
            else:
                w2.append((s, v))
        w = w2
        if eng in ("vector", "scalar", "pool"):
            own = _OWN[eng]
            if self.cnt[own] > 0 and own != "tp":
                w.append((own, self.cnt[own]))
            elif own == "tp" and self.cnt[own] > 0:
                w.append((own, self.cnt[own]))
            if inc is None:
                inc = own
                inc_by = 1
        self.events.append((eng, fn, w, inc, inc_by))
        if inc is not None:
            self.cnt[inc] += inc_by


def build_nc(debug=False):
    nc = bass.Bass("TRN2", target_bir_lowering=False)

    dx = nc.dram_tensor("x", [L, C], F32, kind="ExternalInput").ap()
    dwqkv = nc.dram_tensor("wqkv", [C, 3 * C], F32, kind="ExternalInput").ap()
    dwproj = nc.dram_tensor("wproj", [C, C], F32, kind="ExternalInput").ap()
    dbproj = nc.dram_tensor("bproj", [1, C], F32, kind="ExternalInput").ap()
    dwmod = nc.dram_tensor("wmod", [C, 6 * C], F32, kind="ExternalInput").ap()
    dcvec = nc.dram_tensor("cvec", [NC, 128], F32, kind="ExternalInput").ap()
    dclip = nc.dram_tensor("clips", [1, 2], F32, kind="ExternalInput").ap()
    dwg = nc.dram_tensor("wg", [FI, C], F32, kind="ExternalInput").ap()
    dwu = nc.dram_tensor("wu", [FI, C], F32, kind="ExternalInput").ap()
    dwd = nc.dram_tensor("wd", [C, FI], F32, kind="ExternalInput").ap()
    dtab = nc.dram_tensor("tabs", [L, 2048], F32, kind="ExternalInput").ap()
    deye16 = nc.dram_tensor("eye16", [128, 128], BF16, kind="ExternalInput").ap()
    deye32 = nc.dram_tensor("eye32", [128, 128], F32, kind="ExternalInput").ap()
    dsel = nc.dram_tensor("sel", [2, 128], F32, kind="ExternalInput").ap()
    dy = nc.dram_tensor("y", [2048, C], F32, kind="ExternalOutput").ap()

    dbg = {}
    if debug:
        for nm, shp, dt_ in [("h1", [L, C], BF16), ("mod", [128, 48], F32),
                             ("kpost", [L, C], BF16), ("qpost", [2048, C], BF16),
                             ("attnout", [128, NC * 2048], BF16),
                             ("x1", [2048, C], F32), ("xq", [2048, C], BF16),
                             ("scal", [1, 16], F32)]:
            dbg[nm] = nc.dram_tensor("dbg_" + nm, shp, dt_, kind="ExternalOutput").ap()

    q_sp = nc.dram_tensor("q_sp", [NTO, 128, C], BF16, kind="Internal")
    k_sp = nc.dram_tensor("k_sp", [NT, 128, C], BF16, kind="Internal")
    v_sp = nc.dram_tensor("v_sp", [NT, 128, C], BF16, kind="Internal")
    p_sp = nc.dram_tensor("p_sp", [NI, 128, 2048], F32, kind="Internal")
    x1_sp = nc.dram_tensor("x1_sp", [NTO, 128, C], F32, kind="Internal")
    dl_sp = nc.dram_tensor("dl_sp", [NTO, 128, C], BF16, kind="Internal")
    wgT_sp = nc.dram_tensor("wgT_sp", [NI, 128, C], BF16, kind="Internal")
    wuT_sp = nc.dram_tensor("wuT_sp", [NI, 128, C], BF16, kind="Internal")
    wpj_sp = nc.dram_tensor("wpj_sp", [NC, 128, C], BF16, kind="Internal")
    rowb = nc.dram_tensor("rowb", [1, 4096], F32, kind="Internal")
    rowb_bf = nc.dram_tensor("rowb_bf", [1, 4096], BF16, kind="Internal")
    cc1_in = nc.dram_tensor("cc1_in", [1, 128], F32, kind="Internal")
    cc1_out = nc.dram_tensor("cc1_out", [1, 128], F32, kind="Internal")
    cc2_in = nc.dram_tensor("cc2_in", [1, 128], F32, kind="Internal")
    cc2_out = nc.dram_tensor("cc2_out", [1, 128], F32, kind="Internal")

    A = nc.alloc_sbuf_tensor
    AR1 = A("AR1", [128, 24576], BF16)
    AR2 = A("AR2", [128, 6144], F32)
    AR3 = A("AR3", [128, 16384], BF16)
    PSC = A("PSC", [128, 6656], F32)

    wqkv_s = AR1[:]
    kbuf = AR1[:, 0:8192]
    vbuf = AR1[:, 8192:16384]
    qbuf = AR1[:, 16384:20480]
    ptA = AR1[:, 20480:22528]
    ptB = AR1[:, 22528:24576]
    pqt = AR1[:, 0:5632]
    wgbuf = AR1[:, 11264:13312]
    wubuf = AR1[:, 13312:15360]
    wpjbuf = AR1[:, 15360:17408]
    wst = AR2[:, 0:6144]
    tbuf = AR2[:, 0:4096]
    ceff = AR2[:, 4096:6144]
    kfm = AR2.bitcast(BF16)[:, 0:4096]
    qfm = AR2.bitcast(BF16)[:, 4096:6144]
    drow = AR2[:, 3072:4096]
    rbt = AR2[:, 4096:4608]
    pstg = AR2[:, 0:2816]
    wmb = AR3[:, 0:6144]
    attnout = AR3[:]
    xqfm = AR3[:]
    qkvbias_row = AR3[0:1, 8192:11264]
    rsc32 = PSC[:, 0:1024]
    rrow = PSC[0:1, 0:1024]
    h1t = PSC.bitcast(BF16)[:, 2048:4096]
    h1fm = PSC.bitcast(BF16)[:, 4096:6144]
    qsc = PSC.bitcast(BF16)[:, 6144:8192]
    ksc = PSC.bitcast(BF16)[:, 8192:10240]
    vsc = PSC.bitcast(BF16)[:, 10240:12288]
    wqi = PSC.bitcast(BF16)[:, 0:2816]
    wint = PSC.bitcast(I32)[:, 1536:4352]
    wint6 = PSC.bitcast(I32)[:, 2048:4864]
    qsc32 = PSC[:, 0:2048]
    h2sc = PSC[:, 2048:3072]
    qint = PSC.bitcast(I32)[:, 3072:4096]
    xqt = PSC.bitcast(BF16)[:, 8192:10240]
    silu_t = PSC[:, 5120:5632]
    p_t = PSC[:, 5632:6656]
    ysc = PSC[:, 0:2048]
    x1stg = PSC[:, 2048:4096]
    cv_s = A("cv_s", [NC, 128], F32)
    clip_s = A("clip_s", [1, 2], F32)
    sel_s = A("sel_s", [2, 128], F32)
    rrow2 = A("rrow2", [2, 512], F32)
    eye16_s = A("eye16_s", [128, 128], BF16)
    eye32_s = A("eye32_s", [128, 128], F32)
    ones1b = A("ones1b", [1, 128], BF16)
    ones1f = A("ones1f", [1, 128], F32)
    onesd = A("onesd", [128, 1], BF16)
    silucf = A("silucf", [128, NC], BF16)
    silucf32 = A("silucf32", [128, NC], F32)
    mod_fm = A("mod_fm", [128, 48], F32)
    vec_fm = A("vec_fm", [128, 64], F32)
    sa_bf = A("sa_bf", [128, NC], BF16)
    bias_fm = A("bias_fm", [128, 24], F32)
    bias_fmb = A("bias_fmb", [128, 24], BF16)
    rowst = A("rowst", [32, 128], F32)
    rowtmp = A("rowtmp", [1, C], F32)
    bc0 = A("bc0", [128, C], F32)
    bc1 = A("bc1", [128, C], F32)
    wdT = A("wdT", [128, NI * C], BF16)
    xbuf = A("xbuf", [128, 2 * C], F32)
    stat_sum = A("stat_sum", [128, NT], F32)
    stat_sq = A("stat_sq", [128, NT], F32)
    stat_m = A("stat_m", [128, NT], F32)
    stat_r = A("stat_r", [128, NT], F32)
    dsc = A("dsc", [128, C], F32)
    wsc = A("wsc", [128, 16], F32)
    wgusc = A("wgusc", [128, 44], F32)
    scal = A("scal", [1, 16], F32)
    scal_p = A("scal_p", [128, 4], F32)
    pmax = A("pmax", [128, 2], F32)
    ccs = A("ccs", [1, 128], F32)

    psA = nc.alloc_psum_tensor("psA", [128, 1024], F32)
    psB = nc.alloc_psum_tensor("psB", [128, 1024], F32)
    psC = nc.alloc_psum_tensor("psC", [128, 1024], F32)
    psD = nc.alloc_psum_tensor("psD", [128, 512], F32)
    psE = nc.alloc_psum_tensor("psE", [128, 1024], BF16)

    prog = Prog()
    E = prog.emit
    now = prog.now

    def dma(out, in_):
        return lambda e: e.dma_start(out=out, in_=in_)

    E("sync", dma(cv_s[:], dcvec), inc="ld", inc_by=16)
    E("sync", dma(clip_s[:], dclip), inc="ld", inc_by=16)
    E("sync", dma(sel_s[:], dsel), inc="ld", inc_by=16)
    E("sync", dma(eye16_s[:], deye16), inc="ld", inc_by=16)
    E("sync", dma(eye32_s[:], deye32), inc="ld", inc_by=16)
    ld0 = now("ld")
    E("vector", lambda e: e.memset(ones1b[:], 1.0))
    E("vector", lambda e: e.memset(ones1f[:], 1.0))
    E("vector", lambda e: e.memset(onesd[:], 1.0))
    E("vector", lambda e: e.memset(stat_sum[:], 0.0))
    E("vector", lambda e: e.memset(stat_sq[:], 0.0))
    E("vector", lambda e: e.memset(pmax[:], 0.0))
    E("vector", lambda e: e.memset(scal[:], 0.0))
    E("vector", lambda e: e.memset(scal_p[:], 0.0), inc="tv")

    E("scalar", lambda e: e.activation(cv_s[:], cv_s[:], AF.Silu),
      waits=[("ld", ld0)], inc="ta")
    E("tensor", lambda e: e.transpose(psD[0:128, 0:NC], cv_s[:], eye32_s[0:NC, 0:NC]),
      waits=[("ta", now("ta"))], inc="tt")
    E("vector", lambda e: e.tensor_copy(silucf32[:], psD[0:128, 0:NC]),
      waits=[("tt", now("tt"))])
    E("vector", lambda e: e.tensor_copy(silucf[:], psD[0:128, 0:NC]),
      inc="tv")

    for f in range(NC):
        E("sync", dma(wst, dwmod[f * 128:(f + 1) * 128, :]),
          waits=[("tv", now("tv"))], inc="ld", inc_by=16)
        ld_f = now("ld")
        for j in range(48):
            E("tensor",
              lambda e, f=f, j=j: e.matmul(
                  psC[:, j:j + 1], wst[:, j * 128:(j + 1) * 128],
                  silucf32[:, f:f + 1], start=True, stop=True),
              waits=[("tv", now("tv")), ("ld", ld_f)] if j == 0 else (),
              inc="tt" if j == 47 else None)
        if f == 0:
            E("vector", lambda e: e.tensor_copy(mod_fm[:], psC[:, 0:48]),
              waits=[("tt", now("tt"))], inc="tv")
        else:
            E("vector", lambda e: e.tensor_tensor(mod_fm[:], mod_fm[:],
                                                  psC[:, 0:48], OP.add),
              waits=[("tt", now("tt"))], inc="tv")
    if debug:
        E("sync", dma(dbg["mod"], mod_fm[:]), waits=[("tv", now("tv"))],
          inc="ld", inc_by=16)

    E("vector", lambda e: e.tensor_single_scalar(vec_fm[:, 0:8], mod_fm[:, 8:16], 1.0, OP.add))
    E("vector", lambda e: e.reciprocal(vec_fm[:, 8:16], vec_fm[:, 0:8]))
    E("vector", lambda e: e.tensor_tensor(vec_fm[:, 8:16], vec_fm[:, 8:16], mod_fm[:, 0:8], OP.mult))
    E("vector", lambda e: e.tensor_copy(sa_bf[:], vec_fm[:, 8:16]))
    for dst, src in [(16, 16), (40, 40)]:
        E("vector", lambda e, dst=dst, src=src: e.tensor_single_scalar(
            vec_fm[:, dst:dst + 8], mod_fm[:, src:src + 8], -1.0, OP.mult))
        E("vector", lambda e, dst=dst, src=src: e.tensor_tensor(
            vec_fm[:, dst:dst + 8], vec_fm[:, dst:dst + 8], mod_fm[:, src:src + 8], OP.max))
        E("vector", lambda e, dst=dst: e.tensor_single_scalar(
            vec_fm[:, dst:dst + 8], vec_fm[:, dst:dst + 8], 0.25, OP.max))
        E("vector", lambda e, src=src: e.tensor_scalar(
            vec_fm[:, 56:64], mod_fm[:, src:src + 8], 0.0, 2.0, OP.is_ge, OP.mult))
        E("vector", lambda e: e.tensor_single_scalar(
            vec_fm[:, 56:64], vec_fm[:, 56:64], -1.0, OP.add))
        E("vector", lambda e, dst=dst: e.tensor_tensor(
            vec_fm[:, dst:dst + 8], vec_fm[:, dst:dst + 8], vec_fm[:, 56:64], OP.mult))
    E("vector", lambda e: e.tensor_single_scalar(vec_fm[:, 24:32], mod_fm[:, 32:40], 1.0, OP.add))
    E("vector", lambda e: e.tensor_copy(vec_fm[:, 32:40], mod_fm[:, 24:32]), inc="tv")
    vec_tv = now("tv")

    E("tensor", lambda e: e.transpose(psD[0:32, 0:128], vec_fm[:, 16:48], eye32_s[:]),
      waits=[("tv", vec_tv)], inc="tt")
    E("vector", lambda e: e.tensor_copy(rowst[:], psD[0:32, 0:128]),
      waits=[("tt", now("tt"))], inc="tv")
    E("sync", dma(rowb[0:1, 0:4096], rowst[:]), waits=[("tv", now("tv"))],
      inc="ld", inc_by=16)
    rows_ld = now("ld")

    for f in range(NC):
        E("sync", dma(wst[:, 0:3 * C], dwqkv[f * 128:(f + 1) * 128, :]),
          waits=[("tv", now("tv"))], inc="ld", inc_by=16)
        E("vector",
          lambda e, f=f: e.tensor_scalar(wqkv_s[:, f * 3 * C:(f + 1) * 3 * C],
                                         wst[:, 0:3 * C], vec_fm[:, f:f + 1],
                                         None, OP.mult),
          waits=[("ld", now("ld"))], inc="tv")
    wqkv_tv = now("tv")

    for f in range(NC):
        for j in range(24):
            E("tensor",
              lambda e, j=j, f=f: e.matmul(
                  psD[:, 64 + j:65 + j],
                  wqkv_s[:, f * 3 * C + j * 128:f * 3 * C + (j + 1) * 128],
                  sa_bf[:, f:f + 1], start=True, stop=True),
              waits=[("tv", now("tv"))] if j == 0 else (),
              inc="tt" if j == 23 else None)
        if f == 0:
            E("vector", lambda e: e.tensor_copy(bias_fm[:], psD[:, 64:88]),
              waits=[("tt", now("tt"))], inc="tv")
        else:
            E("vector", lambda e: e.tensor_tensor(bias_fm[:], bias_fm[:],
                                                  psD[:, 64:88], OP.add),
              waits=[("tt", now("tt"))], inc="tv")
    E("vector", lambda e: e.tensor_copy(bias_fmb[:], bias_fm[:]), inc="tv")
    E("tensor", lambda e: e.transpose(psE[0:24, 0:128], bias_fmb[:], eye16_s[:]),
      waits=[("tv", now("tv"))], inc="tt")
    E("vector", lambda e: e.tensor_copy(rowst.bitcast(BF16)[0:24, 0:128],
                                        psE[0:24, 0:128]),
      waits=[("tt", now("tt"))], inc="tv")
    E("sync", dma(rowb_bf[0:1, 0:3072], rowst.bitcast(BF16)[0:24, 0:128]),
      waits=[("tv", now("tv"))], inc="ld", inc_by=16)
    E("sync", dma(qkvbias_row[:], rowb_bf[0:1, 0:3 * C]), waits=[("ld", now("ld"))],
      inc="ld", inc_by=16)

    for f in range(NC):
        E("sync", dma(wst[:, 0:C], dwproj[f * 128:(f + 1) * 128, :]),
          waits=[("tv", now("tv"))], inc="ld", inc_by=16)
        E("vector", lambda e: e.tensor_copy(wqi[:, 0:C], wst[:, 0:C]),
          waits=[("ld", now("ld"))], inc="tv")
        E("sync", dma(wpj_sp[f], wqi[:, 0:C]), waits=[("tv", now("tv"))],
          inc="ld", inc_by=16)
    wpj_spill_ld = now("ld")

    def quant_chunk(width, keep_scale_col=None, keep_tensor=None):
        E("vector", lambda e, width=width: e.tensor_reduce(
            wsc[:, 0:1], wst[:, 0:width], AX.X, OP.max, apply_absolute_value=True),
          waits=[("ld", now("ld"))])
        E("vector", lambda e: e.reciprocal(wsc[:, 1:2], wsc[:, 0:1]))
        E("vector", lambda e: e.tensor_single_scalar(wsc[:, 1:2], wsc[:, 1:2], QMAX, OP.mult))
        if keep_scale_col is not None:
            kt = wsc if keep_tensor is None else keep_tensor
            E("vector", lambda e, k=keep_scale_col, kt=kt: e.tensor_single_scalar(
                kt[:, k:k + 1], wsc[:, 0:1], 1.0 / QMAX, OP.mult))
        E("vector", lambda e, width=width: e.tensor_scalar(
            wint[:, 0:width], wst[:, 0:width], wsc[:, 1:2], None, OP.mult))
        E("vector", lambda e, width=width: e.tensor_copy(wqi[:, 0:width],
                                                         wint[:, 0:width]), inc="tv")

    wgu_spill_ld = 0
    for wi, (src, spT) in enumerate([(dwg, wgT_sp), (dwu, wuT_sp)]):
        for cch in range(NI):
            E("sync", dma(wst[:, 0:C], src[cch * 128:(cch + 1) * 128, :]),
              waits=[("tv", now("tv"))], inc="ld", inc_by=16)
            quant_chunk(C, keep_scale_col=wi * NI + cch, keep_tensor=wgusc)
            for t in range(NC):
                E("tensor",
                  lambda e, t=t: e.transpose(psE[:, t * 128:(t + 1) * 128],
                                             wqi[:, t * 128:(t + 1) * 128],
                                             eye16_s[:]),
                  waits=[("tv", now("tv")), ("ld", wgu_spill_ld)] if t == 0 else (),
                  inc="tt" if t == NC - 1 else None)
            E("vector", lambda e: e.tensor_copy(ksc[:, 0:C], psE[:]),
              waits=[("tt", now("tt"))], inc="tv")
            E("sync", dma(spT[cch], ksc[:, 0:C]), waits=[("tv", now("tv"))],
              inc="ld", inc_by=16)
            wgu_spill_ld = now("ld")
    wdT3 = wdT[:].rearrange("p (cc hc) -> p cc hc", cc=NI)
    for hh in range(NC):
        E("sync", dma(wst[:, 0:FI], dwd[hh * 128:(hh + 1) * 128, :]),
          waits=[("tv", now("tv"))], inc="ld", inc_by=16)
        quant_chunk(FI, keep_scale_col=2 + hh)
        for grp in range(3):
            lo = grp * 8
            n = min(8, NI - lo)
            for u in range(n):
                E("tensor",
                  lambda e, lo=lo, u=u: e.transpose(
                      psE[:, u * 128:(u + 1) * 128],
                      wqi[:, (lo + u) * 128:(lo + u + 1) * 128], eye16_s[:]),
                  waits=[("tv", now("tv"))] if (grp == 0 and u == 0) else
                        ([("tv", now("tv"))] if u == 0 else ()),
                  inc="tt" if u == n - 1 else None)
            E("vector",
              lambda e, lo=lo, n=n, hh=hh: e.tensor_copy(
                  wdT3[:, lo:lo + n, hh * 128:(hh + 1) * 128],
                  psE[:, 0:n * 128].rearrange("p (a b) -> p a b", a=n)),
              waits=[("tt", now("tt"))], inc="tv")
    setup_tv = now("tv")

    r_inv = 1.0 / C
    qkv_ps = {0: psA, 1: psA, 2: psB, 3: psB, 4: psC, 5: psC}
    ev_tv = {}
    sp_ld = {}
    xload_ld = {}
    apply_tv = {}
    tadd_tv = {}
    h1T_tt = {}
    h1ev_ta = {}
    qkv_tt = {}
    last_spill_ld = 0

    for t in range(NT):
        our = t < NTO
        bi = t % 2
        E("sync", dma(xbuf[:, bi * C:(bi + 1) * C], dx[t * 128:(t + 1) * 128, :]),
          waits=[("tv", apply_tv.get(t - 2, 0)), ("ta", h1ev_ta.get(t - 2, 0))],
          inc="ld", inc_by=16)
        E("sync", dma(tbuf[:, bi * 2048:(bi + 1) * 2048],
                      dtab[t * 128:(t + 1) * 128, :]),
          waits=[("tv", max(tadd_tv.get(t - 2, 0), setup_tv if t < 2 else 0))],
          inc="ld", inc_by=16)
        xload_ld[t] = now("ld")
        E("scalar", lambda e, bi=bi, t=t: e.activation(
            dsc[:], xbuf[:, bi * C:(bi + 1) * C], AF.Identity,
            accum_out=stat_sum[:, t:t + 1]),
          waits=[("ld", xload_ld[t]), ("tv", setup_tv if t == 0 else 0)])
        E("scalar", lambda e, bi=bi, t=t: e.activation(
            dsc[:], xbuf[:, bi * C:(bi + 1) * C], AF.Square,
            accum_out=stat_sq[:, t:t + 1]), inc="ta")
        E("vector", lambda e, t=t: e.tensor_single_scalar(
            stat_m[:, t:t + 1], stat_sum[:, t:t + 1], r_inv, OP.mult),
          waits=[("ta", now("ta"))])
        E("vector", lambda e, t=t: e.tensor_single_scalar(
            stat_sq[:, t:t + 1], stat_sq[:, t:t + 1], r_inv, OP.mult))
        E("vector", lambda e, t=t: e.tensor_tensor(
            stat_sum[:, t:t + 1], stat_m[:, t:t + 1], stat_m[:, t:t + 1], OP.mult))
        E("vector", lambda e, t=t: e.scalar_tensor_tensor(
            stat_sq[:, t:t + 1], stat_sq[:, t:t + 1], EPS, stat_sum[:, t:t + 1],
            OP.add, OP.subtract))
        E("vector", lambda e, t=t: e.reciprocal(stat_sum[:, t:t + 1],
                                                stat_sq[:, t:t + 1]), inc="tv")
        E("scalar", lambda e, t=t: e.activation(stat_r[:, t:t + 1],
                                                stat_sum[:, t:t + 1], AF.Sqrt),
          waits=[("tv", now("tv"))], inc="ta")
        E("vector", lambda e, bi=bi, t=t: e.tensor_scalar(
            h1t[:, bi * C:(bi + 1) * C], xbuf[:, bi * C:(bi + 1) * C],
            stat_m[:, t:t + 1], stat_r[:, t:t + 1], OP.subtract, OP.mult),
          waits=[("ta", now("ta")), ("tt", h1T_tt.get(t - 2, 0))], inc="tv")
        apply_tv[t] = now("tv")
        if debug:
            E("sync", dma(dbg["h1"][t * 128:(t + 1) * 128, :],
                          h1t[:, bi * C:(bi + 1) * C]),
              waits=[("tv", now("tv"))], inc="ld", inc_by=16)
        for f in range(NC):
            E("tensor", lambda e, f=f, bi=bi: e.transpose(
                psE[:, f * 128:(f + 1) * 128],
                h1t[:, bi * C + f * 128:bi * C + (f + 1) * 128], eye16_s[:]),
              waits=([("tv", now("tv")), ("ta", h1ev_ta.get(t - 1, 0))]
                     if f == 0 else ()),
              inc="tt" if f == NC - 1 else None)
        h1T_tt[t] = now("tt")
        E("scalar", lambda e, bi=bi: e.activation(h1fm[:, bi * C:(bi + 1) * C],
                                                  psE[:], AF.Identity),
          waits=[("tt", now("tt")), ("tt", qkv_tt.get((t - 2, 2), 0))], inc="ta")
        h1ev_ta[t] = now("ta")

        slices = list(range(6)) if our else [2, 3, 4, 5]
        for sl in slices:
            ps = qkv_ps[sl]
            col = (sl % 2) * 512
            grp = sl // 2
            w = [("ta", h1ev_ta[t])]
            if (t - 1, grp) in ev_tv:
                w.append(("tv", ev_tv[(t - 1, grp)]))
            elif grp == 0 and (NTO - 1, 0) in ev_tv:
                w.append(("tv", ev_tv[(NTO - 1, 0)]))
            E("tensor", lambda e, ps=ps, col=col, sl=sl: e.matmul(
                ps[:, col:col + 512], ones1b[:],
                qkvbias_row[0:1, sl * 512:(sl + 1) * 512], start=True, stop=False),
              waits=w)
            for f in range(NC):
                E("tensor", lambda e, ps=ps, col=col, sl=sl, f=f, bi=bi: e.matmul(
                    ps[:, col:col + 512],
                    h1fm[:, bi * C + f * 128:bi * C + (f + 1) * 128],
                    wqkv_s[:, f * 3 * C + sl * 512:f * 3 * C + (sl + 1) * 512],
                    start=False, stop=(f == NC - 1)),
                  inc="tt" if (f == NC - 1 and sl % 2 == 1) else None)
            if sl % 2 == 1:
                qkv_tt[(t, grp)] = now("tt")

        E("vector", lambda e, bi=bi: e.tensor_tensor(
            ceff[:, bi * 1024:bi * 1024 + 512],
            tbuf[:, bi * 2048:bi * 2048 + 512],
            tbuf[:, bi * 2048 + 512:bi * 2048 + 1024], OP.add))
        E("vector", lambda e, bi=bi: e.tensor_tensor(
            ceff[:, bi * 1024 + 512:bi * 1024 + 1024],
            tbuf[:, bi * 2048 + 1024:bi * 2048 + 1536],
            tbuf[:, bi * 2048 + 1536:bi * 2048 + 2048], OP.add), inc="tv")
        tadd_tv[t] = now("tv")

        def rope(psrc, dst, grp):
            co = bi * 1024
            do = bi * C
            re_v = psrc[:].rearrange("p (h s i) -> p h s i", h=16, s=2)[:, :, 0, :]
            im_v = psrc[:].rearrange("p (h s i) -> p h s i", h=16, s=2)[:, :, 1, :]
            cosv = ceff[:, co:co + 512].rearrange("p (h i) -> p h i", h=16)
            sinv = ceff[:, co + 512:co + 1024].rearrange("p (h i) -> p h i", h=16)
            o_re = dst[:, do:do + C].rearrange("p (h s i) -> p h s i", h=16, s=2)[:, :, 0, :]
            o_im = dst[:, do:do + C].rearrange("p (h s i) -> p h s i", h=16, s=2)[:, :, 1, :]
            sc_a = rsc32[:, 0:512].rearrange("p (h i) -> p h i", h=16)
            sc_b = rsc32[:, 512:1024].rearrange("p (h i) -> p h i", h=16)
            w0 = [("tt", qkv_tt[(t, grp)]), ("tv", now("tv")),
                  ("ld", sp_ld.get((t - 2, grp), 0))]
            E("vector", lambda e: e.tensor_tensor(sc_a, re_v, cosv, OP.mult), waits=w0)
            E("vector", lambda e: e.tensor_tensor(sc_b, im_v, sinv, OP.mult))
            E("vector", lambda e: e.tensor_tensor(o_re, sc_a, sc_b, OP.subtract))
            E("vector", lambda e: e.tensor_tensor(sc_a, re_v, sinv, OP.mult))
            E("vector", lambda e: e.tensor_tensor(sc_b, im_v, cosv, OP.mult))
            E("vector", lambda e: e.tensor_tensor(o_im, sc_a, sc_b, OP.add), inc="tv")

        if our:
            rope(psA, qsc, 0)
            ev_tv[(t, 0)] = now("tv")
            E("sync", dma(q_sp[t], qsc[:, bi * C:(bi + 1) * C]),
              waits=[("tv", now("tv"))], inc="ld", inc_by=16)
            sp_ld[(t, 0)] = now("ld")
            if debug:
                E("sync", dma(dbg["qpost"][t * 128:(t + 1) * 128, :],
                              qsc[:, bi * C:(bi + 1) * C]), inc="ld", inc_by=16)
        rope(psB, ksc, 1)
        ev_tv[(t, 1)] = now("tv")
        E("sync", dma(k_sp[t], ksc[:, bi * C:(bi + 1) * C]),
          waits=[("tv", now("tv"))], inc="ld", inc_by=16)
        sp_ld[(t, 1)] = now("ld")
        if debug:
            E("sync", dma(dbg["kpost"][t * 128:(t + 1) * 128, :],
                          ksc[:, bi * C:(bi + 1) * C]), inc="ld", inc_by=16)
        E("vector", lambda e, bi=bi: e.tensor_copy(vsc[:, bi * C:(bi + 1) * C], psC[:]),
          waits=[("tt", qkv_tt[(t, 2)]), ("ld", sp_ld.get((t - 2, 2), 0))], inc="tv")
        ev_tv[(t, 2)] = now("tv")
        E("sync", dma(v_sp[t], vsc[:, bi * C:(bi + 1) * C]),
          waits=[("tv", now("tv"))], inc="ld", inc_by=16)
        sp_ld[(t, 2)] = now("ld")
        last_spill_ld = now("ld")

    abuf_ld = {}
    hp_last_tv = {}
    tpose_ta = {}
    attn_tv = 0
    gate_drow_ld = 0
    gate_rrow_tt = 0
    gate_rbt_tv = 0
    for hp in range(8):
        bi = hp % 2
        wv = [("tv", hp_last_tv.get(hp - 2, 0)), ("ld", last_spill_ld)]
        E("sync", dma(
            kbuf[:, bi * 4096:(bi + 1) * 4096].rearrange("p (t c) -> p t c", t=NT),
            k_sp.ap().rearrange("t p c -> p t c")[:, :, hp * 128:(hp + 1) * 128]),
          waits=wv, inc="ld", inc_by=16)
        E("sync", dma(
            vbuf[:, bi * 4096:(bi + 1) * 4096].rearrange("p (t c) -> p t c", t=NT),
            v_sp.ap().rearrange("t p c -> p t c")[:, :, hp * 128:(hp + 1) * 128]),
          inc="ld", inc_by=16)
        E("sync", dma(
            qbuf[:, bi * 2048:(bi + 1) * 2048].rearrange("p (t c) -> p t c", t=NTO),
            q_sp.ap().rearrange("t p c -> p t c")[:, :, hp * 128:(hp + 1) * 128]),
          inc="ld", inc_by=16)
        abuf_ld[hp] = now("ld")

        for grp in range(6):
            if grp < 2:
                srcb, soff, dstt, g = qbuf, bi * 2048, qfm, grp
            else:
                srcb, soff, dstt, g = kbuf, bi * 4096, kfm, grp - 2
            for u in range(8):
                idx = g * 8 + u
                E("tensor", lambda e, srcb=srcb, soff=soff, idx=idx, u=u:
                  e.transpose(psE[:, u * 128:(u + 1) * 128],
                              srcb[:, soff + idx * 128:soff + (idx + 1) * 128],
                              eye16_s[:]),
                  waits=([("ld", abuf_ld[hp]), ("ta", now("ta")),
                          ("tv", hp_last_tv.get(hp - 1, 0))] if u == 0 else ()),
                  inc="tt" if u == 7 else None)
            E("scalar", lambda e, dstt=dstt, g=g: e.activation(
                dstt[:, g * 1024:(g + 1) * 1024], psE[:], AF.Identity),
              waits=[("tt", now("tt"))], inc="ta")
        tpose_ta[hp] = now("ta")

        for qb in range(4):
            st_tt = {}
            exp_ta = {}
            pv_tt = {}
            ev_done_tv = attn_tv
            for kc in range(NT):
                sl = (kc % 2) * 512
                for hi, rl in enumerate([0, 64]):
                    ps_slot = psA[:, hi * 512:(hi + 1) * 512]
                    w = [("ta", tpose_ta[hp])]
                    if (hi, kc - 1) in exp_ta:
                        w.append(("ta", exp_ta[(hi, kc - 1)]))
                    E("tensor", lambda e, ps_slot=ps_slot, rl=rl, kc=kc, qb=qb:
                      e.matmul(ps_slot,
                               kfm[rl:rl + 64, kc * 128:(kc + 1) * 128],
                               qfm[rl:rl + 64, qb * 512:(qb + 1) * 512],
                               start=True, stop=True, tile_position=(rl, 0)),
                      waits=w, inc="tt")
                    st_tt[(hi, kc)] = now("tt")
                for hi, pt in enumerate([ptA, ptB]):
                    ps_slot = psA[:, hi * 512:(hi + 1) * 512]
                    w = [("tt", st_tt[(hi, kc)])]
                    if (hi, kc - 2) in pv_tt:
                        w.append(("tt", pv_tt[(hi, kc - 2)]))
                    E("scalar", lambda e, ps_slot=ps_slot, pt=pt, sl=sl: e.activation(
                        pt[:, sl:sl + 512], ps_slot, AF.Exp, scale=0.125),
                      waits=w, inc="ta")
                    exp_ta[(hi, kc)] = now("ta")
                for hi, pt in enumerate([ptA, ptB]):
                    cl = hi * 64
                    pv_dst = psB[0:64, 0:512] if hi == 0 else psB[64:128, 512:1024]
                    dn_dst = psC[0:1, 0:512] if hi == 0 else psC[64:65, 512:1024]
                    w = [("ta", exp_ta[(hi, kc)])]
                    if kc == 0:
                        w.append(("tv", ev_done_tv))
                        w.append(("ta", tpose_ta[hp]))
                    E("tensor", lambda e, pt=pt, sl=sl, kc=kc, cl=cl, bi=bi, hi=hi,
                      pv_dst=pv_dst:
                      e.matmul(pv_dst,
                               vbuf[:, bi * 4096 + kc * 128 + hi * 64:
                                    bi * 4096 + kc * 128 + hi * 64 + 64],
                               pt[:, sl:sl + 512],
                               start=(kc == 0), stop=(kc == NT - 1),
                               tile_position=(0, cl)),
                      waits=w, inc="tt")
                    E("tensor", lambda e, pt=pt, sl=sl, kc=kc, cl=cl, dn_dst=dn_dst:
                      e.matmul(dn_dst, onesd[:], pt[:, sl:sl + 512],
                               start=(kc == 0), stop=(kc == NT - 1),
                               tile_position=(0, cl)),
                      inc="tt")
                    pv_tt[(hi, kc)] = now("tt")
            pv_done_tt = now("tt")
            E("scalar", lambda e: e.activation(drow[0:1, 0:512], psC[0:1, 0:512],
                                               AF.Identity),
              waits=[("tt", pv_done_tt), ("ld", gate_drow_ld)])
            E("scalar", lambda e: e.activation(drow[64:65, 512:1024],
                                               psC[64:65, 512:1024], AF.Identity),
              inc="ta")
            dr_ta = now("ta")
            E("sync", dma(rrow2[0:1, :], drow[0:1, 0:512]),
              waits=[("ta", dr_ta), ("tt", gate_rrow_tt)],
              inc="ld", inc_by=16)
            E("sync", dma(rrow2[1:2, :], drow[64:65, 512:1024]),
              inc="ld", inc_by=16)
            gate_drow_ld = now("ld")
            E("vector", lambda e: e.reciprocal(rrow2[:], rrow2[:]),
              waits=[("ld", now("ld"))], inc="tv")
            E("tensor", lambda e: e.matmul(psD[:, 0:512], sel_s[:], rrow2[:],
                                           start=True, stop=True),
              waits=[("tv", now("tv"))], inc="tt")
            gate_rrow_tt = now("tt")
            E("scalar", lambda e: e.activation(rbt[:], psD[:, 0:512], AF.Identity),
              waits=[("tt", now("tt")), ("tv", gate_rbt_tv)], inc="ta")
            ao = hp * 2048 + qb * 512
            E("vector", lambda e, ao=ao: e.tensor_tensor(
                attnout[0:64, ao:ao + 512], psB[0:64, 0:512], rbt[0:64, :], OP.mult),
              waits=[("ta", now("ta"))])
            E("vector", lambda e, ao=ao: e.tensor_tensor(
                attnout[64:128, ao:ao + 512], psB[64:128, 512:1024], rbt[64:128, :],
                OP.mult), inc="tv")
            attn_tv = now("tv")
            gate_rbt_tv = now("tv")
        hp_last_tv[hp] = now("tv")
    if debug:
        for f in range(NC):
            E("sync", dma(dbg["attnout"][:, f * 2048:(f + 1) * 2048],
                          attnout[:, f * 2048:(f + 1) * 2048]),
              waits=[("tv", attn_tv)], inc="ld", inc_by=16)

    for tile, off in [(bc0, 0), (bc1, -1)]:
        rsrc = dbproj if off < 0 else rowb[0:1, off:off + C]
        E("sync", dma(rowtmp[:], rsrc),
          waits=[("ld", rows_ld), ("tt", now("tt"))], inc="ld", inc_by=16)
        E("tensor", lambda e: e.matmul(psA[:, 0:512], ones1f[:], rowtmp[0:1, 0:512],
                                       start=True, stop=True),
          waits=[("ld", now("ld")), ("ta", now("ta")),
                 ("tv", max(attn_tv, now("tv")))])
        E("tensor", lambda e: e.matmul(psA[:, 512:1024], ones1f[:],
                                       rowtmp[0:1, 512:1024], start=True, stop=True),
          inc="tt")
        E("vector", lambda e, tile=tile: e.tensor_copy(tile[:], psA[:]),
          waits=[("tt", now("tt"))], inc="tv")
    bc_tv = now("tv")

    pj_ld = {}
    pj_ev_ta = {}
    dl_sp_ld = {}
    for fo in range(NC):
        bi = fo % 2
        E("sync", dma(wpjbuf[:, bi * C:(bi + 1) * C].rearrange(
            "p (f c) -> p f c", f=NC),
            wpj_sp.ap().rearrange("f p c -> p f c")[:, :, fo * 128:(fo + 1) * 128]),
          waits=[("ld", wpj_spill_ld), ("tt", pj_ld.get(fo - 2, 0)),
                 ("tv", attn_tv if fo < 2 else 0)],
          inc="ld", inc_by=16)
        wld = now("ld")
        for sl in range(4):
            ps = (psA, psA, psB, psB)[sl]
            col = (sl % 2) * 512
            for f in range(NC):
                w = []
                if sl == 0 and f == 0:
                    w = [("ld", wld), ("tv", bc_tv), ("ta", pj_ev_ta.get(fo - 1, 0))]
                E("tensor", lambda e, ps=ps, col=col, sl=sl, f=f, fo=fo, bi=bi:
                  e.matmul(ps[:, col:col + 512],
                           wpjbuf[:, bi * C + f * 128:bi * C + (f + 1) * 128],
                           attnout[:, f * 2048 + sl * 512:f * 2048 + (sl + 1) * 512],
                           start=(f == 0), stop=(f == NC - 1)),
                  waits=w, inc="tt" if f == NC - 1 else None)
        pj_ld[fo] = now("tt")
        E("scalar", lambda e: e.activation(ksc[:, 0:C], psA[:], AF.Identity),
          waits=[("tt", now("tt"))])
        E("scalar", lambda e: e.activation(ksc[:, C:2 * C], psB[:], AF.Identity),
          inc="ta")
        ev_ta = now("ta")
        for grp in range(2):
            for u in range(8):
                tt_i = grp * 8 + u
                E("tensor", lambda e, tt_i=tt_i, u=u: e.transpose(
                    psE[:, u * 128:(u + 1) * 128],
                    ksc[:, tt_i * 128:(tt_i + 1) * 128], eye16_s[:]),
                  waits=([("ta", ev_ta), ("ta", pj_ev_ta.get((fo, grp - 1),
                                                            pj_ev_ta.get((fo - 1, 1), 0)))]
                         if u == 0 else ()),
                  inc="tt" if u == 7 else None)
            E("scalar", lambda e, grp=grp: e.activation(qsc[:, grp * C:(grp + 1) * C],
                                                        psE[:], AF.Identity),
              waits=[("tt", now("tt")), ("ld", dl_sp_ld.get((fo - 1, grp), 0))],
              inc="ta")
            pj_ev_ta[(fo, grp)] = now("ta")
            E("sync", dma(
                bass.AP(dl_sp, grp * 8 * 128 * 1024 + fo * 128,
                        [[1024, 128], [131072, 8], [1, 128]]),
                qsc[:, grp * C:(grp + 1) * C].rearrange("p (a b) -> p a b", a=8)),
              waits=[("ta", now("ta"))], inc="ld", inc_by=16)
            dl_sp_ld[(fo, grp)] = now("ld")
    dl_done_ld = now("ld")
    pj_ev_ta_last = now("ta")

    x1_tv = {}
    x1_ld = {}
    xq_tt = {}
    for t in range(NTO):
        bi = t % 2
        E("sync", dma(xbuf[:, bi * C:(bi + 1) * C], dx[t * 128:(t + 1) * 128, :]),
          waits=[("tv", x1_tv.get(t - 2, 0))], inc="ld", inc_by=16)
        E("sync", dma(ksc[:, bi * C:(bi + 1) * C], dl_sp[t]),
          waits=[("ld", dl_done_ld), ("tv", x1_tv.get(t - 2, 0)),
                 ("ta", pj_ev_ta_last if t < 2 else 0)],
          inc="ld", inc_by=16)
        ld_t = now("ld")
        E("vector", lambda e, bi=bi: e.tensor_tensor(
            dsc[:], ksc[:, bi * C:(bi + 1) * C], bc1[:], OP.add),
          waits=[("ld", ld_t)])
        E("vector", lambda e: e.tensor_tensor(dsc[:], dsc[:], bc0[:], OP.mult))
        E("vector", lambda e, t=t, bi=bi: e.tensor_tensor(
            x1stg[:, bi * C:(bi + 1) * C], dsc[:], xbuf[:, bi * C:(bi + 1) * C],
            OP.add), inc="tv")
        x1_tv[t] = now("tv")
        E("sync", dma(x1_sp[t], x1stg[:, bi * C:(bi + 1) * C]),
          waits=[("tv", now("tv"))], inc="ld", inc_by=16)
        x1_ld[t] = now("ld")
        if debug:
            E("sync", dma(dbg["x1"][t * 128:(t + 1) * 128, :],
                          x1stg[:, bi * C:(bi + 1) * C]), inc="ld", inc_by=16)
    x1_done_ld = now("ld")

    for tile, off in [(bc0, C), (bc1, 2 * C)]:
        E("sync", dma(rowtmp[:], rowb[0:1, off:off + C]),
          waits=[("tt", now("tt"))], inc="ld", inc_by=16)
        E("tensor", lambda e: e.matmul(psA[:, 0:512], ones1f[:], rowtmp[0:1, 0:512],
                                       start=True, stop=True),
          waits=[("ld", now("ld")), ("tv", now("tv")), ("ta", now("ta"))])
        E("tensor", lambda e: e.matmul(psA[:, 512:1024], ones1f[:],
                                       rowtmp[0:1, 512:1024], start=True, stop=True),
          inc="tt")
        E("vector", lambda e, tile=tile: e.tensor_copy(tile[:], psA[:]),
          waits=[("tt", now("tt"))], inc="tv")
    E("vector", lambda e: e.memset(stat_sum[:], 0.0))
    E("vector", lambda e: e.memset(stat_sq[:], 0.0), inc="tv")
    p4bc_tv = now("tv")

    st_ld = {}
    st_ta = {}
    st_tv = {}
    for t in range(NTO):
        bi = t % 2
        E("sync", dma(xbuf[:, bi * C:(bi + 1) * C], x1_sp[t]),
          waits=[("ld", x1_done_ld), ("ta", st_ta.get(t - 2, 0)),
                 ("tv", max(p4bc_tv if t < 2 else 0, st_tv.get(t - 2, 0)))],
          inc="ld", inc_by=16)
        st_ld[t] = now("ld")
        E("scalar", lambda e, bi=bi, t=t: e.activation(
            dsc[:], xbuf[:, bi * C:(bi + 1) * C], AF.Identity,
            accum_out=stat_sum[:, t:t + 1]), waits=[("ld", st_ld[t])])
        E("scalar", lambda e, bi=bi, t=t: e.activation(
            dsc[:], xbuf[:, bi * C:(bi + 1) * C], AF.Square,
            accum_out=stat_sq[:, t:t + 1]), inc="ta")
        st_ta[t] = now("ta")
        E("vector", lambda e, t=t: e.tensor_single_scalar(
            stat_m[:, t:t + 1], stat_sum[:, t:t + 1], r_inv, OP.mult),
          waits=[("ta", st_ta[t])])
        E("vector", lambda e, t=t: e.tensor_single_scalar(
            stat_sq[:, t:t + 1], stat_sq[:, t:t + 1], r_inv, OP.mult))
        E("vector", lambda e, t=t: e.tensor_tensor(
            stat_sum[:, t:t + 1], stat_m[:, t:t + 1], stat_m[:, t:t + 1], OP.mult))
        E("vector", lambda e, t=t: e.scalar_tensor_tensor(
            stat_sq[:, t:t + 1], stat_sq[:, t:t + 1], EPS, stat_sum[:, t:t + 1],
            OP.add, OP.subtract))
        E("vector", lambda e, t=t: e.reciprocal(stat_sum[:, t:t + 1],
                                                stat_sq[:, t:t + 1]), inc="tv")
        E("scalar", lambda e, t=t: e.activation(stat_r[:, t:t + 1],
                                                stat_sum[:, t:t + 1], AF.Sqrt),
          waits=[("tv", now("tv"))], inc="ta")
        E("vector", lambda e, t=t, bi=bi: e.tensor_scalar(
            h2sc[:], xbuf[:, bi * C:(bi + 1) * C], stat_m[:, t:t + 1],
            stat_r[:, t:t + 1], OP.subtract, OP.mult),
          waits=[("ta", now("ta"))])
        E("vector", lambda e: e.tensor_tensor(h2sc[:], h2sc[:], bc0[:], OP.mult))
        E("vector", lambda e: e.tensor_tensor(h2sc[:], h2sc[:], bc1[:], OP.add))
        E("vector", lambda e: e.tensor_reduce(wsc[:, 12:13], h2sc[:], AX.X, OP.max,
                                              apply_absolute_value=True))
        E("vector", lambda e: e.tensor_tensor(pmax[:, 0:1], pmax[:, 0:1],
                                              wsc[:, 12:13], OP.max), inc="tv")
        st_tv[t] = now("tv")
    E("tensor", lambda e: e.transpose(psD[0:1, 0:128], pmax[:, 0:1], eye32_s[:]),
      waits=[("tv", now("tv"))], inc="tt")
    E("vector", lambda e: e.tensor_copy(ccs[:], psD[0:1, 0:128]),
      waits=[("tt", now("tt"))], inc="tv")
    E("pool", lambda e: e.dma_start(out=cc1_in[0:1, :], in_=ccs[:]),
      waits=[("tv", now("tv"))], inc="tp", inc_by=16)
    E("pool", lambda e: e.collective_compute(
        "AllReduce", OP.max, replica_groups=[list(range(8))],
        ins=[cc1_in.ap().opt()], outs=[cc1_out.ap().opt()]), inc="cc")
    E("pool", lambda e: e.dma_start(out=ccs[:], in_=cc1_out[0:1, :]),
      waits=[("cc", now("cc"))], inc="tp", inc_by=16)
    E("vector", lambda e: e.tensor_reduce(scal[0:1, 0:1], ccs[:], AX.X, OP.max),
      waits=[("tp", now("tp"))], inc="tv")
    E("scalar", lambda e: e.activation(scal[0:1, 2:3], clip_s[0:1, 0:1], AF.Sigmoid),
      waits=[("tp", now("tp"))], inc="ta")
    E("vector", lambda e: e.tensor_tensor(scal[0:1, 4:5], scal[0:1, 0:1],
                                          scal[0:1, 2:3], OP.mult),
      waits=[("ta", now("ta"))])
    E("vector", lambda e: e.tensor_single_scalar(scal[0:1, 4:5], scal[0:1, 4:5],
                                                 1.0 / QMAX, OP.mult))
    E("vector", lambda e: e.reciprocal(scal[0:1, 5:6], scal[0:1, 4:5]), inc="tv")
    E("tensor", lambda e: e.matmul(psD[:, 0:2], ones1f[:], scal[0:1, 4:6],
                                   start=True, stop=True),
      waits=[("tv", now("tv"))], inc="tt")
    E("vector", lambda e: e.tensor_copy(scal_p[:, 0:2], psD[:, 0:2]),
      waits=[("tt", now("tt"))], inc="tv")
    if debug:
        E("sync", dma(dbg["scal"], scal[:]), waits=[("tv", now("tv"))],
          inc="ld", inc_by=16)

    xq_ev_ta = {}
    xqv_tv = {}
    xqfm3 = xqfm.rearrange("p (f tok) -> p f tok", f=NC)
    for t in range(NTO):
        bi = t % 2
        E("sync", dma(xbuf[:, bi * C:(bi + 1) * C], x1_sp[t]),
          waits=[("tv", max(now("tv") if t < 2 else 0, xqv_tv.get(t - 2, 0))),
                 ("tt", xq_tt.get(t - 2, 0))], inc="ld", inc_by=16)
        ld_t = now("ld")
        E("vector", lambda e, t=t, bi=bi: e.tensor_scalar(
            qsc32[:, bi * C:(bi + 1) * C], xbuf[:, bi * C:(bi + 1) * C],
            stat_m[:, t:t + 1], stat_r[:, t:t + 1], OP.subtract, OP.mult),
          waits=[("ld", ld_t)])
        E("vector", lambda e, bi=bi: e.tensor_tensor(
            qsc32[:, bi * C:(bi + 1) * C], qsc32[:, bi * C:(bi + 1) * C],
            bc0[:], OP.mult))
        E("vector", lambda e, bi=bi: e.tensor_tensor(
            qsc32[:, bi * C:(bi + 1) * C], qsc32[:, bi * C:(bi + 1) * C],
            bc1[:], OP.add))
        E("vector", lambda e, bi=bi: e.tensor_scalar(
            qsc32[:, bi * C:(bi + 1) * C], qsc32[:, bi * C:(bi + 1) * C],
            scal_p[:, 1:2], None, OP.mult))
        E("vector", lambda e, bi=bi: e.tensor_scalar(
            qsc32[:, bi * C:(bi + 1) * C], qsc32[:, bi * C:(bi + 1) * C],
            -QMAX - 1.0, QMAX, OP.max, OP.min))
        E("vector", lambda e, bi=bi: e.tensor_copy(qint[:], qsc32[:, bi * C:(bi + 1) * C]))
        E("vector", lambda e, bi=bi: e.tensor_copy(xqt[:, bi * C:(bi + 1) * C],
                                                   qint[:]), inc="tv")
        xqv_tv[t] = now("tv")
        if debug:
            E("sync", dma(dbg["xq"][t * 128:(t + 1) * 128, :],
                          xqt[:, bi * C:(bi + 1) * C]),
              waits=[("tv", now("tv"))], inc="ld", inc_by=16)
        for f in range(NC):
            E("tensor", lambda e, f=f, bi=bi: e.transpose(
                psE[:, f * 128:(f + 1) * 128],
                xqt[:, bi * C + f * 128:bi * C + (f + 1) * 128], eye16_s[:]),
              waits=([("tv", now("tv")), ("ta", xq_ev_ta.get(t - 1, 0))]
                     if f == 0 else ()),
              inc="tt" if f == NC - 1 else None)
        xq_tt[t] = now("tt")
        E("scalar", lambda e, t=t: e.activation(
            xqfm3[:, :, t * 128:(t + 1) * 128],
            psE[:].rearrange("p (a b) -> p a b", a=NC), AF.Identity),
          waits=[("tt", now("tt"))], inc="ta")
        xq_ev_ta[t] = now("ta")
    xq_ta = now("ta")

    g_tt = {}
    g_tv = {}
    pdma_ld = {}
    for cch in range(NI):
        bi = cch % 2
        E("sync", dma(wgbuf[:, bi * C:(bi + 1) * C], wgT_sp[cch]),
          waits=[("ld", wgu_spill_ld), ("tt", g_tt.get(cch - 2, 0))],
          inc="ld", inc_by=16)
        E("sync", dma(wubuf[:, bi * C:(bi + 1) * C], wuT_sp[cch]), inc="ld", inc_by=16)
        wld = now("ld")
        ps = psA if bi == 0 else psB
        E("vector", lambda e, cch=cch: e.tensor_tensor(
            wsc[:, 14:15], wgusc[:, cch:cch + 1], scal_p[:, 0:1], OP.mult),
          waits=[("ta", xq_ta)])
        E("vector", lambda e, cch=cch: e.tensor_tensor(
            wsc[:, 15:16], wgusc[:, NI + cch:NI + cch + 1], scal_p[:, 0:1],
            OP.mult), inc="tv")
        cgcu_tv = now("tv")
        tb_ta = {}
        tb_tv = {}
        for tb in range(4):
            for gu in range(2):
                wb = wgbuf if gu == 0 else wubuf
                for hh in range(NC):
                    w = []
                    if tb == 0 and gu == 0 and hh == 0:
                        w = [("ld", wld), ("ta", xq_ta), ("tv", g_tv.get(cch - 2, 0))]
                    elif gu == 0 and hh == 0:
                        w = [("ta", tb_ta[tb - 1]), ("tv", tb_tv[tb - 1])]
                    E("tensor", lambda e, ps=ps, gu=gu, hh=hh, tb=tb, bi=bi, wb=wb:
                      e.matmul(ps[:, gu * 512:(gu + 1) * 512],
                               wb[:, bi * C + hh * 128:bi * C + (hh + 1) * 128],
                               xqfm[:, hh * 2048 + tb * 512:hh * 2048 + (tb + 1) * 512],
                               start=(hh == 0), stop=(hh == NC - 1)),
                      waits=w, inc="tt" if hh == NC - 1 else None)
            E("scalar", lambda e, ps=ps: e.activation(silu_t[:], ps[:, 0:512],
                                                      AF.Silu, scale=wsc[:, 14:15]),
              waits=[("tt", now("tt")), ("tv", cgcu_tv)], inc="ta")
            tb_ta[tb] = now("ta")
            E("vector", lambda e, ps=ps, tb=tb: e.scalar_tensor_tensor(
                p_t[:, (tb % 2) * 512:(tb % 2 + 1) * 512], ps[:, 512:1024],
                wsc[:, 15:16], silu_t[:], OP.mult, OP.mult),
              waits=[("ta", now("ta")), ("ld", pdma_ld.get((cch, tb - 2), 0))],
              inc="tv")
            tb_tv[tb] = now("tv")
            E("vector", lambda e, tb=tb: e.tensor_reduce(
                wsc[:, 13:14], p_t[:, (tb % 2) * 512:(tb % 2 + 1) * 512], AX.X,
                OP.max, apply_absolute_value=True))
            E("vector", lambda e: e.tensor_tensor(pmax[:, 1:2], pmax[:, 1:2],
                                                  wsc[:, 13:14], OP.max), inc="tv")
            E("sync", dma(p_sp.ap()[cch, :, tb * 512:(tb + 1) * 512],
                          p_t[:, (tb % 2) * 512:(tb % 2 + 1) * 512]),
              waits=[("tv", now("tv"))], inc="ld", inc_by=16)
            pdma_ld[(cch, tb)] = now("ld")
        g_tt[cch] = now("tt")
        g_tv[cch] = now("tv")
    p_done_ld = now("ld")

    E("tensor", lambda e: e.transpose(psD[0:1, 0:128], pmax[:, 1:2], eye32_s[:]),
      waits=[("tv", now("tv"))], inc="tt")
    E("vector", lambda e: e.tensor_copy(ccs[:], psD[0:1, 0:128]),
      waits=[("tt", now("tt"))], inc="tv")
    E("pool", lambda e: e.dma_start(out=cc2_in[0:1, :], in_=ccs[:]),
      waits=[("tv", now("tv"))], inc="tp", inc_by=16)
    E("pool", lambda e: e.collective_compute(
        "AllReduce", OP.max, replica_groups=[list(range(8))],
        ins=[cc2_in.ap().opt()], outs=[cc2_out.ap().opt()]), inc="cc")
    E("pool", lambda e: e.dma_start(out=ccs[:], in_=cc2_out[0:1, :]),
      waits=[("cc", now("cc"))], inc="tp", inc_by=16)
    E("vector", lambda e: e.tensor_reduce(scal[0:1, 1:2], ccs[:], AX.X, OP.max),
      waits=[("tp", now("tp"))], inc="tv")
    E("scalar", lambda e: e.activation(scal[0:1, 3:4], clip_s[0:1, 1:2], AF.Sigmoid),
      waits=[("tp", now("tp"))], inc="ta")
    E("vector", lambda e: e.tensor_tensor(scal[0:1, 6:7], scal[0:1, 1:2],
                                          scal[0:1, 3:4], OP.mult),
      waits=[("ta", now("ta"))])
    E("vector", lambda e: e.tensor_single_scalar(scal[0:1, 6:7], scal[0:1, 6:7],
                                                 1.0 / QMAX, OP.mult))
    E("vector", lambda e: e.reciprocal(scal[0:1, 7:8], scal[0:1, 6:7]), inc="tv")
    E("tensor", lambda e: e.matmul(psD[:, 2:4], ones1f[:], scal[0:1, 6:8],
                                   start=True, stop=True),
      waits=[("tv", now("tv"))], inc="tt")
    E("vector", lambda e: e.tensor_copy(scal_p[:, 2:4], psD[:, 2:4]),
      waits=[("tt", now("tt"))], inc="tv")
    E("vector", lambda e: e.tensor_tensor(wsc[:, 2:10], wsc[:, 2:10],
                                          vec_fm[:, 40:48], OP.mult), inc="tv")
    E("tensor", lambda e: e.transpose(psD[0:8, 0:128], wsc[:, 2:10], eye32_s[:]),
      waits=[("tv", now("tv"))], inc="tt")
    E("vector", lambda e: e.tensor_copy(rowst[0:8, :], psD[0:8, 0:128]),
      waits=[("tt", now("tt"))], inc="tv")
    E("sync", dma(rowb[0:1, 0:1024], rowst[0:8, :]), waits=[("tv", now("tv"))],
      inc="ld", inc_by=16)
    E("sync", dma(rowtmp[:], rowb[0:1, 0:C]), waits=[("ld", now("ld"))],
      inc="ld", inc_by=16)
    E("vector", lambda e: e.tensor_scalar(rowtmp[:], rowtmp[:],
                                          scal[0:1, 6:7], None, OP.mult),
      waits=[("ld", now("ld"))], inc="tv")
    E("tensor", lambda e: e.matmul(psA[:, 0:512], ones1f[:], rowtmp[0:1, 0:512],
                                   start=True, stop=True),
      waits=[("tv", now("tv"))])
    E("tensor", lambda e: e.matmul(psA[:, 512:1024], ones1f[:],
                                   rowtmp[0:1, 512:1024], start=True, stop=True),
      inc="tt")
    E("vector", lambda e: e.tensor_copy(bc0[:], psA[:]),
      waits=[("tt", now("tt"))], inc="tv")
    coef_tv = now("tv")

    y_ld = {}
    p6_tt = {}
    p6_tv = {}
    pqt3 = pqt.rearrange("p (s c) -> p s c", s=2)
    for t in range(NTO):
        bi = t % 2
        E("sync", dma(pstg.rearrange("p (c k) -> p c k", c=NI),
                      p_sp.ap().rearrange("c p k -> p c k")[:, :, t * 128:(t + 1) * 128]),
          waits=[("ld", p_done_ld), ("tv", p6_tv.get(t - 1, 0)),
                 ("tv", coef_tv if t == 0 else 0)],
          inc="ld", inc_by=16)
        ld_t = now("ld")
        E("vector", lambda e: e.tensor_scalar(
            pstg, pstg, scal_p[:, 3:4], None, OP.mult),
          waits=[("ld", ld_t), ("tt", p6_tt.get(t - 2, 0))])
        E("vector", lambda e: e.tensor_scalar(
            pstg, pstg, -QMAX - 1.0, QMAX, OP.max, OP.min))
        E("vector", lambda e: e.tensor_copy(wint6[:, 0:FI], pstg))
        E("vector", lambda e, bi=bi: e.tensor_copy(
            pqt3[:, bi, :], wint6[:, 0:FI]), inc="tv")
        p6_tv[t] = now("tv")
        ps = psA if bi == 0 else psB
        for sl in range(2):
            for cch in range(NI):
                w = []
                if sl == 0 and cch == 0:
                    w = [("tv", p6_tv[t]), ("tv", y_ld.get(("ev", t - 2), 0))]
                E("tensor", lambda e, ps=ps, sl=sl, cch=cch, bi=bi: e.matmul(
                    ps[:, sl * 512:(sl + 1) * 512],
                    pqt3[:, bi, cch * 128:(cch + 1) * 128],
                    wdT[:, cch * C + sl * 512:cch * C + (sl + 1) * 512],
                    start=(cch == 0), stop=(cch == NI - 1)),
                  waits=w, inc="tt" if cch == NI - 1 else None)
        p6_tt[t] = now("tt")
        E("sync", dma(xbuf[:, bi * C:(bi + 1) * C], x1_sp[t]),
          waits=[("tv", y_ld.get(("ev", t - 2), 0))], inc="ld", inc_by=16)
        xld = now("ld")
        E("vector", lambda e, ps=ps: e.tensor_tensor(dsc[:], ps[:], bc0[:], OP.mult),
          waits=[("tt", p6_tt[t]), ("ld", y_ld.get(t - 2, 0))])
        E("vector", lambda e, bi=bi: e.tensor_tensor(
            ysc[:, bi * C:(bi + 1) * C], dsc[:], xbuf[:, bi * C:(bi + 1) * C],
            OP.add), waits=[("ld", xld)], inc="tv")
        y_ld[("ev", t)] = now("tv")
        E("sync", dma(dy[t * 128:(t + 1) * 128, :], ysc[:, bi * C:(bi + 1) * C]),
          waits=[("tv", now("tv"))], inc="ld", inc_by=16)
        y_ld[t] = now("ld")

    E("sync", lambda e: e.nop(), waits=[("ld", now("ld"))])

    import contextlib
    stack = contextlib.ExitStack()
    sems = {s: stack.enter_context(nc.semaphore("sem_" + s)) for s in SEMS}
    block = stack.enter_context(nc.Block())

    def mk(engine_name):
        evs = [ev for ev in prog.events if ev[0] == engine_name]

        def body(eng):
            maxw = {k: 0 for k in SEMS}
            for _, fn, waits, inc, inc_by in evs:
                for sem, val in waits:
                    if val <= maxw[sem]:
                        continue
                    maxw[sem] = val
                    if engine_name == "tensor" and sem == "tt":
                        continue
                    eng.wait_ge(sems[sem], val)
                inst = fn(eng)
                if inc is not None:
                    assert inst is not None
                    inst.then_inc(sems[inc], inc_by)
        return body

    block.sync(mk("sync"))
    block.tensor(mk("tensor"))
    block.vector(mk("vector"))
    block.scalar(mk("scalar"))
    block.gpsimd(mk("pool"))
    stack.close()
    return nc


def _rope_tables():
    j32 = np.arange(0, D, 2, dtype=np.float32)
    theta32 = (np.float32(1.0) / (np.float32(10000.0) ** (j32 / np.float32(D)))
               ) * np.float32(3.0)
    pos32 = np.arange(L, dtype=np.float32)
    ang32 = (pos32[:, None] * theta32[None, :]).astype(np.float32)
    cos_hi = np.cos(ang32.astype(np.float64)).astype(np.float32)
    sin_hi = np.sin(ang32.astype(np.float64)).astype(np.float32)
    theta64 = (1.0 / (10000.0 ** (np.arange(0, D, 2, dtype=np.float64) / D))) * 3.0
    ang64 = np.arange(L, dtype=np.float64)[:, None] * theta64[None, :]
    cos_lo = (np.cos(ang64) - cos_hi).astype(np.float32)
    sin_lo = (np.sin(ang64) - sin_hi).astype(np.float32)
    return (np.tile(cos_hi, (1, 16)), np.tile(cos_lo, (1, 16)),
            np.tile(sin_hi, (1, 16)), np.tile(sin_lo, (1, 16)))


def _perm_qk():
    perm = np.arange(3 * C)
    for blk in range(2):
        for h in range(H):
            base = blk * C + h * D
            perm[base:base + 32] = base + np.arange(0, D, 2)
            perm[base + 32:base + D] = base + np.arange(1, D, 2)
    return perm


_NC_CACHE = {}


def kernel(L0_attn_x, L0_attn_w_qkv, L0_attn_w_proj, L0_attn_b_proj,
           L0_ffn_clip_x, L0_ffn_clip_down, L0_ffn_w_gate, L0_ffn_w_up,
           L0_ffn_w_down, L0_c, L0_wmod, _debug=False):
    x = np.ascontiguousarray(np.asarray(L0_attn_x, dtype=np.float32))
    wqkv = np.ascontiguousarray(np.asarray(L0_attn_w_qkv, dtype=np.float32))
    wproj = np.ascontiguousarray(np.asarray(L0_attn_w_proj, dtype=np.float32))
    bproj = np.asarray(L0_attn_b_proj, dtype=np.float32).reshape(1, C)
    wg = np.ascontiguousarray(np.asarray(L0_ffn_w_gate, dtype=np.float32))
    wu = np.ascontiguousarray(np.asarray(L0_ffn_w_up, dtype=np.float32))
    wd = np.ascontiguousarray(np.asarray(L0_ffn_w_down, dtype=np.float32))
    cv = np.asarray(L0_c, dtype=np.float32)
    wmod = np.ascontiguousarray(np.asarray(L0_wmod, dtype=np.float32))
    clips = np.array([[float(np.asarray(L0_ffn_clip_x).reshape(-1)[0]),
                       float(np.asarray(L0_ffn_clip_down).reshape(-1)[0])]],
                     dtype=np.float32)

    cos_hi, cos_lo, sin_hi, sin_lo = _rope_tables()
    tabs_base = np.ascontiguousarray(
        np.concatenate([cos_hi, cos_lo, sin_hi, sin_lo], axis=1))
    wqkv_p = np.ascontiguousarray(wqkv[:, _perm_qk()])
    eye16 = np.eye(128, dtype=BF)
    eye32 = np.eye(128, dtype=np.float32)
    sel = np.zeros((2, 128), np.float32)
    sel[0, 0:64] = 1.0
    sel[1, 64:128] = 1.0

    key = ("nc", bool(_debug))
    if key not in _NC_CACHE:
        _NC_CACHE[key] = build_nc(debug=_debug)
    ncobj = _NC_CACHE[key]

    in_maps = []
    for core in range(8):
        b, hf = core // 2, core % 2
        rot = hf * 2048
        xr = np.ascontiguousarray(np.roll(x[b], -rot, axis=0))
        tr = np.ascontiguousarray(np.roll(tabs_base, -rot, axis=0))
        in_maps.append({
            "x": xr, "wqkv": wqkv_p, "wproj": wproj, "bproj": bproj,
            "wmod": wmod, "cvec": np.ascontiguousarray(cv[b].reshape(NC, 128)),
            "clips": clips, "wg": wg, "wu": wu, "wd": wd,
            "tabs": tr, "eye16": eye16, "eye32": eye32, "sel": sel,
        })
    res = run_bass_kernel_spmd(ncobj, in_maps, core_ids=list(range(8)))
    out = np.empty((B, L, C), dtype=np.float32)
    for core in range(8):
        b, hf = core // 2, core % 2
        out[b, hf * 2048:(hf + 1) * 2048, :] = res.results[core]["y"]
    if _debug:
        return out, res
    return out


# version 02
# speedup vs baseline: 1.0436x; 1.0436x over previous
import numpy as np
import ml_dtypes
import concourse.bass as bass
import concourse.mybir as mybir
from concourse.bass_utils import run_bass_kernel_spmd

F32 = mybir.dt.float32
BF16 = mybir.dt.bfloat16
I32 = mybir.dt.int32
AF = mybir.ActivationFunctionType
OP = mybir.AluOpType
AX = mybir.AxisListType
BF = ml_dtypes.bfloat16

B, L, C = 4, 4096, 1024
H, D = 16, 64
NT = L // 128
NTO = 2048 // 128
FI = 2816
NI = FI // 128
NC = C // 128
QMAX = 127.0
EPS = 1e-5

ENGINES = ("sync", "tensor", "vector", "scalar", "pool")
_OWN = {"sync": "ld", "tensor": "tt", "vector": "tv", "scalar": "ta", "pool": "tp"}
SEMS = ("ld", "tt", "tv", "ta", "tp", "cc", "rd")


class Prog:
    def __init__(self):
        self.events = []
        self.cnt = {k: 0 for k in SEMS}
        self._gates = {}

    def now(self, sem):
        return self.cnt[sem]

    def _gate(self, owner, sem, v):
        key = (sem, v)
        if key in self._gates:
            return self._gates[key]
        full = self.cnt[sem]
        self.events.append((owner, lambda e: e.nop(), [(sem, full)], "rd", 1))
        self.cnt["rd"] += 1
        self._gates[key] = self.cnt["rd"]
        return self.cnt["rd"]

    def emit(self, eng, fn, waits=(), inc=None, inc_by=1):
        w = [(s, v) for (s, v) in waits if v > 0]
        w2 = []
        for s, v in w:
            if s == "ld" and eng != "sync":
                w2.append(("rd", self._gate("sync", "ld", v)))
            elif s == "tp" and eng != "pool":
                w2.append(("rd", self._gate("pool", "tp", v)))
            elif s == "ld" and eng == "sync":
                w2.append(("ld", self.cnt["ld"]))
            elif s == "tp" and eng == "pool":
                w2.append(("tp", self.cnt["tp"]))
            else:
                w2.append((s, v))
        w = w2
        if eng in ("vector", "scalar", "pool"):
            own = _OWN[eng]
            if self.cnt[own] > 0 and own != "tp":
                w.append((own, self.cnt[own]))
            elif own == "tp" and self.cnt[own] > 0:
                w.append((own, self.cnt[own]))
            if inc is None:
                inc = own
                inc_by = 1
        self.events.append((eng, fn, w, inc, inc_by))
        if inc is not None:
            self.cnt[inc] += inc_by


def build_nc(debug=False):
    nc = bass.Bass("TRN2", target_bir_lowering=False)

    dx = nc.dram_tensor("x", [L, C], F32, kind="ExternalInput").ap()
    dwqkv = nc.dram_tensor("wqkv", [C, 3 * C], F32, kind="ExternalInput").ap()
    dwproj = nc.dram_tensor("wproj", [C, C], F32, kind="ExternalInput").ap()
    dbproj = nc.dram_tensor("bproj", [1, C], F32, kind="ExternalInput").ap()
    dwmod = nc.dram_tensor("wmod", [C, 6 * C], F32, kind="ExternalInput").ap()
    dcvec = nc.dram_tensor("cvec", [NC, 128], F32, kind="ExternalInput").ap()
    dclip = nc.dram_tensor("clips", [1, 2], F32, kind="ExternalInput").ap()
    dwg = nc.dram_tensor("wg", [FI, C], F32, kind="ExternalInput").ap()
    dwu = nc.dram_tensor("wu", [FI, C], F32, kind="ExternalInput").ap()
    dwd = nc.dram_tensor("wd", [C, FI], F32, kind="ExternalInput").ap()
    dtab = nc.dram_tensor("tabs", [L, 2048], F32, kind="ExternalInput").ap()
    deye16 = nc.dram_tensor("eye16", [128, 128], BF16, kind="ExternalInput").ap()
    deye32 = nc.dram_tensor("eye32", [128, 128], F32, kind="ExternalInput").ap()
    dsel = nc.dram_tensor("sel", [2, 128], F32, kind="ExternalInput").ap()
    dy = nc.dram_tensor("y", [2048, C], F32, kind="ExternalOutput").ap()

    dbg = {}
    if debug:
        for nm, shp, dt_ in [("h1", [L, C], BF16), ("mod", [128, 48], F32),
                             ("kpost", [L, C], BF16), ("qpost", [2048, C], BF16),
                             ("attnout", [128, NC * 2048], BF16),
                             ("x1", [2048, C], F32), ("xq", [2048, C], BF16),
                             ("scal", [1, 16], F32)]:
            dbg[nm] = nc.dram_tensor("dbg_" + nm, shp, dt_, kind="ExternalOutput").ap()

    q_sp = nc.dram_tensor("q_sp", [NTO, 128, C], BF16, kind="Internal")
    k_sp = nc.dram_tensor("k_sp", [NT, 128, C], BF16, kind="Internal")
    v_sp = nc.dram_tensor("v_sp", [NT, 128, C], BF16, kind="Internal")
    p_sp = nc.dram_tensor("p_sp", [NI, 128, 2048], F32, kind="Internal")
    x1_sp = nc.dram_tensor("x1_sp", [NTO, 128, C], F32, kind="Internal")
    dl_sp = nc.dram_tensor("dl_sp", [NTO, 128, C], BF16, kind="Internal")
    wgT_sp = nc.dram_tensor("wgT_sp", [NI, 128, C], BF16, kind="Internal")
    wuT_sp = nc.dram_tensor("wuT_sp", [NI, 128, C], BF16, kind="Internal")
    wpj_sp = nc.dram_tensor("wpj_sp", [NC, 128, C], BF16, kind="Internal")
    rowb = nc.dram_tensor("rowb", [1, 4096], F32, kind="Internal")
    rowb_bf = nc.dram_tensor("rowb_bf", [1, 4096], BF16, kind="Internal")
    cc1_in = nc.dram_tensor("cc1_in", [1, 128], F32, kind="Internal")
    cc1_out = nc.dram_tensor("cc1_out", [1, 128], F32, kind="Internal")
    cc2_in = nc.dram_tensor("cc2_in", [1, 128], F32, kind="Internal")
    cc2_out = nc.dram_tensor("cc2_out", [1, 128], F32, kind="Internal")

    A = nc.alloc_sbuf_tensor
    AR1 = A("AR1", [128, 24576], BF16)
    AR2 = A("AR2", [128, 6144], F32)
    AR3 = A("AR3", [128, 16384], BF16)
    PSC = A("PSC", [128, 6656], F32)

    wqkv_s = AR1[:]
    kbuf = AR1[:, 0:8192]
    vbuf = AR1[:, 8192:16384]
    qbuf = AR1[:, 16384:20480]
    ptA = AR1[:, 20480:22528]
    ptB = AR1[:, 22528:24576]
    pqt = AR1[:, 0:5632]
    wgbuf = AR1[:, 11264:13312]
    wubuf = AR1[:, 13312:15360]
    wpjbuf = AR1[:, 15360:17408]
    wst = AR2[:, 0:6144]
    tbuf = AR2[:, 0:4096]
    ceff = AR2[:, 4096:6144]
    kfm = AR2.bitcast(BF16)[:, 0:4096]
    qfm = AR2.bitcast(BF16)[:, 4096:6144]
    drow = AR2[:, 3072:4096]
    rbt = AR2[:, 4096:4608]
    pstg = AR2[:, 0:2816]
    wmb = AR3[:, 0:6144]
    attnout = AR3[:]
    xqfm = AR3[:]
    qkvbias_row = AR3[0:1, 8192:11264]
    rsc32 = PSC[:, 0:1024]
    rrow = PSC[0:1, 0:1024]
    h1t = PSC.bitcast(BF16)[:, 2048:4096]
    h1fm = PSC.bitcast(BF16)[:, 4096:6144]
    qsc = PSC.bitcast(BF16)[:, 6144:8192]
    ksc = PSC.bitcast(BF16)[:, 8192:10240]
    vsc = PSC.bitcast(BF16)[:, 10240:12288]
    wqi = PSC.bitcast(BF16)[:, 0:2816]
    wint = PSC.bitcast(I32)[:, 1536:4352]
    wint6 = PSC.bitcast(I32)[:, 2048:4864]
    qsc32 = PSC[:, 0:2048]
    h2sc = PSC[:, 2048:3072]
    qint = PSC.bitcast(I32)[:, 3072:4096]
    xqt = PSC.bitcast(BF16)[:, 8192:10240]
    silu_t = PSC[:, 5120:5632]
    p_t = PSC[:, 5632:6656]
    ysc = PSC[:, 0:2048]
    x1stg = PSC[:, 2048:4096]
    cv_s = A("cv_s", [NC, 128], F32)
    clip_s = A("clip_s", [1, 2], F32)
    sel_s = A("sel_s", [2, 128], F32)
    rrow2 = A("rrow2", [2, 512], F32)
    eye16_s = A("eye16_s", [128, 128], BF16)
    eye32_s = A("eye32_s", [128, 128], F32)
    ones1b = A("ones1b", [1, 128], BF16)
    ones1f = A("ones1f", [1, 128], F32)
    onesd = A("onesd", [128, 1], BF16)
    silucf = A("silucf", [128, NC], BF16)
    silucf32 = A("silucf32", [128, NC], F32)
    mod_fm = A("mod_fm", [128, 48], F32)
    vec_fm = A("vec_fm", [128, 64], F32)
    sa_bf = A("sa_bf", [128, NC], BF16)
    bias_fm = A("bias_fm", [128, 24], F32)
    bias_fmb = A("bias_fmb", [128, 24], BF16)
    rowst = A("rowst", [32, 128], F32)
    rowtmp = A("rowtmp", [1, C], F32)
    bc0 = A("bc0", [128, C], F32)
    bc1 = A("bc1", [128, C], F32)
    wdT = A("wdT", [128, NI * C], BF16)
    xbuf = A("xbuf", [128, 2 * C], F32)
    stat_sum = A("stat_sum", [128, NT], F32)
    stat_sq = A("stat_sq", [128, NT], F32)
    stat_m = A("stat_m", [128, NT], F32)
    stat_r = A("stat_r", [128, NT], F32)
    dsc = A("dsc", [128, C], F32)
    wsc = A("wsc", [128, 16], F32)
    wgusc = A("wgusc", [128, 44], F32)
    scal = A("scal", [1, 16], F32)
    scal_p = A("scal_p", [128, 4], F32)
    pmax = A("pmax", [128, 2], F32)
    ccs = A("ccs", [1, 128], F32)

    psA = nc.alloc_psum_tensor("psA", [128, 1024], F32)
    psB = nc.alloc_psum_tensor("psB", [128, 1024], F32)
    psC = nc.alloc_psum_tensor("psC", [128, 1024], F32)
    psD = nc.alloc_psum_tensor("psD", [128, 512], F32)
    psF = nc.alloc_psum_tensor("psF", [128, 512], F32)
    psAb = psA[:].bitcast(BF16)

    prog = Prog()
    E = prog.emit
    now = prog.now

    def dma(out, in_):
        return lambda e: e.dma_start(out=out, in_=in_)

    E("sync", dma(cv_s[:], dcvec), inc="ld", inc_by=16)
    E("sync", dma(clip_s[:], dclip), inc="ld", inc_by=16)
    E("sync", dma(sel_s[:], dsel), inc="ld", inc_by=16)
    E("sync", dma(eye16_s[:], deye16), inc="ld", inc_by=16)
    E("sync", dma(eye32_s[:], deye32), inc="ld", inc_by=16)
    ld0 = now("ld")
    E("vector", lambda e: e.memset(ones1b[:], 1.0))
    E("vector", lambda e: e.memset(ones1f[:], 1.0))
    E("vector", lambda e: e.memset(onesd[:], 1.0))
    E("vector", lambda e: e.memset(stat_sum[:], 0.0))
    E("vector", lambda e: e.memset(stat_sq[:], 0.0))
    E("vector", lambda e: e.memset(pmax[:], 0.0))
    E("vector", lambda e: e.memset(scal[:], 0.0))
    E("vector", lambda e: e.memset(scal_p[:], 0.0), inc="tv")

    E("scalar", lambda e: e.activation(cv_s[:], cv_s[:], AF.Silu),
      waits=[("ld", ld0)], inc="ta")
    E("tensor", lambda e: e.transpose(psD[0:128, 0:NC], cv_s[:], eye32_s[0:NC, 0:NC]),
      waits=[("ta", now("ta"))], inc="tt")
    E("vector", lambda e: e.tensor_copy(silucf32[:], psD[0:128, 0:NC]),
      waits=[("tt", now("tt"))])
    E("vector", lambda e: e.tensor_copy(silucf[:], psD[0:128, 0:NC]),
      inc="tv")

    for f in range(NC):
        E("sync", dma(wst, dwmod[f * 128:(f + 1) * 128, :]),
          waits=[("tv", now("tv"))], inc="ld", inc_by=16)
        ld_f = now("ld")
        for j in range(48):
            E("tensor",
              lambda e, f=f, j=j: e.matmul(
                  psC[:, j:j + 1], wst[:, j * 128:(j + 1) * 128],
                  silucf32[:, f:f + 1], start=True, stop=True),
              waits=[("tv", now("tv")), ("ld", ld_f)] if j == 0 else (),
              inc="tt" if j == 47 else None)
        if f == 0:
            E("vector", lambda e: e.tensor_copy(mod_fm[:], psC[:, 0:48]),
              waits=[("tt", now("tt"))], inc="tv")
        else:
            E("vector", lambda e: e.tensor_tensor(mod_fm[:], mod_fm[:],
                                                  psC[:, 0:48], OP.add),
              waits=[("tt", now("tt"))], inc="tv")
    if debug:
        E("sync", dma(dbg["mod"], mod_fm[:]), waits=[("tv", now("tv"))],
          inc="ld", inc_by=16)

    E("vector", lambda e: e.tensor_single_scalar(vec_fm[:, 0:8], mod_fm[:, 8:16], 1.0, OP.add))
    E("vector", lambda e: e.reciprocal(vec_fm[:, 8:16], vec_fm[:, 0:8]))
    E("vector", lambda e: e.tensor_tensor(vec_fm[:, 8:16], vec_fm[:, 8:16], mod_fm[:, 0:8], OP.mult))
    E("vector", lambda e: e.tensor_copy(sa_bf[:], vec_fm[:, 8:16]))
    for dst, src in [(16, 16), (40, 40)]:
        E("vector", lambda e, dst=dst, src=src: e.tensor_single_scalar(
            vec_fm[:, dst:dst + 8], mod_fm[:, src:src + 8], -1.0, OP.mult))
        E("vector", lambda e, dst=dst, src=src: e.tensor_tensor(
            vec_fm[:, dst:dst + 8], vec_fm[:, dst:dst + 8], mod_fm[:, src:src + 8], OP.max))
        E("vector", lambda e, dst=dst: e.tensor_single_scalar(
            vec_fm[:, dst:dst + 8], vec_fm[:, dst:dst + 8], 0.25, OP.max))
        E("vector", lambda e, src=src: e.tensor_scalar(
            vec_fm[:, 56:64], mod_fm[:, src:src + 8], 0.0, 2.0, OP.is_ge, OP.mult))
        E("vector", lambda e: e.tensor_single_scalar(
            vec_fm[:, 56:64], vec_fm[:, 56:64], -1.0, OP.add))
        E("vector", lambda e, dst=dst: e.tensor_tensor(
            vec_fm[:, dst:dst + 8], vec_fm[:, dst:dst + 8], vec_fm[:, 56:64], OP.mult))
    E("vector", lambda e: e.tensor_single_scalar(vec_fm[:, 24:32], mod_fm[:, 32:40], 1.0, OP.add))
    E("vector", lambda e: e.tensor_copy(vec_fm[:, 32:40], mod_fm[:, 24:32]), inc="tv")
    vec_tv = now("tv")

    E("tensor", lambda e: e.transpose(psD[0:32, 0:128], vec_fm[:, 16:48], eye32_s[:]),
      waits=[("tv", vec_tv)], inc="tt")
    E("vector", lambda e: e.tensor_copy(rowst[:], psD[0:32, 0:128]),
      waits=[("tt", now("tt"))], inc="tv")
    E("sync", dma(rowb[0:1, 0:4096], rowst[:]), waits=[("tv", now("tv"))],
      inc="ld", inc_by=16)
    rows_ld = now("ld")

    for f in range(NC):
        E("sync", dma(wst[:, 0:3 * C], dwqkv[f * 128:(f + 1) * 128, :]),
          waits=[("tv", now("tv"))], inc="ld", inc_by=16)
        E("vector",
          lambda e, f=f: e.tensor_scalar(wqkv_s[:, f * 3 * C:(f + 1) * 3 * C],
                                         wst[:, 0:3 * C], vec_fm[:, f:f + 1],
                                         None, OP.mult),
          waits=[("ld", now("ld"))], inc="tv")
    wqkv_tv = now("tv")

    for f in range(NC):
        for j in range(24):
            E("tensor",
              lambda e, j=j, f=f: e.matmul(
                  psD[:, 64 + j:65 + j],
                  wqkv_s[:, f * 3 * C + j * 128:f * 3 * C + (j + 1) * 128],
                  sa_bf[:, f:f + 1], start=True, stop=True),
              waits=[("tv", now("tv"))] if j == 0 else (),
              inc="tt" if j == 23 else None)
        if f == 0:
            E("vector", lambda e: e.tensor_copy(bias_fm[:], psD[:, 64:88]),
              waits=[("tt", now("tt"))], inc="tv")
        else:
            E("vector", lambda e: e.tensor_tensor(bias_fm[:], bias_fm[:],
                                                  psD[:, 64:88], OP.add),
              waits=[("tt", now("tt"))], inc="tv")
    E("vector", lambda e: e.tensor_copy(bias_fmb[:], bias_fm[:]), inc="tv")
    E("tensor", lambda e: e.transpose(psAb[0:24, 0:128], bias_fmb[:], eye16_s[:]),
      waits=[("tv", now("tv"))], inc="tt")
    E("vector", lambda e: e.tensor_copy(rowst.bitcast(BF16)[0:24, 0:128],
                                        psAb[0:24, 0:128]),
      waits=[("tt", now("tt"))], inc="tv")
    E("sync", dma(rowb_bf[0:1, 0:3072], rowst.bitcast(BF16)[0:24, 0:128]),
      waits=[("tv", now("tv"))], inc="ld", inc_by=16)
    E("sync", dma(qkvbias_row[:], rowb_bf[0:1, 0:3 * C]), waits=[("ld", now("ld"))],
      inc="ld", inc_by=16)

    for f in range(NC):
        E("sync", dma(wst[:, 0:C], dwproj[f * 128:(f + 1) * 128, :]),
          waits=[("tv", now("tv"))], inc="ld", inc_by=16)
        E("vector", lambda e: e.tensor_copy(wqi[:, 0:C], wst[:, 0:C]),
          waits=[("ld", now("ld"))], inc="tv")
        E("sync", dma(wpj_sp[f], wqi[:, 0:C]), waits=[("tv", now("tv"))],
          inc="ld", inc_by=16)
    wpj_spill_ld = now("ld")

    def quant_chunk(width, keep_scale_col=None, keep_tensor=None):
        E("vector", lambda e, width=width: e.tensor_reduce(
            wsc[:, 0:1], wst[:, 0:width], AX.X, OP.max, apply_absolute_value=True),
          waits=[("ld", now("ld"))])
        E("vector", lambda e: e.reciprocal(wsc[:, 1:2], wsc[:, 0:1]))
        E("vector", lambda e: e.tensor_single_scalar(wsc[:, 1:2], wsc[:, 1:2], QMAX, OP.mult))
        if keep_scale_col is not None:
            kt = wsc if keep_tensor is None else keep_tensor
            E("vector", lambda e, k=keep_scale_col, kt=kt: e.tensor_single_scalar(
                kt[:, k:k + 1], wsc[:, 0:1], 1.0 / QMAX, OP.mult))
        E("vector", lambda e, width=width: e.tensor_scalar(
            wint[:, 0:width], wst[:, 0:width], wsc[:, 1:2], None, OP.mult))
        E("vector", lambda e, width=width: e.tensor_copy(wqi[:, 0:width],
                                                         wint[:, 0:width]), inc="tv")

    wgu_spill_ld = 0
    for wi, (src, spT) in enumerate([(dwg, wgT_sp), (dwu, wuT_sp)]):
        for cch in range(NI):
            E("sync", dma(wst[:, 0:C], src[cch * 128:(cch + 1) * 128, :]),
              waits=[("tv", now("tv"))], inc="ld", inc_by=16)
            quant_chunk(C, keep_scale_col=wi * NI + cch, keep_tensor=wgusc)
            for t in range(NC):
                E("tensor",
                  lambda e, t=t: e.transpose(psAb[:, t * 128:(t + 1) * 128],
                                             wqi[:, t * 128:(t + 1) * 128],
                                             eye16_s[:]),
                  waits=[("tv", now("tv")), ("ld", wgu_spill_ld)] if t == 0 else (),
                  inc="tt" if t == NC - 1 else None)
            E("vector", lambda e: e.tensor_copy(ksc[:, 0:C], psAb[:, 0:1024]),
              waits=[("tt", now("tt"))], inc="tv")
            E("sync", dma(spT[cch], ksc[:, 0:C]), waits=[("tv", now("tv"))],
              inc="ld", inc_by=16)
            wgu_spill_ld = now("ld")
    wdT3 = wdT[:].rearrange("p (cc hc) -> p cc hc", cc=NI)
    for hh in range(NC):
        E("sync", dma(wst[:, 0:FI], dwd[hh * 128:(hh + 1) * 128, :]),
          waits=[("tv", now("tv"))], inc="ld", inc_by=16)
        quant_chunk(FI, keep_scale_col=2 + hh)
        for grp in range(3):
            lo = grp * 8
            n = min(8, NI - lo)
            for u in range(n):
                E("tensor",
                  lambda e, lo=lo, u=u: e.transpose(
                      psAb[:, u * 128:(u + 1) * 128],
                      wqi[:, (lo + u) * 128:(lo + u + 1) * 128], eye16_s[:]),
                  waits=[("tv", now("tv"))] if (grp == 0 and u == 0) else
                        ([("tv", now("tv"))] if u == 0 else ()),
                  inc="tt" if u == n - 1 else None)
            E("vector",
              lambda e, lo=lo, n=n, hh=hh: e.tensor_copy(
                  wdT3[:, lo:lo + n, hh * 128:(hh + 1) * 128],
                  psAb[:, 0:n * 128].rearrange("p (a b) -> p a b", a=n)),
              waits=[("tt", now("tt"))], inc="tv")
    setup_tv = now("tv")

    r_inv = 1.0 / C
    qkv_ps = {0: psA, 1: psA, 2: psB, 3: psB, 4: psC, 5: psC}
    ev_tv = {}
    sp_ld = {}
    xload_ld = {}
    apply_tv = {}
    tadd_tv = {}
    h1T_tt = {}
    h1ev_ta = {}
    qkv_tt = {}
    last_spill_ld = 0

    for t in range(NT):
        our = t < NTO
        bi = t % 2
        E("sync", dma(xbuf[:, bi * C:(bi + 1) * C], dx[t * 128:(t + 1) * 128, :]),
          waits=[("tv", apply_tv.get(t - 2, 0)), ("ta", h1ev_ta.get(t - 2, 0))],
          inc="ld", inc_by=16)
        E("sync", dma(tbuf[:, bi * 2048:(bi + 1) * 2048],
                      dtab[t * 128:(t + 1) * 128, :]),
          waits=[("tv", max(tadd_tv.get(t - 2, 0), setup_tv if t < 2 else 0))],
          inc="ld", inc_by=16)
        xload_ld[t] = now("ld")
        E("scalar", lambda e, bi=bi, t=t: e.activation(
            dsc[:], xbuf[:, bi * C:(bi + 1) * C], AF.Identity,
            accum_out=stat_sum[:, t:t + 1]),
          waits=[("ld", xload_ld[t]), ("tv", setup_tv if t == 0 else 0)])
        E("scalar", lambda e, bi=bi, t=t: e.activation(
            dsc[:], xbuf[:, bi * C:(bi + 1) * C], AF.Square,
            accum_out=stat_sq[:, t:t + 1]), inc="ta")
        E("vector", lambda e, t=t: e.tensor_single_scalar(
            stat_m[:, t:t + 1], stat_sum[:, t:t + 1], r_inv, OP.mult),
          waits=[("ta", now("ta"))])
        E("vector", lambda e, t=t: e.tensor_single_scalar(
            stat_sq[:, t:t + 1], stat_sq[:, t:t + 1], r_inv, OP.mult))
        E("vector", lambda e, t=t: e.tensor_tensor(
            stat_sum[:, t:t + 1], stat_m[:, t:t + 1], stat_m[:, t:t + 1], OP.mult))
        E("vector", lambda e, t=t: e.scalar_tensor_tensor(
            stat_sq[:, t:t + 1], stat_sq[:, t:t + 1], EPS, stat_sum[:, t:t + 1],
            OP.add, OP.subtract))
        E("vector", lambda e, t=t: e.reciprocal(stat_sum[:, t:t + 1],
                                                stat_sq[:, t:t + 1]), inc="tv")
        E("scalar", lambda e, t=t: e.activation(stat_r[:, t:t + 1],
                                                stat_sum[:, t:t + 1], AF.Sqrt),
          waits=[("tv", now("tv"))], inc="ta")
        E("vector", lambda e, bi=bi, t=t: e.tensor_scalar(
            h1t[:, bi * C:(bi + 1) * C], xbuf[:, bi * C:(bi + 1) * C],
            stat_m[:, t:t + 1], stat_r[:, t:t + 1], OP.subtract, OP.mult),
          waits=[("ta", now("ta")), ("tt", h1T_tt.get(t - 2, 0))], inc="tv")
        apply_tv[t] = now("tv")
        if debug:
            E("sync", dma(dbg["h1"][t * 128:(t + 1) * 128, :],
                          h1t[:, bi * C:(bi + 1) * C]),
              waits=[("tv", now("tv"))], inc="ld", inc_by=16)
        for f in range(NC):
            E("tensor", lambda e, f=f, bi=bi: e.transpose(
                psAb[:, f * 128:(f + 1) * 128],
                h1t[:, bi * C + f * 128:bi * C + (f + 1) * 128], eye16_s[:]),
              waits=([("tv", now("tv")), ("ta", h1ev_ta.get(t - 1, 0))]
                     if f == 0 else ()),
              inc="tt" if f == NC - 1 else None)
        h1T_tt[t] = now("tt")
        E("scalar", lambda e, bi=bi: e.activation(h1fm[:, bi * C:(bi + 1) * C],
                                                  psAb[:, 0:1024], AF.Identity),
          waits=[("tt", now("tt")), ("tt", qkv_tt.get((t - 2, 2), 0))], inc="ta")
        h1ev_ta[t] = now("ta")

        slices = list(range(6)) if our else [2, 3, 4, 5]
        for sl in slices:
            ps = qkv_ps[sl]
            col = (sl % 2) * 512
            grp = sl // 2
            w = [("ta", h1ev_ta[t])]
            if (t - 1, grp) in ev_tv:
                w.append(("tv", ev_tv[(t - 1, grp)]))
            elif grp == 0 and (NTO - 1, 0) in ev_tv:
                w.append(("tv", ev_tv[(NTO - 1, 0)]))
            E("tensor", lambda e, ps=ps, col=col, sl=sl: e.matmul(
                ps[:, col:col + 512], ones1b[:],
                qkvbias_row[0:1, sl * 512:(sl + 1) * 512], start=True, stop=False),
              waits=w)
            for f in range(NC):
                E("tensor", lambda e, ps=ps, col=col, sl=sl, f=f, bi=bi: e.matmul(
                    ps[:, col:col + 512],
                    h1fm[:, bi * C + f * 128:bi * C + (f + 1) * 128],
                    wqkv_s[:, f * 3 * C + sl * 512:f * 3 * C + (sl + 1) * 512],
                    start=False, stop=(f == NC - 1)),
                  inc="tt" if (f == NC - 1 and sl % 2 == 1) else None)
            if sl % 2 == 1:
                qkv_tt[(t, grp)] = now("tt")

        E("vector", lambda e, bi=bi: e.tensor_tensor(
            ceff[:, bi * 1024:bi * 1024 + 512],
            tbuf[:, bi * 2048:bi * 2048 + 512],
            tbuf[:, bi * 2048 + 512:bi * 2048 + 1024], OP.add))
        E("vector", lambda e, bi=bi: e.tensor_tensor(
            ceff[:, bi * 1024 + 512:bi * 1024 + 1024],
            tbuf[:, bi * 2048 + 1024:bi * 2048 + 1536],
            tbuf[:, bi * 2048 + 1536:bi * 2048 + 2048], OP.add), inc="tv")
        tadd_tv[t] = now("tv")

        def rope(psrc, dst, grp):
            co = bi * 1024
            do = bi * C
            re_v = psrc[:].rearrange("p (h s i) -> p h s i", h=16, s=2)[:, :, 0, :]
            im_v = psrc[:].rearrange("p (h s i) -> p h s i", h=16, s=2)[:, :, 1, :]
            cosv = ceff[:, co:co + 512].rearrange("p (h i) -> p h i", h=16)
            sinv = ceff[:, co + 512:co + 1024].rearrange("p (h i) -> p h i", h=16)
            o_re = dst[:, do:do + C].rearrange("p (h s i) -> p h s i", h=16, s=2)[:, :, 0, :]
            o_im = dst[:, do:do + C].rearrange("p (h s i) -> p h s i", h=16, s=2)[:, :, 1, :]
            sc_a = rsc32[:, 0:512].rearrange("p (h i) -> p h i", h=16)
            sc_b = rsc32[:, 512:1024].rearrange("p (h i) -> p h i", h=16)
            w0 = [("tt", qkv_tt[(t, grp)]), ("tv", now("tv")),
                  ("ld", sp_ld.get((t - 2, grp), 0))]
            E("vector", lambda e: e.tensor_tensor(sc_a, re_v, cosv, OP.mult), waits=w0)
            E("vector", lambda e: e.tensor_tensor(sc_b, im_v, sinv, OP.mult))
            E("vector", lambda e: e.tensor_tensor(o_re, sc_a, sc_b, OP.subtract))
            E("vector", lambda e: e.tensor_tensor(sc_a, re_v, sinv, OP.mult))
            E("vector", lambda e: e.tensor_tensor(sc_b, im_v, cosv, OP.mult))
            E("vector", lambda e: e.tensor_tensor(o_im, sc_a, sc_b, OP.add), inc="tv")

        if our:
            rope(psA, qsc, 0)
            ev_tv[(t, 0)] = now("tv")
            E("sync", dma(q_sp[t], qsc[:, bi * C:(bi + 1) * C]),
              waits=[("tv", now("tv"))], inc="ld", inc_by=16)
            sp_ld[(t, 0)] = now("ld")
            if debug:
                E("sync", dma(dbg["qpost"][t * 128:(t + 1) * 128, :],
                              qsc[:, bi * C:(bi + 1) * C]), inc="ld", inc_by=16)
        rope(psB, ksc, 1)
        ev_tv[(t, 1)] = now("tv")
        E("sync", dma(k_sp[t], ksc[:, bi * C:(bi + 1) * C]),
          waits=[("tv", now("tv"))], inc="ld", inc_by=16)
        sp_ld[(t, 1)] = now("ld")
        if debug:
            E("sync", dma(dbg["kpost"][t * 128:(t + 1) * 128, :],
                          ksc[:, bi * C:(bi + 1) * C]), inc="ld", inc_by=16)
        E("vector", lambda e, bi=bi: e.tensor_copy(vsc[:, bi * C:(bi + 1) * C], psC[:]),
          waits=[("tt", qkv_tt[(t, 2)]), ("ld", sp_ld.get((t - 2, 2), 0))], inc="tv")
        ev_tv[(t, 2)] = now("tv")
        E("sync", dma(v_sp[t], vsc[:, bi * C:(bi + 1) * C]),
          waits=[("tv", now("tv"))], inc="ld", inc_by=16)
        sp_ld[(t, 2)] = now("ld")
        last_spill_ld = now("ld")

    abuf_ld = {}
    hp_last_tv = {}
    tpose_ta = {}
    attn_tv = 0
    gate_drow_ld = 0
    gate_rrow_tt = 0
    gate_rbt_tv = 0
    rb_ta = 0
    for hp in range(8):
        bi = hp % 2
        wv = [("tv", hp_last_tv.get(hp - 2, 0)), ("ld", last_spill_ld)]
        E("sync", dma(
            kbuf[:, bi * 4096:(bi + 1) * 4096].rearrange("p (t c) -> p t c", t=NT),
            k_sp.ap().rearrange("t p c -> p t c")[:, :, hp * 128:(hp + 1) * 128]),
          waits=wv, inc="ld", inc_by=16)
        E("sync", dma(
            vbuf[:, bi * 4096:(bi + 1) * 4096].rearrange("p (t c) -> p t c", t=NT),
            v_sp.ap().rearrange("t p c -> p t c")[:, :, hp * 128:(hp + 1) * 128]),
          inc="ld", inc_by=16)
        E("sync", dma(
            qbuf[:, bi * 2048:(bi + 1) * 2048].rearrange("p (t c) -> p t c", t=NTO),
            q_sp.ap().rearrange("t p c -> p t c")[:, :, hp * 128:(hp + 1) * 128]),
          inc="ld", inc_by=16)
        abuf_ld[hp] = now("ld")

        for grp in range(6):
            if grp < 2:
                srcb, soff, dstt, g = qbuf, bi * 2048, qfm, grp
            else:
                srcb, soff, dstt, g = kbuf, bi * 4096, kfm, grp - 2
            for u in range(8):
                idx = g * 8 + u
                E("tensor", lambda e, srcb=srcb, soff=soff, idx=idx, u=u:
                  e.transpose(psAb[:, u * 128:(u + 1) * 128],
                              srcb[:, soff + idx * 128:soff + (idx + 1) * 128],
                              eye16_s[:]),
                  waits=([("ld", abuf_ld[hp]), ("ta", now("ta")),
                          ("tv", hp_last_tv.get(hp - 1, 0))] if u == 0 else ()),
                  inc="tt" if u == 7 else None)
            E("scalar", lambda e, dstt=dstt, g=g: e.activation(
                dstt[:, g * 1024:(g + 1) * 1024], psAb[:, 0:1024], AF.Identity),
              waits=[("tt", now("tt"))], inc="ta")
        tpose_ta[hp] = now("ta")

        for qb in range(4):
            st_tt = {}
            exp_ta = {}
            pv_tt = {}
            ev_done_tv = attn_tv
            for j in range(16):
                for g2 in range(2):
                    kc = j * 2 + g2
                    for hi, (ps, rl) in enumerate([(psA, 0), (psB, 64)]):
                        w = []
                        if g2 == 0 and hi == 0:
                            w = [("ta", tpose_ta[hp]),
                                 ("ta", exp_ta.get((0, j - 1), 0)),
                                 ("ta", exp_ta.get((1, j - 1), 0))]
                        E("tensor", lambda e, ps=ps, rl=rl, kc=kc, qb=qb, g2=g2:
                          e.matmul(ps[:, g2 * 512:(g2 + 1) * 512],
                                   kfm[rl:rl + 64, kc * 128:(kc + 1) * 128],
                                   qfm[rl:rl + 64, qb * 512:(qb + 1) * 512],
                                   start=True, stop=True, tile_position=(rl, 0)),
                          waits=w,
                          inc="tt" if g2 == 1 else None)
                        st_tt[(hi, kc)] = now("tt")
                for hi, (ps, pt) in enumerate([(psA, ptA), (psB, ptB)]):
                    w = [("tt", st_tt[(hi, j * 2 + 1)])]
                    if (hi, j - 2) in pv_tt:
                        w.append(("tt", pv_tt[(hi, j - 2)]))
                    E("scalar", lambda e, ps=ps, pt=pt, j=j: e.activation(
                        pt[:, (j % 2) * 1024:(j % 2 + 1) * 1024], ps[:],
                        AF.Exp, scale=0.125),
                      waits=w, inc="ta")
                    exp_ta[(hi, j)] = now("ta")
                for g2 in range(2):
                    kc = j * 2 + g2
                    for hi, pt in enumerate([ptA, ptB]):
                        cl = hi * 64
                        pv_dst = psC[0:64, 0:512] if hi == 0 else psC[64:128, 512:1024]
                        dn_dst = psD[0:1, 0:512] if hi == 0 else psF[64:65, 0:512]
                        ptap = pt[:, (j % 2) * 1024 + g2 * 512:
                                  (j % 2) * 1024 + (g2 + 1) * 512]
                        w = []
                        if g2 == 0:
                            w = [("ta", exp_ta[(hi, j)])]
                        if kc == 0:
                            w.append(("tv", ev_done_tv))
                            w.append(("ta", rb_ta))
                        E("tensor", lambda e, ptap=ptap, kc=kc, cl=cl, bi=bi, hi=hi,
                          pv_dst=pv_dst:
                          e.matmul(pv_dst,
                                   vbuf[:, bi * 4096 + kc * 128 + hi * 64:
                                        bi * 4096 + kc * 128 + hi * 64 + 64],
                                   ptap, start=(kc == 0), stop=(kc == NT - 1),
                                   tile_position=(0, cl)),
                          waits=w, inc=None)
                        E("tensor", lambda e, ptap=ptap, kc=kc, cl=cl, dn_dst=dn_dst:
                          e.matmul(dn_dst, onesd[:], ptap,
                                   start=(kc == 0), stop=(kc == NT - 1),
                                   tile_position=(0, cl)),
                          inc="tt" if g2 == 1 else None)
                        if g2 == 1:
                            pv_tt[(hi, j)] = now("tt")
            pv_done_tt = now("tt")
            E("scalar", lambda e: e.activation(drow[0:1, 0:512], psD[0:1, 0:512],
                                               AF.Identity),
              waits=[("tt", pv_done_tt), ("ld", gate_drow_ld)])
            E("scalar", lambda e: e.activation(drow[64:65, 512:1024],
                                               psF[64:65, 0:512], AF.Identity),
              inc="ta")
            dr_ta = now("ta")
            E("sync", dma(rrow2[0:1, :], drow[0:1, 0:512]),
              waits=[("ta", dr_ta), ("tt", gate_rrow_tt)],
              inc="ld", inc_by=16)
            E("sync", dma(rrow2[1:2, :], drow[64:65, 512:1024]),
              inc="ld", inc_by=16)
            gate_drow_ld = now("ld")
            E("vector", lambda e: e.reciprocal(rrow2[:], rrow2[:]),
              waits=[("ld", now("ld"))], inc="tv")
            E("tensor", lambda e: e.matmul(psD[:, 0:512], sel_s[:], rrow2[:],
                                           start=True, stop=True),
              waits=[("tv", now("tv")), ("ta", dr_ta)], inc="tt")
            gate_rrow_tt = now("tt")
            E("scalar", lambda e: e.activation(rbt[:], psD[:, 0:512], AF.Identity),
              waits=[("tt", now("tt")), ("tv", gate_rbt_tv)], inc="ta")
            rb_ta = now("ta")
            ao = hp * 2048 + qb * 512
            E("vector", lambda e, ao=ao: e.tensor_tensor(
                attnout[0:64, ao:ao + 512], psC[0:64, 0:512], rbt[0:64, :], OP.mult),
              waits=[("ta", now("ta"))])
            E("vector", lambda e, ao=ao: e.tensor_tensor(
                attnout[64:128, ao:ao + 512], psC[64:128, 512:1024],
                rbt[64:128, :], OP.mult), inc="tv")
            attn_tv = now("tv")
            gate_rbt_tv = now("tv")
        hp_last_tv[hp] = now("tv")
    if debug:
        for f in range(NC):
            E("sync", dma(dbg["attnout"][:, f * 2048:(f + 1) * 2048],
                          attnout[:, f * 2048:(f + 1) * 2048]),
              waits=[("tv", attn_tv)], inc="ld", inc_by=16)

    for tile, off in [(bc0, 0), (bc1, -1)]:
        rsrc = dbproj if off < 0 else rowb[0:1, off:off + C]
        E("sync", dma(rowtmp[:], rsrc),
          waits=[("ld", rows_ld), ("tt", now("tt"))], inc="ld", inc_by=16)
        E("tensor", lambda e: e.matmul(psA[:, 0:512], ones1f[:], rowtmp[0:1, 0:512],
                                       start=True, stop=True),
          waits=[("ld", now("ld")), ("ta", now("ta")),
                 ("tv", max(attn_tv, now("tv")))])
        E("tensor", lambda e: e.matmul(psA[:, 512:1024], ones1f[:],
                                       rowtmp[0:1, 512:1024], start=True, stop=True),
          inc="tt")
        E("vector", lambda e, tile=tile: e.tensor_copy(tile[:], psA[:]),
          waits=[("tt", now("tt"))], inc="tv")
    bc_tv = now("tv")

    pj_ld = {}
    pj_ev_ta = {}
    dl_sp_ld = {}
    for fo in range(NC):
        bi = fo % 2
        E("sync", dma(wpjbuf[:, bi * C:(bi + 1) * C].rearrange(
            "p (f c) -> p f c", f=NC),
            wpj_sp.ap().rearrange("f p c -> p f c")[:, :, fo * 128:(fo + 1) * 128]),
          waits=[("ld", wpj_spill_ld), ("tt", pj_ld.get(fo - 2, 0)),
                 ("tv", attn_tv if fo < 2 else 0)],
          inc="ld", inc_by=16)
        wld = now("ld")
        for sl in range(4):
            ps = (psA, psA, psB, psB)[sl]
            col = (sl % 2) * 512
            for f in range(NC):
                w = []
                if sl == 0 and f == 0:
                    w = [("ld", wld), ("tv", bc_tv), ("ta", pj_ev_ta.get(fo - 1, 0))]
                E("tensor", lambda e, ps=ps, col=col, sl=sl, f=f, fo=fo, bi=bi:
                  e.matmul(ps[:, col:col + 512],
                           wpjbuf[:, bi * C + f * 128:bi * C + (f + 1) * 128],
                           attnout[:, f * 2048 + sl * 512:f * 2048 + (sl + 1) * 512],
                           start=(f == 0), stop=(f == NC - 1)),
                  waits=w, inc="tt" if f == NC - 1 else None)
        pj_ld[fo] = now("tt")
        E("scalar", lambda e: e.activation(ksc[:, 0:C], psA[:], AF.Identity),
          waits=[("tt", now("tt"))])
        E("scalar", lambda e: e.activation(ksc[:, C:2 * C], psB[:], AF.Identity),
          inc="ta")
        ev_ta = now("ta")
        for grp in range(2):
            for u in range(8):
                tt_i = grp * 8 + u
                E("tensor", lambda e, tt_i=tt_i, u=u: e.transpose(
                    psAb[:, u * 128:(u + 1) * 128],
                    ksc[:, tt_i * 128:(tt_i + 1) * 128], eye16_s[:]),
                  waits=([("ta", ev_ta), ("ta", pj_ev_ta.get((fo, grp - 1),
                                                            pj_ev_ta.get((fo - 1, 1), 0)))]
                         if u == 0 else ()),
                  inc="tt" if u == 7 else None)
            E("scalar", lambda e, grp=grp: e.activation(qsc[:, grp * C:(grp + 1) * C],
                                                        psAb[:, 0:1024], AF.Identity),
              waits=[("tt", now("tt")), ("ld", dl_sp_ld.get((fo - 1, grp), 0))],
              inc="ta")
            pj_ev_ta[(fo, grp)] = now("ta")
            E("sync", dma(
                bass.AP(dl_sp, grp * 8 * 128 * 1024 + fo * 128,
                        [[1024, 128], [131072, 8], [1, 128]]),
                qsc[:, grp * C:(grp + 1) * C].rearrange("p (a b) -> p a b", a=8)),
              waits=[("ta", now("ta"))], inc="ld", inc_by=16)
            dl_sp_ld[(fo, grp)] = now("ld")
    dl_done_ld = now("ld")
    pj_ev_ta_last = now("ta")

    x1_tv = {}
    x1_ld = {}
    xq_tt = {}
    for t in range(NTO):
        bi = t % 2
        E("sync", dma(xbuf[:, bi * C:(bi + 1) * C], dx[t * 128:(t + 1) * 128, :]),
          waits=[("tv", x1_tv.get(t - 2, 0))], inc="ld", inc_by=16)
        E("sync", dma(ksc[:, bi * C:(bi + 1) * C], dl_sp[t]),
          waits=[("ld", dl_done_ld), ("tv", x1_tv.get(t - 2, 0)),
                 ("ta", pj_ev_ta_last if t < 2 else 0)],
          inc="ld", inc_by=16)
        ld_t = now("ld")
        E("vector", lambda e, bi=bi: e.tensor_tensor(
            dsc[:], ksc[:, bi * C:(bi + 1) * C], bc1[:], OP.add),
          waits=[("ld", ld_t)])
        E("vector", lambda e: e.tensor_tensor(dsc[:], dsc[:], bc0[:], OP.mult))
        E("vector", lambda e, t=t, bi=bi: e.tensor_tensor(
            x1stg[:, bi * C:(bi + 1) * C], dsc[:], xbuf[:, bi * C:(bi + 1) * C],
            OP.add), inc="tv")
        x1_tv[t] = now("tv")
        E("sync", dma(x1_sp[t], x1stg[:, bi * C:(bi + 1) * C]),
          waits=[("tv", now("tv"))], inc="ld", inc_by=16)
        x1_ld[t] = now("ld")
        if debug:
            E("sync", dma(dbg["x1"][t * 128:(t + 1) * 128, :],
                          x1stg[:, bi * C:(bi + 1) * C]), inc="ld", inc_by=16)
    x1_done_ld = now("ld")

    for tile, off in [(bc0, C), (bc1, 2 * C)]:
        E("sync", dma(rowtmp[:], rowb[0:1, off:off + C]),
          waits=[("tt", now("tt"))], inc="ld", inc_by=16)
        E("tensor", lambda e: e.matmul(psA[:, 0:512], ones1f[:], rowtmp[0:1, 0:512],
                                       start=True, stop=True),
          waits=[("ld", now("ld")), ("tv", now("tv")), ("ta", now("ta"))])
        E("tensor", lambda e: e.matmul(psA[:, 512:1024], ones1f[:],
                                       rowtmp[0:1, 512:1024], start=True, stop=True),
          inc="tt")
        E("vector", lambda e, tile=tile: e.tensor_copy(tile[:], psA[:]),
          waits=[("tt", now("tt"))], inc="tv")
    E("vector", lambda e: e.memset(stat_sum[:], 0.0))
    E("vector", lambda e: e.memset(stat_sq[:], 0.0), inc="tv")
    p4bc_tv = now("tv")

    st_ld = {}
    st_ta = {}
    st_tv = {}
    for t in range(NTO):
        bi = t % 2
        E("sync", dma(xbuf[:, bi * C:(bi + 1) * C], x1_sp[t]),
          waits=[("ld", x1_done_ld), ("ta", st_ta.get(t - 2, 0)),
                 ("tv", max(p4bc_tv if t < 2 else 0, st_tv.get(t - 2, 0)))],
          inc="ld", inc_by=16)
        st_ld[t] = now("ld")
        E("scalar", lambda e, bi=bi, t=t: e.activation(
            dsc[:], xbuf[:, bi * C:(bi + 1) * C], AF.Identity,
            accum_out=stat_sum[:, t:t + 1]), waits=[("ld", st_ld[t])])
        E("scalar", lambda e, bi=bi, t=t: e.activation(
            dsc[:], xbuf[:, bi * C:(bi + 1) * C], AF.Square,
            accum_out=stat_sq[:, t:t + 1]), inc="ta")
        st_ta[t] = now("ta")
        E("vector", lambda e, t=t: e.tensor_single_scalar(
            stat_m[:, t:t + 1], stat_sum[:, t:t + 1], r_inv, OP.mult),
          waits=[("ta", st_ta[t])])
        E("vector", lambda e, t=t: e.tensor_single_scalar(
            stat_sq[:, t:t + 1], stat_sq[:, t:t + 1], r_inv, OP.mult))
        E("vector", lambda e, t=t: e.tensor_tensor(
            stat_sum[:, t:t + 1], stat_m[:, t:t + 1], stat_m[:, t:t + 1], OP.mult))
        E("vector", lambda e, t=t: e.scalar_tensor_tensor(
            stat_sq[:, t:t + 1], stat_sq[:, t:t + 1], EPS, stat_sum[:, t:t + 1],
            OP.add, OP.subtract))
        E("vector", lambda e, t=t: e.reciprocal(stat_sum[:, t:t + 1],
                                                stat_sq[:, t:t + 1]), inc="tv")
        E("scalar", lambda e, t=t: e.activation(stat_r[:, t:t + 1],
                                                stat_sum[:, t:t + 1], AF.Sqrt),
          waits=[("tv", now("tv"))], inc="ta")
        E("vector", lambda e, t=t, bi=bi: e.tensor_scalar(
            h2sc[:], xbuf[:, bi * C:(bi + 1) * C], stat_m[:, t:t + 1],
            stat_r[:, t:t + 1], OP.subtract, OP.mult),
          waits=[("ta", now("ta"))])
        E("vector", lambda e: e.tensor_tensor(h2sc[:], h2sc[:], bc0[:], OP.mult))
        E("vector", lambda e: e.tensor_tensor(h2sc[:], h2sc[:], bc1[:], OP.add))
        E("vector", lambda e: e.tensor_reduce(wsc[:, 12:13], h2sc[:], AX.X, OP.max,
                                              apply_absolute_value=True))
        E("vector", lambda e: e.tensor_tensor(pmax[:, 0:1], pmax[:, 0:1],
                                              wsc[:, 12:13], OP.max), inc="tv")
        st_tv[t] = now("tv")
    E("tensor", lambda e: e.transpose(psD[0:1, 0:128], pmax[:, 0:1], eye32_s[:]),
      waits=[("tv", now("tv"))], inc="tt")
    E("vector", lambda e: e.tensor_copy(ccs[:], psD[0:1, 0:128]),
      waits=[("tt", now("tt"))], inc="tv")
    E("pool", lambda e: e.dma_start(out=cc1_in[0:1, :], in_=ccs[:]),
      waits=[("tv", now("tv"))], inc="tp", inc_by=16)
    E("pool", lambda e: e.collective_compute(
        "AllReduce", OP.max, replica_groups=[list(range(8))],
        ins=[cc1_in.ap().opt()], outs=[cc1_out.ap().opt()]), inc="cc")
    E("pool", lambda e: e.dma_start(out=ccs[:], in_=cc1_out[0:1, :]),
      waits=[("cc", now("cc"))], inc="tp", inc_by=16)
    E("vector", lambda e: e.tensor_reduce(scal[0:1, 0:1], ccs[:], AX.X, OP.max),
      waits=[("tp", now("tp"))], inc="tv")
    E("scalar", lambda e: e.activation(scal[0:1, 2:3], clip_s[0:1, 0:1], AF.Sigmoid),
      waits=[("tp", now("tp"))], inc="ta")
    E("vector", lambda e: e.tensor_tensor(scal[0:1, 4:5], scal[0:1, 0:1],
                                          scal[0:1, 2:3], OP.mult),
      waits=[("ta", now("ta"))])
    E("vector", lambda e: e.tensor_single_scalar(scal[0:1, 4:5], scal[0:1, 4:5],
                                                 1.0 / QMAX, OP.mult))
    E("vector", lambda e: e.reciprocal(scal[0:1, 5:6], scal[0:1, 4:5]), inc="tv")
    E("tensor", lambda e: e.matmul(psD[:, 0:2], ones1f[:], scal[0:1, 4:6],
                                   start=True, stop=True),
      waits=[("tv", now("tv"))], inc="tt")
    E("vector", lambda e: e.tensor_copy(scal_p[:, 0:2], psD[:, 0:2]),
      waits=[("tt", now("tt"))], inc="tv")
    if debug:
        E("sync", dma(dbg["scal"], scal[:]), waits=[("tv", now("tv"))],
          inc="ld", inc_by=16)

    xq_ev_ta = {}
    xqv_tv = {}
    xqfm3 = xqfm.rearrange("p (f tok) -> p f tok", f=NC)
    for t in range(NTO):
        bi = t % 2
        E("sync", dma(xbuf[:, bi * C:(bi + 1) * C], x1_sp[t]),
          waits=[("tv", max(now("tv") if t < 2 else 0, xqv_tv.get(t - 2, 0))),
                 ("tt", xq_tt.get(t - 2, 0))], inc="ld", inc_by=16)
        ld_t = now("ld")
        E("vector", lambda e, t=t, bi=bi: e.tensor_scalar(
            qsc32[:, bi * C:(bi + 1) * C], xbuf[:, bi * C:(bi + 1) * C],
            stat_m[:, t:t + 1], stat_r[:, t:t + 1], OP.subtract, OP.mult),
          waits=[("ld", ld_t)])
        E("vector", lambda e, bi=bi: e.tensor_tensor(
            qsc32[:, bi * C:(bi + 1) * C], qsc32[:, bi * C:(bi + 1) * C],
            bc0[:], OP.mult))
        E("vector", lambda e, bi=bi: e.tensor_tensor(
            qsc32[:, bi * C:(bi + 1) * C], qsc32[:, bi * C:(bi + 1) * C],
            bc1[:], OP.add))
        E("vector", lambda e, bi=bi: e.tensor_scalar(
            qsc32[:, bi * C:(bi + 1) * C], qsc32[:, bi * C:(bi + 1) * C],
            scal_p[:, 1:2], None, OP.mult))
        E("vector", lambda e, bi=bi: e.tensor_scalar(
            qsc32[:, bi * C:(bi + 1) * C], qsc32[:, bi * C:(bi + 1) * C],
            -QMAX - 1.0, QMAX, OP.max, OP.min))
        E("vector", lambda e, bi=bi: e.tensor_copy(qint[:], qsc32[:, bi * C:(bi + 1) * C]))
        E("vector", lambda e, bi=bi: e.tensor_copy(xqt[:, bi * C:(bi + 1) * C],
                                                   qint[:]), inc="tv")
        xqv_tv[t] = now("tv")
        if debug:
            E("sync", dma(dbg["xq"][t * 128:(t + 1) * 128, :],
                          xqt[:, bi * C:(bi + 1) * C]),
              waits=[("tv", now("tv"))], inc="ld", inc_by=16)
        for f in range(NC):
            E("tensor", lambda e, f=f, bi=bi: e.transpose(
                psAb[:, f * 128:(f + 1) * 128],
                xqt[:, bi * C + f * 128:bi * C + (f + 1) * 128], eye16_s[:]),
              waits=([("tv", now("tv")), ("ta", xq_ev_ta.get(t - 1, 0))]
                     if f == 0 else ()),
              inc="tt" if f == NC - 1 else None)
        xq_tt[t] = now("tt")
        E("scalar", lambda e, t=t: e.activation(
            xqfm3[:, :, t * 128:(t + 1) * 128],
            psAb[:, 0:1024].rearrange("p (a b) -> p a b", a=NC), AF.Identity),
          waits=[("tt", now("tt"))], inc="ta")
        xq_ev_ta[t] = now("ta")
    xq_ta = now("ta")

    g_tt = {}
    g_tv = {}
    pdma_ld = {}
    for cch in range(NI):
        bi = cch % 2
        E("sync", dma(wgbuf[:, bi * C:(bi + 1) * C], wgT_sp[cch]),
          waits=[("ld", wgu_spill_ld), ("tt", g_tt.get(cch - 2, 0))],
          inc="ld", inc_by=16)
        E("sync", dma(wubuf[:, bi * C:(bi + 1) * C], wuT_sp[cch]), inc="ld", inc_by=16)
        wld = now("ld")
        ps = psA if bi == 0 else psB
        E("vector", lambda e, cch=cch: e.tensor_tensor(
            wsc[:, 14:15], wgusc[:, cch:cch + 1], scal_p[:, 0:1], OP.mult),
          waits=[("ta", xq_ta)])
        E("vector", lambda e, cch=cch: e.tensor_tensor(
            wsc[:, 15:16], wgusc[:, NI + cch:NI + cch + 1], scal_p[:, 0:1],
            OP.mult), inc="tv")
        cgcu_tv = now("tv")
        tb_ta = {}
        tb_tv = {}
        for tb in range(4):
            for gu in range(2):
                wb = wgbuf if gu == 0 else wubuf
                for hh in range(NC):
                    w = []
                    if tb == 0 and gu == 0 and hh == 0:
                        w = [("ld", wld), ("ta", xq_ta), ("tv", g_tv.get(cch - 2, 0))]
                    elif gu == 0 and hh == 0:
                        w = [("ta", tb_ta[tb - 1]), ("tv", tb_tv[tb - 1])]
                    E("tensor", lambda e, ps=ps, gu=gu, hh=hh, tb=tb, bi=bi, wb=wb:
                      e.matmul(ps[:, gu * 512:(gu + 1) * 512],
                               wb[:, bi * C + hh * 128:bi * C + (hh + 1) * 128],
                               xqfm[:, hh * 2048 + tb * 512:hh * 2048 + (tb + 1) * 512],
                               start=(hh == 0), stop=(hh == NC - 1)),
                      waits=w, inc="tt" if hh == NC - 1 else None)
            E("scalar", lambda e, ps=ps: e.activation(silu_t[:], ps[:, 0:512],
                                                      AF.Silu, scale=wsc[:, 14:15]),
              waits=[("tt", now("tt")), ("tv", cgcu_tv)], inc="ta")
            tb_ta[tb] = now("ta")
            E("vector", lambda e, ps=ps, tb=tb: e.scalar_tensor_tensor(
                p_t[:, (tb % 2) * 512:(tb % 2 + 1) * 512], ps[:, 512:1024],
                wsc[:, 15:16], silu_t[:], OP.mult, OP.mult),
              waits=[("ta", now("ta")), ("ld", pdma_ld.get((cch, tb - 2), 0))],
              inc="tv")
            tb_tv[tb] = now("tv")
            E("vector", lambda e, tb=tb: e.tensor_reduce(
                wsc[:, 13:14], p_t[:, (tb % 2) * 512:(tb % 2 + 1) * 512], AX.X,
                OP.max, apply_absolute_value=True))
            E("vector", lambda e: e.tensor_tensor(pmax[:, 1:2], pmax[:, 1:2],
                                                  wsc[:, 13:14], OP.max), inc="tv")
            E("sync", dma(p_sp.ap()[cch, :, tb * 512:(tb + 1) * 512],
                          p_t[:, (tb % 2) * 512:(tb % 2 + 1) * 512]),
              waits=[("tv", now("tv"))], inc="ld", inc_by=16)
            pdma_ld[(cch, tb)] = now("ld")
        g_tt[cch] = now("tt")
        g_tv[cch] = now("tv")
    p_done_ld = now("ld")

    E("tensor", lambda e: e.transpose(psD[0:1, 0:128], pmax[:, 1:2], eye32_s[:]),
      waits=[("tv", now("tv"))], inc="tt")
    E("vector", lambda e: e.tensor_copy(ccs[:], psD[0:1, 0:128]),
      waits=[("tt", now("tt"))], inc="tv")
    E("pool", lambda e: e.dma_start(out=cc2_in[0:1, :], in_=ccs[:]),
      waits=[("tv", now("tv"))], inc="tp", inc_by=16)
    E("pool", lambda e: e.collective_compute(
        "AllReduce", OP.max, replica_groups=[list(range(8))],
        ins=[cc2_in.ap().opt()], outs=[cc2_out.ap().opt()]), inc="cc")
    E("pool", lambda e: e.dma_start(out=ccs[:], in_=cc2_out[0:1, :]),
      waits=[("cc", now("cc"))], inc="tp", inc_by=16)
    E("vector", lambda e: e.tensor_reduce(scal[0:1, 1:2], ccs[:], AX.X, OP.max),
      waits=[("tp", now("tp"))], inc="tv")
    E("scalar", lambda e: e.activation(scal[0:1, 3:4], clip_s[0:1, 1:2], AF.Sigmoid),
      waits=[("tp", now("tp"))], inc="ta")
    E("vector", lambda e: e.tensor_tensor(scal[0:1, 6:7], scal[0:1, 1:2],
                                          scal[0:1, 3:4], OP.mult),
      waits=[("ta", now("ta"))])
    E("vector", lambda e: e.tensor_single_scalar(scal[0:1, 6:7], scal[0:1, 6:7],
                                                 1.0 / QMAX, OP.mult))
    E("vector", lambda e: e.reciprocal(scal[0:1, 7:8], scal[0:1, 6:7]), inc="tv")
    E("tensor", lambda e: e.matmul(psD[:, 2:4], ones1f[:], scal[0:1, 6:8],
                                   start=True, stop=True),
      waits=[("tv", now("tv"))], inc="tt")
    E("vector", lambda e: e.tensor_copy(scal_p[:, 2:4], psD[:, 2:4]),
      waits=[("tt", now("tt"))], inc="tv")
    E("vector", lambda e: e.tensor_tensor(wsc[:, 2:10], wsc[:, 2:10],
                                          vec_fm[:, 40:48], OP.mult), inc="tv")
    E("tensor", lambda e: e.transpose(psD[0:8, 0:128], wsc[:, 2:10], eye32_s[:]),
      waits=[("tv", now("tv"))], inc="tt")
    E("vector", lambda e: e.tensor_copy(rowst[0:8, :], psD[0:8, 0:128]),
      waits=[("tt", now("tt"))], inc="tv")
    E("sync", dma(rowb[0:1, 0:1024], rowst[0:8, :]), waits=[("tv", now("tv"))],
      inc="ld", inc_by=16)
    E("sync", dma(rowtmp[:], rowb[0:1, 0:C]), waits=[("ld", now("ld"))],
      inc="ld", inc_by=16)
    E("vector", lambda e: e.tensor_scalar(rowtmp[:], rowtmp[:],
                                          scal[0:1, 6:7], None, OP.mult),
      waits=[("ld", now("ld"))], inc="tv")
    E("tensor", lambda e: e.matmul(psA[:, 0:512], ones1f[:], rowtmp[0:1, 0:512],
                                   start=True, stop=True),
      waits=[("tv", now("tv"))])
    E("tensor", lambda e: e.matmul(psA[:, 512:1024], ones1f[:],
                                   rowtmp[0:1, 512:1024], start=True, stop=True),
      inc="tt")
    E("vector", lambda e: e.tensor_copy(bc0[:], psA[:]),
      waits=[("tt", now("tt"))], inc="tv")
    coef_tv = now("tv")

    y_ld = {}
    p6_tt = {}
    p6_tv = {}
    pqt3 = pqt.rearrange("p (s c) -> p s c", s=2)
    for t in range(NTO):
        bi = t % 2
        E("sync", dma(pstg.rearrange("p (c k) -> p c k", c=NI),
                      p_sp.ap().rearrange("c p k -> p c k")[:, :, t * 128:(t + 1) * 128]),
          waits=[("ld", p_done_ld), ("tv", p6_tv.get(t - 1, 0)),
                 ("tv", coef_tv if t == 0 else 0)],
          inc="ld", inc_by=16)
        ld_t = now("ld")
        E("vector", lambda e: e.tensor_scalar(
            pstg, pstg, scal_p[:, 3:4], None, OP.mult),
          waits=[("ld", ld_t), ("tt", p6_tt.get(t - 2, 0))])
        E("vector", lambda e: e.tensor_scalar(
            pstg, pstg, -QMAX - 1.0, QMAX, OP.max, OP.min))
        E("vector", lambda e: e.tensor_copy(wint6[:, 0:FI], pstg))
        E("vector", lambda e, bi=bi: e.tensor_copy(
            pqt3[:, bi, :], wint6[:, 0:FI]), inc="tv")
        p6_tv[t] = now("tv")
        ps = psA if bi == 0 else psB
        for sl in range(2):
            for cch in range(NI):
                w = []
                if sl == 0 and cch == 0:
                    w = [("tv", p6_tv[t]), ("tv", y_ld.get(("ev", t - 2), 0))]
                E("tensor", lambda e, ps=ps, sl=sl, cch=cch, bi=bi: e.matmul(
                    ps[:, sl * 512:(sl + 1) * 512],
                    pqt3[:, bi, cch * 128:(cch + 1) * 128],
                    wdT[:, cch * C + sl * 512:cch * C + (sl + 1) * 512],
                    start=(cch == 0), stop=(cch == NI - 1)),
                  waits=w, inc="tt" if cch == NI - 1 else None)
        p6_tt[t] = now("tt")
        E("sync", dma(xbuf[:, bi * C:(bi + 1) * C], x1_sp[t]),
          waits=[("tv", y_ld.get(("ev", t - 2), 0))], inc="ld", inc_by=16)
        xld = now("ld")
        E("vector", lambda e, ps=ps: e.tensor_tensor(dsc[:], ps[:], bc0[:], OP.mult),
          waits=[("tt", p6_tt[t]), ("ld", y_ld.get(t - 2, 0))])
        E("vector", lambda e, bi=bi: e.tensor_tensor(
            ysc[:, bi * C:(bi + 1) * C], dsc[:], xbuf[:, bi * C:(bi + 1) * C],
            OP.add), waits=[("ld", xld)], inc="tv")
        y_ld[("ev", t)] = now("tv")
        E("sync", dma(dy[t * 128:(t + 1) * 128, :], ysc[:, bi * C:(bi + 1) * C]),
          waits=[("tv", now("tv"))], inc="ld", inc_by=16)
        y_ld[t] = now("ld")

    E("sync", lambda e: e.nop(), waits=[("ld", now("ld"))])

    import contextlib
    stack = contextlib.ExitStack()
    sems = {s: stack.enter_context(nc.semaphore("sem_" + s)) for s in SEMS}
    block = stack.enter_context(nc.Block())

    def mk(engine_name):
        evs = [ev for ev in prog.events if ev[0] == engine_name]

        def body(eng):
            maxw = {k: 0 for k in SEMS}
            for _, fn, waits, inc, inc_by in evs:
                for sem, val in waits:
                    if val <= maxw[sem]:
                        continue
                    maxw[sem] = val
                    if engine_name == "tensor" and sem == "tt":
                        continue
                    eng.wait_ge(sems[sem], val)
                inst = fn(eng)
                if inc is not None:
                    assert inst is not None
                    inst.then_inc(sems[inc], inc_by)
        return body

    block.sync(mk("sync"))
    block.tensor(mk("tensor"))
    block.vector(mk("vector"))
    block.scalar(mk("scalar"))
    block.gpsimd(mk("pool"))
    stack.close()
    return nc


def _rope_tables():
    j32 = np.arange(0, D, 2, dtype=np.float32)
    theta32 = (np.float32(1.0) / (np.float32(10000.0) ** (j32 / np.float32(D)))
               ) * np.float32(3.0)
    pos32 = np.arange(L, dtype=np.float32)
    ang32 = (pos32[:, None] * theta32[None, :]).astype(np.float32)
    cos_hi = np.cos(ang32.astype(np.float64)).astype(np.float32)
    sin_hi = np.sin(ang32.astype(np.float64)).astype(np.float32)
    theta64 = (1.0 / (10000.0 ** (np.arange(0, D, 2, dtype=np.float64) / D))) * 3.0
    ang64 = np.arange(L, dtype=np.float64)[:, None] * theta64[None, :]
    cos_lo = (np.cos(ang64) - cos_hi).astype(np.float32)
    sin_lo = (np.sin(ang64) - sin_hi).astype(np.float32)
    return (np.tile(cos_hi, (1, 16)), np.tile(cos_lo, (1, 16)),
            np.tile(sin_hi, (1, 16)), np.tile(sin_lo, (1, 16)))


def _perm_qk():
    perm = np.arange(3 * C)
    for blk in range(2):
        for h in range(H):
            base = blk * C + h * D
            perm[base:base + 32] = base + np.arange(0, D, 2)
            perm[base + 32:base + D] = base + np.arange(1, D, 2)
    return perm


_NC_CACHE = {}


def kernel(L0_attn_x, L0_attn_w_qkv, L0_attn_w_proj, L0_attn_b_proj,
           L0_ffn_clip_x, L0_ffn_clip_down, L0_ffn_w_gate, L0_ffn_w_up,
           L0_ffn_w_down, L0_c, L0_wmod, _debug=False):
    x = np.ascontiguousarray(np.asarray(L0_attn_x, dtype=np.float32))
    wqkv = np.ascontiguousarray(np.asarray(L0_attn_w_qkv, dtype=np.float32))
    wproj = np.ascontiguousarray(np.asarray(L0_attn_w_proj, dtype=np.float32))
    bproj = np.asarray(L0_attn_b_proj, dtype=np.float32).reshape(1, C)
    wg = np.ascontiguousarray(np.asarray(L0_ffn_w_gate, dtype=np.float32))
    wu = np.ascontiguousarray(np.asarray(L0_ffn_w_up, dtype=np.float32))
    wd = np.ascontiguousarray(np.asarray(L0_ffn_w_down, dtype=np.float32))
    cv = np.asarray(L0_c, dtype=np.float32)
    wmod = np.ascontiguousarray(np.asarray(L0_wmod, dtype=np.float32))
    clips = np.array([[float(np.asarray(L0_ffn_clip_x).reshape(-1)[0]),
                       float(np.asarray(L0_ffn_clip_down).reshape(-1)[0])]],
                     dtype=np.float32)

    cos_hi, cos_lo, sin_hi, sin_lo = _rope_tables()
    tabs_base = np.ascontiguousarray(
        np.concatenate([cos_hi, cos_lo, sin_hi, sin_lo], axis=1))
    wqkv_p = np.ascontiguousarray(wqkv[:, _perm_qk()])
    eye16 = np.eye(128, dtype=BF)
    eye32 = np.eye(128, dtype=np.float32)
    sel = np.zeros((2, 128), np.float32)
    sel[0, 0:64] = 1.0
    sel[1, 64:128] = 1.0

    key = ("nc", bool(_debug))
    if key not in _NC_CACHE:
        _NC_CACHE[key] = build_nc(debug=_debug)
    ncobj = _NC_CACHE[key]

    in_maps = []
    for core in range(8):
        b, hf = core // 2, core % 2
        rot = hf * 2048
        xr = np.ascontiguousarray(np.roll(x[b], -rot, axis=0))
        tr = np.ascontiguousarray(np.roll(tabs_base, -rot, axis=0))
        in_maps.append({
            "x": xr, "wqkv": wqkv_p, "wproj": wproj, "bproj": bproj,
            "wmod": wmod, "cvec": np.ascontiguousarray(cv[b].reshape(NC, 128)),
            "clips": clips, "wg": wg, "wu": wu, "wd": wd,
            "tabs": tr, "eye16": eye16, "eye32": eye32, "sel": sel,
        })
    res = run_bass_kernel_spmd(ncobj, in_maps, core_ids=list(range(8)))
    out = np.empty((B, L, C), dtype=np.float32)
    for core in range(8):
        b, hf = core // 2, core % 2
        out[b, hf * 2048:(hf + 1) * 2048, :] = res.results[core]["y"]
    if _debug:
        return out, res
    return out


# version 03
# speedup vs baseline: 1.0526x; 1.0086x over previous
import numpy as np
import ml_dtypes
import concourse.bass as bass
import concourse.mybir as mybir
from concourse.bass_utils import run_bass_kernel_spmd

F32 = mybir.dt.float32
BF16 = mybir.dt.bfloat16
I32 = mybir.dt.int32
AF = mybir.ActivationFunctionType
OP = mybir.AluOpType
AX = mybir.AxisListType
BF = ml_dtypes.bfloat16

B, L, C = 4, 4096, 1024
H, D = 16, 64
NT = L // 128
NTO = 2048 // 128
FI = 2816
NI = FI // 128
NC = C // 128
QMAX = 127.0
EPS = 1e-5

ENGINES = ("sync", "tensor", "vector", "scalar", "pool")
_OWN = {"sync": "ld", "tensor": "tt", "vector": "tv", "scalar": "ta", "pool": "tp"}
SEMS = ("ld", "tt", "tv", "ta", "tp", "cc", "rd")


class Prog:
    def __init__(self):
        self.events = []
        self.cnt = {k: 0 for k in SEMS}
        self._gates = {}

    def now(self, sem):
        return self.cnt[sem]

    def _gate(self, owner, sem, v):
        key = (sem, v)
        if key in self._gates:
            return self._gates[key]
        full = self.cnt[sem]
        self.events.append((owner, lambda e: e.nop(), [(sem, full)], "rd", 1))
        self.cnt["rd"] += 1
        self._gates[key] = self.cnt["rd"]
        return self.cnt["rd"]

    def emit(self, eng, fn, waits=(), inc=None, inc_by=1):
        w = [(s, v) for (s, v) in waits if v > 0]
        w2 = []
        for s, v in w:
            if s == "ld" and eng != "sync":
                w2.append(("rd", self._gate("sync", "ld", v)))
            elif s == "tp" and eng != "pool":
                w2.append(("rd", self._gate("pool", "tp", v)))
            elif s == "ld" and eng == "sync":
                w2.append(("ld", self.cnt["ld"]))
            elif s == "tp" and eng == "pool":
                w2.append(("tp", self.cnt["tp"]))
            else:
                w2.append((s, v))
        w = w2
        if eng in ("vector", "scalar", "pool"):
            own = _OWN[eng]
            if self.cnt[own] > 0 and own != "tp":
                w.append((own, self.cnt[own]))
            elif own == "tp" and self.cnt[own] > 0:
                w.append((own, self.cnt[own]))
            if inc is None:
                inc = own
                inc_by = 1
        self.events.append((eng, fn, w, inc, inc_by))
        if inc is not None:
            self.cnt[inc] += inc_by


def build_nc(debug=False):
    nc = bass.Bass("TRN2", target_bir_lowering=False)

    dx = nc.dram_tensor("x", [L, C], F32, kind="ExternalInput").ap()
    dwqkv = nc.dram_tensor("wqkv", [C, 3 * C], F32, kind="ExternalInput").ap()
    dwproj = nc.dram_tensor("wproj", [C, C], F32, kind="ExternalInput").ap()
    dbproj = nc.dram_tensor("bproj", [1, C], F32, kind="ExternalInput").ap()
    dwmod = nc.dram_tensor("wmod", [C, 6 * C], F32, kind="ExternalInput").ap()
    dcvec = nc.dram_tensor("cvec", [NC, 128], F32, kind="ExternalInput").ap()
    dclip = nc.dram_tensor("clips", [1, 2], F32, kind="ExternalInput").ap()
    dwg = nc.dram_tensor("wg", [FI, C], F32, kind="ExternalInput").ap()
    dwu = nc.dram_tensor("wu", [FI, C], F32, kind="ExternalInput").ap()
    dwd = nc.dram_tensor("wd", [C, FI], F32, kind="ExternalInput").ap()
    dtab = nc.dram_tensor("tabs", [L, 2048], F32, kind="ExternalInput").ap()
    deye16 = nc.dram_tensor("eye16", [128, 128], BF16, kind="ExternalInput").ap()
    deye32 = nc.dram_tensor("eye32", [128, 128], F32, kind="ExternalInput").ap()
    dsel = nc.dram_tensor("sel", [2, 128], F32, kind="ExternalInput").ap()
    dy = nc.dram_tensor("y", [2048, C], F32, kind="ExternalOutput").ap()

    dbg = {}
    if debug:
        for nm, shp, dt_ in [("h1", [L, C], BF16), ("mod", [128, 48], F32),
                             ("kpost", [L, C], BF16), ("qpost", [2048, C], BF16),
                             ("attnout", [128, NC * 2048], BF16),
                             ("x1", [2048, C], F32), ("xq", [2048, C], BF16),
                             ("scal", [1, 16], F32)]:
            dbg[nm] = nc.dram_tensor("dbg_" + nm, shp, dt_, kind="ExternalOutput").ap()

    q_sp = nc.dram_tensor("q_sp", [NTO, 128, C], BF16, kind="Internal")
    k_sp = nc.dram_tensor("k_sp", [NT, 128, C], BF16, kind="Internal")
    v_sp = nc.dram_tensor("v_sp", [NT, 128, C], BF16, kind="Internal")
    p_sp = nc.dram_tensor("p_sp", [NI, 128, 2048], F32, kind="Internal")
    x1_sp = nc.dram_tensor("x1_sp", [NTO, 128, C], F32, kind="Internal")
    dl_sp = nc.dram_tensor("dl_sp", [NTO, 128, C], BF16, kind="Internal")
    wgT_sp = nc.dram_tensor("wgT_sp", [NI, 128, C], BF16, kind="Internal")
    wuT_sp = nc.dram_tensor("wuT_sp", [NI, 128, C], BF16, kind="Internal")
    wpj_sp = nc.dram_tensor("wpj_sp", [NC, 128, C], BF16, kind="Internal")
    rowb = nc.dram_tensor("rowb", [1, 4096], F32, kind="Internal")
    rowb_bf = nc.dram_tensor("rowb_bf", [1, 4096], BF16, kind="Internal")
    cc1_in = nc.dram_tensor("cc1_in", [1, 128], F32, kind="Internal")
    cc1_out = nc.dram_tensor("cc1_out", [1, 128], F32, kind="Internal")
    cc2_in = nc.dram_tensor("cc2_in", [1, 128], F32, kind="Internal")
    cc2_out = nc.dram_tensor("cc2_out", [1, 128], F32, kind="Internal")

    A = nc.alloc_sbuf_tensor
    AR1 = A("AR1", [128, 24576], BF16)
    AR2 = A("AR2", [128, 6144], F32)
    AR3 = A("AR3", [128, 16384], BF16)
    PSC = A("PSC", [128, 6656], F32)

    wqkv_s = AR1[:]
    kbuf = AR1[:, 0:8192]
    vbuf = AR1[:, 8192:16384]
    qbuf = AR1[:, 16384:20480]
    ptA = AR1[:, 20480:22528]
    ptB = AR1[:, 22528:24576]
    pqt = AR1[:, 0:5632]
    wgbuf = AR1[:, 11264:13312]
    wubuf = AR1[:, 13312:15360]
    wpjbuf = AR1[:, 15360:17408]
    wst = AR2[:, 0:6144]
    tbuf = AR2[:, 0:4096]
    ceff = AR2[:, 4096:6144]
    kfm = AR2.bitcast(BF16)[:, 0:4096]
    qfm = AR2.bitcast(BF16)[:, 4096:6144]
    drow = AR2[:, 3072:4096]
    rbt = AR2[:, 4096:4608]
    pstg = AR2[:, 0:2816]
    wmb = AR3[:, 0:6144]
    attnout = AR3[:]
    xqfm = AR3[:]
    qkvbias_row = AR3[0:1, 8192:11264]
    rsc32 = PSC[:, 0:1024]
    rrow = PSC[0:1, 0:1024]
    h1t = PSC.bitcast(BF16)[:, 2048:4096]
    h1fm = PSC.bitcast(BF16)[:, 4096:6144]
    qsc = PSC.bitcast(BF16)[:, 6144:8192]
    ksc = PSC.bitcast(BF16)[:, 8192:10240]
    vsc = PSC.bitcast(BF16)[:, 10240:12288]
    wqi = PSC.bitcast(BF16)[:, 0:2816]
    wint = PSC.bitcast(I32)[:, 1536:4352]
    wint6 = PSC.bitcast(I32)[:, 2048:4864]
    qsc32 = PSC[:, 0:2048]
    h2sc = PSC[:, 2048:3072]
    qint = PSC.bitcast(I32)[:, 3072:4096]
    xqt = PSC.bitcast(BF16)[:, 8192:10240]
    silu_t = PSC[:, 5120:5632]
    p_t = PSC[:, 5632:6656]
    ysc = PSC[:, 0:2048]
    x1stg = PSC[:, 2048:4096]
    cv_s = A("cv_s", [NC, 128], F32)
    clip_s = A("clip_s", [1, 2], F32)
    sel_s = A("sel_s", [2, 128], F32)
    rrow2 = A("rrow2", [2, 512], F32)
    eye16_s = A("eye16_s", [128, 128], BF16)
    eye32_s = A("eye32_s", [128, 128], F32)
    ones1b = A("ones1b", [1, 128], BF16)
    ones1f = A("ones1f", [1, 128], F32)
    onesd = A("onesd", [128, 1], BF16)
    onesr = A("onesr", [128, 64], F32)
    silucf = A("silucf", [128, NC], BF16)
    silucf32 = A("silucf32", [128, NC], F32)
    mod_fm = A("mod_fm", [128, 48], F32)
    vec_fm = A("vec_fm", [128, 64], F32)
    sa_bf = A("sa_bf", [128, NC], BF16)
    bias_fm = A("bias_fm", [128, 24], F32)
    bias_fmb = A("bias_fmb", [128, 24], BF16)
    rowst = A("rowst", [32, 128], F32)
    rowtmp = A("rowtmp", [1, C], F32)
    bc0 = A("bc0", [128, C], F32)
    bc1 = A("bc1", [128, C], F32)
    wdT = A("wdT", [128, NI * C], BF16)
    xbuf = A("xbuf", [128, 2 * C], F32)
    stat_sum = A("stat_sum", [128, NT], F32)
    stat_sq = A("stat_sq", [128, NT], F32)
    stat_m = A("stat_m", [128, NT], F32)
    stat_r = A("stat_r", [128, NT], F32)
    dsc = A("dsc", [128, C], F32)
    wsc = A("wsc", [128, 16], F32)
    wgusc = A("wgusc", [128, 44], F32)
    scal = A("scal", [1, 16], F32)
    scal_p = A("scal_p", [128, 4], F32)
    pmax = A("pmax", [128, 2], F32)
    ccs = A("ccs", [1, 128], F32)

    psA = nc.alloc_psum_tensor("psA", [128, 1024], F32)
    psB = nc.alloc_psum_tensor("psB", [128, 1024], F32)
    psC = nc.alloc_psum_tensor("psC", [128, 1024], F32)
    psD = nc.alloc_psum_tensor("psD", [128, 512], F32)
    psF = nc.alloc_psum_tensor("psF", [128, 512], F32)
    psAb = psA[:].bitcast(BF16)

    prog = Prog()
    E = prog.emit
    now = prog.now

    def dma(out, in_):
        return lambda e: e.dma_start(out=out, in_=in_)

    E("sync", dma(cv_s[:], dcvec), inc="ld", inc_by=16)
    E("sync", dma(clip_s[:], dclip), inc="ld", inc_by=16)
    E("sync", dma(sel_s[:], dsel), inc="ld", inc_by=16)
    E("sync", dma(eye16_s[:], deye16), inc="ld", inc_by=16)
    E("sync", dma(eye32_s[:], deye32), inc="ld", inc_by=16)
    ld0 = now("ld")
    E("vector", lambda e: e.memset(ones1b[:], 1.0))
    E("vector", lambda e: e.memset(ones1f[:], 1.0))
    E("vector", lambda e: e.memset(onesd[:], 1.0))
    E("vector", lambda e: e.memset(onesr[:], 1.0))
    E("vector", lambda e: e.memset(stat_sum[:], 0.0))
    E("vector", lambda e: e.memset(stat_sq[:], 0.0))
    E("vector", lambda e: e.memset(pmax[:], 0.0))
    E("vector", lambda e: e.memset(scal[:], 0.0))
    E("vector", lambda e: e.memset(scal_p[:], 0.0), inc="tv")

    E("scalar", lambda e: e.activation(cv_s[:], cv_s[:], AF.Silu),
      waits=[("ld", ld0)], inc="ta")
    E("tensor", lambda e: e.transpose(psD[0:128, 0:NC], cv_s[:], eye32_s[0:NC, 0:NC]),
      waits=[("ta", now("ta"))], inc="tt")
    E("vector", lambda e: e.tensor_copy(silucf32[:], psD[0:128, 0:NC]),
      waits=[("tt", now("tt"))])
    E("vector", lambda e: e.tensor_copy(silucf[:], psD[0:128, 0:NC]),
      inc="tv")

    for f in range(NC):
        E("sync", dma(wst, dwmod[f * 128:(f + 1) * 128, :]),
          waits=[("tv", now("tv"))], inc="ld", inc_by=16)
        ld_f = now("ld")
        for j in range(48):
            E("tensor",
              lambda e, f=f, j=j: e.matmul(
                  psC[:, j:j + 1], wst[:, j * 128:(j + 1) * 128],
                  silucf32[:, f:f + 1], start=True, stop=True),
              waits=[("tv", now("tv")), ("ld", ld_f)] if j == 0 else (),
              inc="tt" if j == 47 else None)
        if f == 0:
            E("vector", lambda e: e.tensor_copy(mod_fm[:], psC[:, 0:48]),
              waits=[("tt", now("tt"))], inc="tv")
        else:
            E("vector", lambda e: e.tensor_tensor(mod_fm[:], mod_fm[:],
                                                  psC[:, 0:48], OP.add),
              waits=[("tt", now("tt"))], inc="tv")
    if debug:
        E("sync", dma(dbg["mod"], mod_fm[:]), waits=[("tv", now("tv"))],
          inc="ld", inc_by=16)

    E("vector", lambda e: e.tensor_single_scalar(vec_fm[:, 0:8], mod_fm[:, 8:16], 1.0, OP.add))
    E("vector", lambda e: e.reciprocal(vec_fm[:, 8:16], vec_fm[:, 0:8]))
    E("vector", lambda e: e.tensor_tensor(vec_fm[:, 8:16], vec_fm[:, 8:16], mod_fm[:, 0:8], OP.mult))
    E("vector", lambda e: e.tensor_copy(sa_bf[:], vec_fm[:, 8:16]))
    for dst, src in [(16, 16), (40, 40)]:
        E("vector", lambda e, dst=dst, src=src: e.tensor_single_scalar(
            vec_fm[:, dst:dst + 8], mod_fm[:, src:src + 8], -1.0, OP.mult))
        E("vector", lambda e, dst=dst, src=src: e.tensor_tensor(
            vec_fm[:, dst:dst + 8], vec_fm[:, dst:dst + 8], mod_fm[:, src:src + 8], OP.max))
        E("vector", lambda e, dst=dst: e.tensor_single_scalar(
            vec_fm[:, dst:dst + 8], vec_fm[:, dst:dst + 8], 0.25, OP.max))
        E("vector", lambda e, src=src: e.tensor_scalar(
            vec_fm[:, 56:64], mod_fm[:, src:src + 8], 0.0, 2.0, OP.is_ge, OP.mult))
        E("vector", lambda e: e.tensor_single_scalar(
            vec_fm[:, 56:64], vec_fm[:, 56:64], -1.0, OP.add))
        E("vector", lambda e, dst=dst: e.tensor_tensor(
            vec_fm[:, dst:dst + 8], vec_fm[:, dst:dst + 8], vec_fm[:, 56:64], OP.mult))
    E("vector", lambda e: e.tensor_single_scalar(vec_fm[:, 24:32], mod_fm[:, 32:40], 1.0, OP.add))
    E("vector", lambda e: e.tensor_copy(vec_fm[:, 32:40], mod_fm[:, 24:32]), inc="tv")
    vec_tv = now("tv")

    E("tensor", lambda e: e.transpose(psD[0:32, 0:128], vec_fm[:, 16:48], eye32_s[:]),
      waits=[("tv", vec_tv)], inc="tt")
    E("vector", lambda e: e.tensor_copy(rowst[:], psD[0:32, 0:128]),
      waits=[("tt", now("tt"))], inc="tv")
    E("sync", dma(rowb[0:1, 0:4096], rowst[:]), waits=[("tv", now("tv"))],
      inc="ld", inc_by=16)
    rows_ld = now("ld")

    for f in range(NC):
        E("sync", dma(wst[:, 0:3 * C], dwqkv[f * 128:(f + 1) * 128, :]),
          waits=[("tv", now("tv"))], inc="ld", inc_by=16)
        E("vector",
          lambda e, f=f: e.tensor_scalar(wqkv_s[:, f * 3 * C:(f + 1) * 3 * C],
                                         wst[:, 0:3 * C], vec_fm[:, f:f + 1],
                                         None, OP.mult),
          waits=[("ld", now("ld"))], inc="tv")
    wqkv_tv = now("tv")

    for f in range(NC):
        for j in range(24):
            E("tensor",
              lambda e, j=j, f=f: e.matmul(
                  psD[:, 64 + j:65 + j],
                  wqkv_s[:, f * 3 * C + j * 128:f * 3 * C + (j + 1) * 128],
                  sa_bf[:, f:f + 1], start=True, stop=True),
              waits=[("tv", now("tv"))] if j == 0 else (),
              inc="tt" if j == 23 else None)
        if f == 0:
            E("vector", lambda e: e.tensor_copy(bias_fm[:], psD[:, 64:88]),
              waits=[("tt", now("tt"))], inc="tv")
        else:
            E("vector", lambda e: e.tensor_tensor(bias_fm[:], bias_fm[:],
                                                  psD[:, 64:88], OP.add),
              waits=[("tt", now("tt"))], inc="tv")
    E("vector", lambda e: e.tensor_copy(bias_fmb[:], bias_fm[:]), inc="tv")
    E("tensor", lambda e: e.transpose(psAb[0:24, 0:128], bias_fmb[:], eye16_s[:]),
      waits=[("tv", now("tv"))], inc="tt")
    E("vector", lambda e: e.tensor_copy(rowst.bitcast(BF16)[0:24, 0:128],
                                        psAb[0:24, 0:128]),
      waits=[("tt", now("tt"))], inc="tv")
    E("sync", dma(rowb_bf[0:1, 0:3072], rowst.bitcast(BF16)[0:24, 0:128]),
      waits=[("tv", now("tv"))], inc="ld", inc_by=16)
    E("sync", dma(qkvbias_row[:], rowb_bf[0:1, 0:3 * C]), waits=[("ld", now("ld"))],
      inc="ld", inc_by=16)

    for f in range(NC):
        E("sync", dma(wst[:, 0:C], dwproj[f * 128:(f + 1) * 128, :]),
          waits=[("tv", now("tv"))], inc="ld", inc_by=16)
        E("vector", lambda e: e.tensor_copy(wqi[:, 0:C], wst[:, 0:C]),
          waits=[("ld", now("ld"))], inc="tv")
        E("sync", dma(wpj_sp[f], wqi[:, 0:C]), waits=[("tv", now("tv"))],
          inc="ld", inc_by=16)
    wpj_spill_ld = now("ld")

    def quant_chunk(width, keep_scale_col=None, keep_tensor=None):
        E("vector", lambda e, width=width: e.tensor_reduce(
            wsc[:, 0:1], wst[:, 0:width], AX.X, OP.max, apply_absolute_value=True),
          waits=[("ld", now("ld"))])
        E("vector", lambda e: e.reciprocal(wsc[:, 1:2], wsc[:, 0:1]))
        E("vector", lambda e: e.tensor_single_scalar(wsc[:, 1:2], wsc[:, 1:2], QMAX, OP.mult))
        if keep_scale_col is not None:
            kt = wsc if keep_tensor is None else keep_tensor
            E("vector", lambda e, k=keep_scale_col, kt=kt: e.tensor_single_scalar(
                kt[:, k:k + 1], wsc[:, 0:1], 1.0 / QMAX, OP.mult))
        E("vector", lambda e, width=width: e.tensor_scalar(
            wint[:, 0:width], wst[:, 0:width], wsc[:, 1:2], None, OP.mult))
        E("vector", lambda e, width=width: e.tensor_copy(wqi[:, 0:width],
                                                         wint[:, 0:width]), inc="tv")

    wgu_spill_ld = 0
    for wi, (src, spT) in enumerate([(dwg, wgT_sp), (dwu, wuT_sp)]):
        for cch in range(NI):
            E("sync", dma(wst[:, 0:C], src[cch * 128:(cch + 1) * 128, :]),
              waits=[("tv", now("tv"))], inc="ld", inc_by=16)
            quant_chunk(C, keep_scale_col=wi * NI + cch, keep_tensor=wgusc)
            for t in range(NC):
                E("tensor",
                  lambda e, t=t: e.transpose(psAb[:, t * 128:(t + 1) * 128],
                                             wqi[:, t * 128:(t + 1) * 128],
                                             eye16_s[:]),
                  waits=[("tv", now("tv")), ("ld", wgu_spill_ld)] if t == 0 else (),
                  inc="tt" if t == NC - 1 else None)
            E("vector", lambda e: e.tensor_copy(ksc[:, 0:C], psAb[:, 0:1024]),
              waits=[("tt", now("tt"))], inc="tv")
            E("sync", dma(spT[cch], ksc[:, 0:C]), waits=[("tv", now("tv"))],
              inc="ld", inc_by=16)
            wgu_spill_ld = now("ld")
    wdT3 = wdT[:].rearrange("p (cc hc) -> p cc hc", cc=NI)
    for hh in range(NC):
        E("sync", dma(wst[:, 0:FI], dwd[hh * 128:(hh + 1) * 128, :]),
          waits=[("tv", now("tv"))], inc="ld", inc_by=16)
        quant_chunk(FI, keep_scale_col=2 + hh)
        for grp in range(3):
            lo = grp * 8
            n = min(8, NI - lo)
            for u in range(n):
                E("tensor",
                  lambda e, lo=lo, u=u: e.transpose(
                      psAb[:, u * 128:(u + 1) * 128],
                      wqi[:, (lo + u) * 128:(lo + u + 1) * 128], eye16_s[:]),
                  waits=[("tv", now("tv"))] if (grp == 0 and u == 0) else
                        ([("tv", now("tv"))] if u == 0 else ()),
                  inc="tt" if u == n - 1 else None)
            E("vector",
              lambda e, lo=lo, n=n, hh=hh: e.tensor_copy(
                  wdT3[:, lo:lo + n, hh * 128:(hh + 1) * 128],
                  psAb[:, 0:n * 128].rearrange("p (a b) -> p a b", a=n)),
              waits=[("tt", now("tt"))], inc="tv")
    setup_tv = now("tv")

    r_inv = 1.0 / C
    qkv_ps = {0: psA, 1: psA, 2: psB, 3: psB, 4: psC, 5: psC}
    ev_tv = {}
    sp_ld = {}
    xload_ld = {}
    apply_tv = {}
    tadd_tv = {}
    h1T_tt = {}
    h1ev_ta = {}
    qkv_tt = {}
    last_spill_ld = 0

    for t in range(NT):
        our = t < NTO
        bi = t % 2
        E("sync", dma(xbuf[:, bi * C:(bi + 1) * C], dx[t * 128:(t + 1) * 128, :]),
          waits=[("tv", apply_tv.get(t - 2, 0)), ("ta", h1ev_ta.get(t - 2, 0))],
          inc="ld", inc_by=16)
        E("sync", dma(tbuf[:, bi * 2048:(bi + 1) * 2048],
                      dtab[t * 128:(t + 1) * 128, :]),
          waits=[("tv", max(tadd_tv.get(t - 2, 0), setup_tv if t < 2 else 0))],
          inc="ld", inc_by=16)
        xload_ld[t] = now("ld")
        E("scalar", lambda e, bi=bi, t=t: e.activation(
            dsc[:], xbuf[:, bi * C:(bi + 1) * C], AF.Identity,
            accum_out=stat_sum[:, t:t + 1]),
          waits=[("ld", xload_ld[t]), ("tv", setup_tv if t == 0 else 0)])
        E("scalar", lambda e, bi=bi, t=t: e.activation(
            dsc[:], xbuf[:, bi * C:(bi + 1) * C], AF.Square,
            accum_out=stat_sq[:, t:t + 1]), inc="ta")
        E("vector", lambda e, t=t: e.tensor_single_scalar(
            stat_m[:, t:t + 1], stat_sum[:, t:t + 1], r_inv, OP.mult),
          waits=[("ta", now("ta"))])
        E("vector", lambda e, t=t: e.tensor_single_scalar(
            stat_sq[:, t:t + 1], stat_sq[:, t:t + 1], r_inv, OP.mult))
        E("vector", lambda e, t=t: e.tensor_tensor(
            stat_sum[:, t:t + 1], stat_m[:, t:t + 1], stat_m[:, t:t + 1], OP.mult))
        E("vector", lambda e, t=t: e.scalar_tensor_tensor(
            stat_sq[:, t:t + 1], stat_sq[:, t:t + 1], EPS, stat_sum[:, t:t + 1],
            OP.add, OP.subtract))
        E("vector", lambda e, t=t: e.reciprocal(stat_sum[:, t:t + 1],
                                                stat_sq[:, t:t + 1]), inc="tv")
        E("scalar", lambda e, t=t: e.activation(stat_r[:, t:t + 1],
                                                stat_sum[:, t:t + 1], AF.Sqrt),
          waits=[("tv", now("tv"))], inc="ta")
        E("vector", lambda e, bi=bi, t=t: e.tensor_scalar(
            h1t[:, bi * C:(bi + 1) * C], xbuf[:, bi * C:(bi + 1) * C],
            stat_m[:, t:t + 1], stat_r[:, t:t + 1], OP.subtract, OP.mult),
          waits=[("ta", now("ta")), ("tt", h1T_tt.get(t - 2, 0))], inc="tv")
        apply_tv[t] = now("tv")
        if debug:
            E("sync", dma(dbg["h1"][t * 128:(t + 1) * 128, :],
                          h1t[:, bi * C:(bi + 1) * C]),
              waits=[("tv", now("tv"))], inc="ld", inc_by=16)
        for f in range(NC):
            E("tensor", lambda e, f=f, bi=bi: e.transpose(
                psAb[:, f * 128:(f + 1) * 128],
                h1t[:, bi * C + f * 128:bi * C + (f + 1) * 128], eye16_s[:]),
              waits=([("tv", now("tv")), ("ta", h1ev_ta.get(t - 1, 0))]
                     if f == 0 else ()),
              inc="tt" if f == NC - 1 else None)
        h1T_tt[t] = now("tt")
        E("scalar", lambda e, bi=bi: e.activation(h1fm[:, bi * C:(bi + 1) * C],
                                                  psAb[:, 0:1024], AF.Identity),
          waits=[("tt", now("tt")), ("tt", qkv_tt.get((t - 2, 2), 0))], inc="ta")
        h1ev_ta[t] = now("ta")

        slices = list(range(6)) if our else [2, 3, 4, 5]
        for sl in slices:
            ps = qkv_ps[sl]
            col = (sl % 2) * 512
            grp = sl // 2
            w = [("ta", h1ev_ta[t])]
            if (t - 1, grp) in ev_tv:
                w.append(("tv", ev_tv[(t - 1, grp)]))
            elif grp == 0 and (NTO - 1, 0) in ev_tv:
                w.append(("tv", ev_tv[(NTO - 1, 0)]))
            E("tensor", lambda e, ps=ps, col=col, sl=sl: e.matmul(
                ps[:, col:col + 512], ones1b[:],
                qkvbias_row[0:1, sl * 512:(sl + 1) * 512], start=True, stop=False),
              waits=w)
            for f in range(NC):
                E("tensor", lambda e, ps=ps, col=col, sl=sl, f=f, bi=bi: e.matmul(
                    ps[:, col:col + 512],
                    h1fm[:, bi * C + f * 128:bi * C + (f + 1) * 128],
                    wqkv_s[:, f * 3 * C + sl * 512:f * 3 * C + (sl + 1) * 512],
                    start=False, stop=(f == NC - 1)),
                  inc="tt" if (f == NC - 1 and sl % 2 == 1) else None)
            if sl % 2 == 1:
                qkv_tt[(t, grp)] = now("tt")

        E("vector", lambda e, bi=bi: e.tensor_tensor(
            ceff[:, bi * 1024:bi * 1024 + 512],
            tbuf[:, bi * 2048:bi * 2048 + 512],
            tbuf[:, bi * 2048 + 512:bi * 2048 + 1024], OP.add))
        E("vector", lambda e, bi=bi: e.tensor_tensor(
            ceff[:, bi * 1024 + 512:bi * 1024 + 1024],
            tbuf[:, bi * 2048 + 1024:bi * 2048 + 1536],
            tbuf[:, bi * 2048 + 1536:bi * 2048 + 2048], OP.add), inc="tv")
        tadd_tv[t] = now("tv")

        def rope(psrc, dst, grp):
            co = bi * 1024
            do = bi * C
            re_v = psrc[:].rearrange("p (h s i) -> p h s i", h=16, s=2)[:, :, 0, :]
            im_v = psrc[:].rearrange("p (h s i) -> p h s i", h=16, s=2)[:, :, 1, :]
            cosv = ceff[:, co:co + 512].rearrange("p (h i) -> p h i", h=16)
            sinv = ceff[:, co + 512:co + 1024].rearrange("p (h i) -> p h i", h=16)
            o_re = dst[:, do:do + C].rearrange("p (h s i) -> p h s i", h=16, s=2)[:, :, 0, :]
            o_im = dst[:, do:do + C].rearrange("p (h s i) -> p h s i", h=16, s=2)[:, :, 1, :]
            sc_a = rsc32[:, 0:512].rearrange("p (h i) -> p h i", h=16)
            sc_b = rsc32[:, 512:1024].rearrange("p (h i) -> p h i", h=16)
            w0 = [("tt", qkv_tt[(t, grp)]), ("tv", now("tv")),
                  ("ld", sp_ld.get((t - 2, grp), 0))]
            E("vector", lambda e: e.tensor_tensor(sc_a, re_v, cosv, OP.mult), waits=w0)
            E("vector", lambda e: e.tensor_tensor(sc_b, im_v, sinv, OP.mult))
            E("vector", lambda e: e.tensor_tensor(o_re, sc_a, sc_b, OP.subtract))
            E("vector", lambda e: e.tensor_tensor(sc_a, re_v, sinv, OP.mult))
            E("vector", lambda e: e.tensor_tensor(sc_b, im_v, cosv, OP.mult))
            E("vector", lambda e: e.tensor_tensor(o_im, sc_a, sc_b, OP.add), inc="tv")

        if our:
            rope(psA, qsc, 0)
            ev_tv[(t, 0)] = now("tv")
            E("sync", dma(q_sp[t], qsc[:, bi * C:(bi + 1) * C]),
              waits=[("tv", now("tv"))], inc="ld", inc_by=16)
            sp_ld[(t, 0)] = now("ld")
            if debug:
                E("sync", dma(dbg["qpost"][t * 128:(t + 1) * 128, :],
                              qsc[:, bi * C:(bi + 1) * C]), inc="ld", inc_by=16)
        rope(psB, ksc, 1)
        ev_tv[(t, 1)] = now("tv")
        E("sync", dma(k_sp[t], ksc[:, bi * C:(bi + 1) * C]),
          waits=[("tv", now("tv"))], inc="ld", inc_by=16)
        sp_ld[(t, 1)] = now("ld")
        if debug:
            E("sync", dma(dbg["kpost"][t * 128:(t + 1) * 128, :],
                          ksc[:, bi * C:(bi + 1) * C]), inc="ld", inc_by=16)
        E("vector", lambda e, bi=bi: e.tensor_copy(vsc[:, bi * C:(bi + 1) * C], psC[:]),
          waits=[("tt", qkv_tt[(t, 2)]), ("ld", sp_ld.get((t - 2, 2), 0))], inc="tv")
        ev_tv[(t, 2)] = now("tv")
        E("sync", dma(v_sp[t], vsc[:, bi * C:(bi + 1) * C]),
          waits=[("tv", now("tv"))], inc="ld", inc_by=16)
        sp_ld[(t, 2)] = now("ld")
        last_spill_ld = now("ld")

    abuf_ld = {}
    hp_last_tv = {}
    tpose_ta = {}
    attn_tv = 0
    gate_drow_ld = 0
    gate_rrow_tt = 0
    gate_rbt_tv = 0
    rb_ta = 0
    for hp in range(8):
        bi = hp % 2
        wv = [("tv", hp_last_tv.get(hp - 2, 0)), ("ld", last_spill_ld)]
        E("sync", dma(
            kbuf[:, bi * 4096:(bi + 1) * 4096].rearrange("p (t c) -> p t c", t=NT),
            k_sp.ap().rearrange("t p c -> p t c")[:, :, hp * 128:(hp + 1) * 128]),
          waits=wv, inc="ld", inc_by=16)
        E("sync", dma(
            vbuf[:, bi * 4096:(bi + 1) * 4096].rearrange("p (t c) -> p t c", t=NT),
            v_sp.ap().rearrange("t p c -> p t c")[:, :, hp * 128:(hp + 1) * 128]),
          inc="ld", inc_by=16)
        E("sync", dma(
            qbuf[:, bi * 2048:(bi + 1) * 2048].rearrange("p (t c) -> p t c", t=NTO),
            q_sp.ap().rearrange("t p c -> p t c")[:, :, hp * 128:(hp + 1) * 128]),
          inc="ld", inc_by=16)
        abuf_ld[hp] = now("ld")

        for grp in range(6):
            if grp < 2:
                srcb, soff, dstt, g = qbuf, bi * 2048, qfm, grp
            else:
                srcb, soff, dstt, g = kbuf, bi * 4096, kfm, grp - 2
            for u in range(8):
                idx = g * 8 + u
                E("tensor", lambda e, srcb=srcb, soff=soff, idx=idx, u=u:
                  e.transpose(psAb[:, u * 128:(u + 1) * 128],
                              srcb[:, soff + idx * 128:soff + (idx + 1) * 128],
                              eye16_s[:]),
                  waits=([("ld", abuf_ld[hp]), ("ta", now("ta")),
                          ("tv", hp_last_tv.get(hp - 1, 0))] if u == 0 else ()),
                  inc="tt" if u == 7 else None)
            E("scalar", lambda e, dstt=dstt, g=g: e.activation(
                dstt[:, g * 1024:(g + 1) * 1024], psAb[:, 0:1024], AF.Identity),
              waits=[("tt", now("tt"))], inc="ta")
        tpose_ta[hp] = now("ta")

        for qb in range(4):
            st_tt = {}
            exp_ta = {}
            pv_tt = {}
            ev_done_tv = attn_tv
            for j in range(16):
                for g2 in range(2):
                    kc = j * 2 + g2
                    for hi, (ps, rl) in enumerate([(psA, 0), (psB, 64)]):
                        w = []
                        if g2 == 0 and hi == 0:
                            w = [("ta", tpose_ta[hp]),
                                 ("ta", exp_ta.get((0, j - 1), 0)),
                                 ("ta", exp_ta.get((1, j - 1), 0))]
                        E("tensor", lambda e, ps=ps, rl=rl, kc=kc, qb=qb, g2=g2:
                          e.matmul(ps[:, g2 * 512:(g2 + 1) * 512],
                                   kfm[rl:rl + 64, kc * 128:(kc + 1) * 128],
                                   qfm[rl:rl + 64, qb * 512:(qb + 1) * 512],
                                   start=True, stop=True, tile_position=(rl, 0)),
                          waits=w,
                          inc="tt" if g2 == 1 else None)
                        st_tt[(hi, kc)] = now("tt")
                for hi, (ps, pt) in enumerate([(psA, ptA), (psB, ptB)]):
                    w = [("tt", st_tt[(hi, j * 2 + 1)])]
                    if (hi, j - 2) in pv_tt:
                        w.append(("tt", pv_tt[(hi, j - 2)]))
                    E("scalar", lambda e, ps=ps, pt=pt, j=j: e.activation(
                        pt[:, (j % 2) * 1024:(j % 2 + 1) * 1024], ps[:],
                        AF.Exp, scale=0.125),
                      waits=w, inc="ta")
                    exp_ta[(hi, j)] = now("ta")
                for g2 in range(2):
                    kc = j * 2 + g2
                    for hi, pt in enumerate([ptA, ptB]):
                        cl = hi * 64
                        pv_dst = psC[0:64, 0:512] if hi == 0 else psC[64:128, 512:1024]
                        dn_dst = psD[0:1, 0:512] if hi == 0 else psF[64:65, 0:512]
                        ptap = pt[:, (j % 2) * 1024 + g2 * 512:
                                  (j % 2) * 1024 + (g2 + 1) * 512]
                        w = []
                        if g2 == 0:
                            w = [("ta", exp_ta[(hi, j)])]
                        if kc == 0:
                            w.append(("tv", ev_done_tv))
                            w.append(("ta", rb_ta))
                        E("tensor", lambda e, ptap=ptap, kc=kc, cl=cl, bi=bi, hi=hi,
                          pv_dst=pv_dst:
                          e.matmul(pv_dst,
                                   vbuf[:, bi * 4096 + kc * 128 + hi * 64:
                                        bi * 4096 + kc * 128 + hi * 64 + 64],
                                   ptap, start=(kc == 0), stop=(kc == NT - 1),
                                   tile_position=(0, cl)),
                          waits=w, inc=None)
                        E("tensor", lambda e, ptap=ptap, kc=kc, cl=cl, dn_dst=dn_dst:
                          e.matmul(dn_dst, onesd[:], ptap,
                                   start=(kc == 0), stop=(kc == NT - 1),
                                   tile_position=(0, cl)),
                          inc="tt" if g2 == 1 else None)
                        if g2 == 1:
                            pv_tt[(hi, j)] = now("tt")
            pv_done_tt = now("tt")
            E("scalar", lambda e: e.activation(drow[0:1, 0:512], psD[0:1, 0:512],
                                               AF.Identity),
              waits=[("tt", pv_done_tt), ("tt", gate_drow_ld)])
            E("scalar", lambda e: e.activation(drow[64:65, 512:1024],
                                               psF[64:65, 0:512], AF.Identity),
              inc="ta")
            dr_ta = now("ta")
            E("vector", lambda e: e.reciprocal(drow[0:1, 0:512], drow[0:1, 0:512]),
              waits=[("ta", dr_ta)])
            E("vector", lambda e: e.reciprocal(drow[64:65, 512:1024],
                                               drow[64:65, 512:1024]), inc="tv")
            E("tensor", lambda e: e.matmul(psD[0:64, 0:512], onesr[0:1, :],
                                           drow[0:1, 0:512],
                                           start=True, stop=True),
              waits=[("tv", now("tv"))])
            E("tensor", lambda e: e.matmul(psF[64:128, 0:512], onesr[64:65, :],
                                           drow[64:65, 512:1024],
                                           start=True, stop=True,
                                           tile_position=(64, 64)), inc="tt")
            gate_drow_ld = now("tt")
            E("scalar", lambda e: e.activation(rbt[0:64, :], psD[0:64, 0:512],
                                               AF.Identity),
              waits=[("tt", now("tt")), ("tv", gate_rbt_tv)])
            E("scalar", lambda e: e.activation(rbt[64:128, :], psF[64:128, 0:512],
                                               AF.Identity), inc="ta")
            rb_ta = now("ta")
            ao = hp * 2048 + qb * 512
            E("vector", lambda e, ao=ao: e.tensor_tensor(
                attnout[0:64, ao:ao + 512], psC[0:64, 0:512], rbt[0:64, :], OP.mult),
              waits=[("ta", now("ta"))])
            E("vector", lambda e, ao=ao: e.tensor_tensor(
                attnout[64:128, ao:ao + 512], psC[64:128, 512:1024],
                rbt[64:128, :], OP.mult), inc="tv")
            attn_tv = now("tv")
            gate_rbt_tv = now("tv")
        hp_last_tv[hp] = now("tv")
    if debug:
        for f in range(NC):
            E("sync", dma(dbg["attnout"][:, f * 2048:(f + 1) * 2048],
                          attnout[:, f * 2048:(f + 1) * 2048]),
              waits=[("tv", attn_tv)], inc="ld", inc_by=16)

    for tile, off in [(bc0, 0), (bc1, -1)]:
        rsrc = dbproj if off < 0 else rowb[0:1, off:off + C]
        E("sync", dma(rowtmp[:], rsrc),
          waits=[("ld", rows_ld), ("tt", now("tt"))], inc="ld", inc_by=16)
        E("tensor", lambda e: e.matmul(psA[:, 0:512], ones1f[:], rowtmp[0:1, 0:512],
                                       start=True, stop=True),
          waits=[("ld", now("ld")), ("ta", now("ta")),
                 ("tv", max(attn_tv, now("tv")))])
        E("tensor", lambda e: e.matmul(psA[:, 512:1024], ones1f[:],
                                       rowtmp[0:1, 512:1024], start=True, stop=True),
          inc="tt")
        E("vector", lambda e, tile=tile: e.tensor_copy(tile[:], psA[:]),
          waits=[("tt", now("tt"))], inc="tv")
    bc_tv = now("tv")

    pj_ld = {}
    pj_ev_ta = {}
    dl_sp_ld = {}
    for fo in range(NC):
        bi = fo % 2
        E("sync", dma(wpjbuf[:, bi * C:(bi + 1) * C].rearrange(
            "p (f c) -> p f c", f=NC),
            wpj_sp.ap().rearrange("f p c -> p f c")[:, :, fo * 128:(fo + 1) * 128]),
          waits=[("ld", wpj_spill_ld), ("tt", pj_ld.get(fo - 2, 0)),
                 ("tv", attn_tv if fo < 2 else 0)],
          inc="ld", inc_by=16)
        wld = now("ld")
        for sl in range(4):
            ps = (psA, psA, psB, psB)[sl]
            col = (sl % 2) * 512
            for f in range(NC):
                w = []
                if sl == 0 and f == 0:
                    w = [("ld", wld), ("tv", bc_tv), ("ta", pj_ev_ta.get(fo - 1, 0))]
                E("tensor", lambda e, ps=ps, col=col, sl=sl, f=f, fo=fo, bi=bi:
                  e.matmul(ps[:, col:col + 512],
                           wpjbuf[:, bi * C + f * 128:bi * C + (f + 1) * 128],
                           attnout[:, f * 2048 + sl * 512:f * 2048 + (sl + 1) * 512],
                           start=(f == 0), stop=(f == NC - 1)),
                  waits=w, inc="tt" if f == NC - 1 else None)
        pj_ld[fo] = now("tt")
        E("scalar", lambda e: e.activation(ksc[:, 0:C], psA[:], AF.Identity),
          waits=[("tt", now("tt"))])
        E("scalar", lambda e: e.activation(ksc[:, C:2 * C], psB[:], AF.Identity),
          inc="ta")
        ev_ta = now("ta")
        for grp in range(2):
            for u in range(8):
                tt_i = grp * 8 + u
                E("tensor", lambda e, tt_i=tt_i, u=u: e.transpose(
                    psAb[:, u * 128:(u + 1) * 128],
                    ksc[:, tt_i * 128:(tt_i + 1) * 128], eye16_s[:]),
                  waits=([("ta", ev_ta), ("ta", pj_ev_ta.get((fo, grp - 1),
                                                            pj_ev_ta.get((fo - 1, 1), 0)))]
                         if u == 0 else ()),
                  inc="tt" if u == 7 else None)
            E("scalar", lambda e, grp=grp: e.activation(qsc[:, grp * C:(grp + 1) * C],
                                                        psAb[:, 0:1024], AF.Identity),
              waits=[("tt", now("tt")), ("ld", dl_sp_ld.get((fo - 1, grp), 0))],
              inc="ta")
            pj_ev_ta[(fo, grp)] = now("ta")
            E("sync", dma(
                bass.AP(dl_sp, grp * 8 * 128 * 1024 + fo * 128,
                        [[1024, 128], [131072, 8], [1, 128]]),
                qsc[:, grp * C:(grp + 1) * C].rearrange("p (a b) -> p a b", a=8)),
              waits=[("ta", now("ta"))], inc="ld", inc_by=16)
            dl_sp_ld[(fo, grp)] = now("ld")
    dl_done_ld = now("ld")
    pj_ev_ta_last = now("ta")

    x1_tv = {}
    x1_ld = {}
    xq_tt = {}
    for t in range(NTO):
        bi = t % 2
        E("sync", dma(xbuf[:, bi * C:(bi + 1) * C], dx[t * 128:(t + 1) * 128, :]),
          waits=[("tv", x1_tv.get(t - 2, 0))], inc="ld", inc_by=16)
        E("sync", dma(ksc[:, bi * C:(bi + 1) * C], dl_sp[t]),
          waits=[("ld", dl_done_ld), ("tv", x1_tv.get(t - 2, 0)),
                 ("ta", pj_ev_ta_last if t < 2 else 0)],
          inc="ld", inc_by=16)
        ld_t = now("ld")
        E("vector", lambda e, bi=bi: e.tensor_tensor(
            dsc[:], ksc[:, bi * C:(bi + 1) * C], bc1[:], OP.add),
          waits=[("ld", ld_t)])
        E("vector", lambda e: e.tensor_tensor(dsc[:], dsc[:], bc0[:], OP.mult))
        E("vector", lambda e, t=t, bi=bi: e.tensor_tensor(
            x1stg[:, bi * C:(bi + 1) * C], dsc[:], xbuf[:, bi * C:(bi + 1) * C],
            OP.add), inc="tv")
        x1_tv[t] = now("tv")
        E("sync", dma(x1_sp[t], x1stg[:, bi * C:(bi + 1) * C]),
          waits=[("tv", now("tv"))], inc="ld", inc_by=16)
        x1_ld[t] = now("ld")
        if debug:
            E("sync", dma(dbg["x1"][t * 128:(t + 1) * 128, :],
                          x1stg[:, bi * C:(bi + 1) * C]), inc="ld", inc_by=16)
    x1_done_ld = now("ld")

    for tile, off in [(bc0, C), (bc1, 2 * C)]:
        E("sync", dma(rowtmp[:], rowb[0:1, off:off + C]),
          waits=[("tt", now("tt"))], inc="ld", inc_by=16)
        E("tensor", lambda e: e.matmul(psA[:, 0:512], ones1f[:], rowtmp[0:1, 0:512],
                                       start=True, stop=True),
          waits=[("ld", now("ld")), ("tv", now("tv")), ("ta", now("ta"))])
        E("tensor", lambda e: e.matmul(psA[:, 512:1024], ones1f[:],
                                       rowtmp[0:1, 512:1024], start=True, stop=True),
          inc="tt")
        E("vector", lambda e, tile=tile: e.tensor_copy(tile[:], psA[:]),
          waits=[("tt", now("tt"))], inc="tv")
    E("vector", lambda e: e.memset(stat_sum[:], 0.0))
    E("vector", lambda e: e.memset(stat_sq[:], 0.0), inc="tv")
    p4bc_tv = now("tv")

    st_ld = {}
    st_ta = {}
    st_tv = {}
    for t in range(NTO):
        bi = t % 2
        E("sync", dma(xbuf[:, bi * C:(bi + 1) * C], x1_sp[t]),
          waits=[("ld", x1_done_ld), ("ta", st_ta.get(t - 2, 0)),
                 ("tv", max(p4bc_tv if t < 2 else 0, st_tv.get(t - 2, 0)))],
          inc="ld", inc_by=16)
        st_ld[t] = now("ld")
        E("scalar", lambda e, bi=bi, t=t: e.activation(
            dsc[:], xbuf[:, bi * C:(bi + 1) * C], AF.Identity,
            accum_out=stat_sum[:, t:t + 1]), waits=[("ld", st_ld[t])])
        E("scalar", lambda e, bi=bi, t=t: e.activation(
            dsc[:], xbuf[:, bi * C:(bi + 1) * C], AF.Square,
            accum_out=stat_sq[:, t:t + 1]), inc="ta")
        st_ta[t] = now("ta")
        E("vector", lambda e, t=t: e.tensor_single_scalar(
            stat_m[:, t:t + 1], stat_sum[:, t:t + 1], r_inv, OP.mult),
          waits=[("ta", st_ta[t])])
        E("vector", lambda e, t=t: e.tensor_single_scalar(
            stat_sq[:, t:t + 1], stat_sq[:, t:t + 1], r_inv, OP.mult))
        E("vector", lambda e, t=t: e.tensor_tensor(
            stat_sum[:, t:t + 1], stat_m[:, t:t + 1], stat_m[:, t:t + 1], OP.mult))
        E("vector", lambda e, t=t: e.scalar_tensor_tensor(
            stat_sq[:, t:t + 1], stat_sq[:, t:t + 1], EPS, stat_sum[:, t:t + 1],
            OP.add, OP.subtract))
        E("vector", lambda e, t=t: e.reciprocal(stat_sum[:, t:t + 1],
                                                stat_sq[:, t:t + 1]), inc="tv")
        E("scalar", lambda e, t=t: e.activation(stat_r[:, t:t + 1],
                                                stat_sum[:, t:t + 1], AF.Sqrt),
          waits=[("tv", now("tv"))], inc="ta")
        E("vector", lambda e, t=t, bi=bi: e.tensor_scalar(
            h2sc[:], xbuf[:, bi * C:(bi + 1) * C], stat_m[:, t:t + 1],
            stat_r[:, t:t + 1], OP.subtract, OP.mult),
          waits=[("ta", now("ta"))])
        E("vector", lambda e: e.tensor_tensor(h2sc[:], h2sc[:], bc0[:], OP.mult))
        E("vector", lambda e: e.tensor_tensor(h2sc[:], h2sc[:], bc1[:], OP.add))
        E("vector", lambda e: e.tensor_reduce(wsc[:, 12:13], h2sc[:], AX.X, OP.max,
                                              apply_absolute_value=True))
        E("vector", lambda e: e.tensor_tensor(pmax[:, 0:1], pmax[:, 0:1],
                                              wsc[:, 12:13], OP.max), inc="tv")
        st_tv[t] = now("tv")
    E("tensor", lambda e: e.transpose(psD[0:1, 0:128], pmax[:, 0:1], eye32_s[:]),
      waits=[("tv", now("tv"))], inc="tt")
    E("vector", lambda e: e.tensor_copy(ccs[:], psD[0:1, 0:128]),
      waits=[("tt", now("tt"))], inc="tv")
    E("pool", lambda e: e.dma_start(out=cc1_in[0:1, :], in_=ccs[:]),
      waits=[("tv", now("tv"))], inc="tp", inc_by=16)
    E("pool", lambda e: e.collective_compute(
        "AllReduce", OP.max, replica_groups=[list(range(8))],
        ins=[cc1_in.ap().opt()], outs=[cc1_out.ap().opt()]), inc="cc")
    E("pool", lambda e: e.dma_start(out=ccs[:], in_=cc1_out[0:1, :]),
      waits=[("cc", now("cc"))], inc="tp", inc_by=16)
    E("vector", lambda e: e.tensor_reduce(scal[0:1, 0:1], ccs[:], AX.X, OP.max),
      waits=[("tp", now("tp"))], inc="tv")
    E("scalar", lambda e: e.activation(scal[0:1, 2:3], clip_s[0:1, 0:1], AF.Sigmoid),
      waits=[("tp", now("tp"))], inc="ta")
    E("vector", lambda e: e.tensor_tensor(scal[0:1, 4:5], scal[0:1, 0:1],
                                          scal[0:1, 2:3], OP.mult),
      waits=[("ta", now("ta"))])
    E("vector", lambda e: e.tensor_single_scalar(scal[0:1, 4:5], scal[0:1, 4:5],
                                                 1.0 / QMAX, OP.mult))
    E("vector", lambda e: e.reciprocal(scal[0:1, 5:6], scal[0:1, 4:5]), inc="tv")
    E("tensor", lambda e: e.matmul(psD[:, 0:2], ones1f[:], scal[0:1, 4:6],
                                   start=True, stop=True),
      waits=[("tv", now("tv"))], inc="tt")
    E("vector", lambda e: e.tensor_copy(scal_p[:, 0:2], psD[:, 0:2]),
      waits=[("tt", now("tt"))], inc="tv")
    if debug:
        E("sync", dma(dbg["scal"], scal[:]), waits=[("tv", now("tv"))],
          inc="ld", inc_by=16)

    xq_ev_ta = {}
    xqv_tv = {}
    xqfm3 = xqfm.rearrange("p (f tok) -> p f tok", f=NC)
    for t in range(NTO):
        bi = t % 2
        E("sync", dma(xbuf[:, bi * C:(bi + 1) * C], x1_sp[t]),
          waits=[("tv", max(now("tv") if t < 2 else 0, xqv_tv.get(t - 2, 0))),
                 ("tt", xq_tt.get(t - 2, 0))], inc="ld", inc_by=16)
        ld_t = now("ld")
        E("vector", lambda e, t=t, bi=bi: e.tensor_scalar(
            qsc32[:, bi * C:(bi + 1) * C], xbuf[:, bi * C:(bi + 1) * C],
            stat_m[:, t:t + 1], stat_r[:, t:t + 1], OP.subtract, OP.mult),
          waits=[("ld", ld_t)])
        E("vector", lambda e, bi=bi: e.tensor_tensor(
            qsc32[:, bi * C:(bi + 1) * C], qsc32[:, bi * C:(bi + 1) * C],
            bc0[:], OP.mult))
        E("vector", lambda e, bi=bi: e.tensor_tensor(
            qsc32[:, bi * C:(bi + 1) * C], qsc32[:, bi * C:(bi + 1) * C],
            bc1[:], OP.add))
        E("vector", lambda e, bi=bi: e.tensor_scalar(
            qsc32[:, bi * C:(bi + 1) * C], qsc32[:, bi * C:(bi + 1) * C],
            scal_p[:, 1:2], None, OP.mult))
        E("vector", lambda e, bi=bi: e.tensor_scalar(
            qsc32[:, bi * C:(bi + 1) * C], qsc32[:, bi * C:(bi + 1) * C],
            -QMAX - 1.0, QMAX, OP.max, OP.min))
        E("vector", lambda e, bi=bi: e.tensor_copy(qint[:], qsc32[:, bi * C:(bi + 1) * C]))
        E("vector", lambda e, bi=bi: e.tensor_copy(xqt[:, bi * C:(bi + 1) * C],
                                                   qint[:]), inc="tv")
        xqv_tv[t] = now("tv")
        if debug:
            E("sync", dma(dbg["xq"][t * 128:(t + 1) * 128, :],
                          xqt[:, bi * C:(bi + 1) * C]),
              waits=[("tv", now("tv"))], inc="ld", inc_by=16)
        for f in range(NC):
            E("tensor", lambda e, f=f, bi=bi: e.transpose(
                psAb[:, f * 128:(f + 1) * 128],
                xqt[:, bi * C + f * 128:bi * C + (f + 1) * 128], eye16_s[:]),
              waits=([("tv", now("tv")), ("ta", xq_ev_ta.get(t - 1, 0))]
                     if f == 0 else ()),
              inc="tt" if f == NC - 1 else None)
        xq_tt[t] = now("tt")
        E("scalar", lambda e, t=t: e.activation(
            xqfm3[:, :, t * 128:(t + 1) * 128],
            psAb[:, 0:1024].rearrange("p (a b) -> p a b", a=NC), AF.Identity),
          waits=[("tt", now("tt"))], inc="ta")
        xq_ev_ta[t] = now("ta")
    xq_ta = now("ta")

    g_tt = {}
    g_tv = {}
    pdma_ld = {}
    for cch in range(NI):
        bi = cch % 2
        E("sync", dma(wgbuf[:, bi * C:(bi + 1) * C], wgT_sp[cch]),
          waits=[("ld", wgu_spill_ld), ("tt", g_tt.get(cch - 2, 0))],
          inc="ld", inc_by=16)
        E("sync", dma(wubuf[:, bi * C:(bi + 1) * C], wuT_sp[cch]), inc="ld", inc_by=16)
        wld = now("ld")
        ps = psA if bi == 0 else psB
        E("vector", lambda e, cch=cch: e.tensor_tensor(
            wsc[:, 14:15], wgusc[:, cch:cch + 1], scal_p[:, 0:1], OP.mult),
          waits=[("ta", xq_ta)])
        E("vector", lambda e, cch=cch: e.tensor_tensor(
            wsc[:, 15:16], wgusc[:, NI + cch:NI + cch + 1], scal_p[:, 0:1],
            OP.mult), inc="tv")
        cgcu_tv = now("tv")
        tb_ta = {}
        tb_tv = {}
        for tb in range(4):
            for gu in range(2):
                wb = wgbuf if gu == 0 else wubuf
                for hh in range(NC):
                    w = []
                    if tb == 0 and gu == 0 and hh == 0:
                        w = [("ld", wld), ("ta", xq_ta), ("tv", g_tv.get(cch - 2, 0))]
                    elif gu == 0 and hh == 0:
                        w = [("ta", tb_ta[tb - 1]), ("tv", tb_tv[tb - 1])]
                    E("tensor", lambda e, ps=ps, gu=gu, hh=hh, tb=tb, bi=bi, wb=wb:
                      e.matmul(ps[:, gu * 512:(gu + 1) * 512],
                               wb[:, bi * C + hh * 128:bi * C + (hh + 1) * 128],
                               xqfm[:, hh * 2048 + tb * 512:hh * 2048 + (tb + 1) * 512],
                               start=(hh == 0), stop=(hh == NC - 1)),
                      waits=w, inc="tt" if hh == NC - 1 else None)
            E("scalar", lambda e, ps=ps: e.activation(silu_t[:], ps[:, 0:512],
                                                      AF.Silu, scale=wsc[:, 14:15]),
              waits=[("tt", now("tt")), ("tv", cgcu_tv)], inc="ta")
            tb_ta[tb] = now("ta")
            E("vector", lambda e, ps=ps, tb=tb: e.scalar_tensor_tensor(
                p_t[:, (tb % 2) * 512:(tb % 2 + 1) * 512], ps[:, 512:1024],
                wsc[:, 15:16], silu_t[:], OP.mult, OP.mult),
              waits=[("ta", now("ta")), ("ld", pdma_ld.get((cch, tb - 2), 0))],
              inc="tv")
            tb_tv[tb] = now("tv")
            E("vector", lambda e, tb=tb: e.tensor_reduce(
                wsc[:, 13:14], p_t[:, (tb % 2) * 512:(tb % 2 + 1) * 512], AX.X,
                OP.max, apply_absolute_value=True))
            E("vector", lambda e: e.tensor_tensor(pmax[:, 1:2], pmax[:, 1:2],
                                                  wsc[:, 13:14], OP.max), inc="tv")
            E("sync", dma(p_sp.ap()[cch, :, tb * 512:(tb + 1) * 512],
                          p_t[:, (tb % 2) * 512:(tb % 2 + 1) * 512]),
              waits=[("tv", now("tv"))], inc="ld", inc_by=16)
            pdma_ld[(cch, tb)] = now("ld")
        g_tt[cch] = now("tt")
        g_tv[cch] = now("tv")
    p_done_ld = now("ld")

    E("tensor", lambda e: e.transpose(psD[0:1, 0:128], pmax[:, 1:2], eye32_s[:]),
      waits=[("tv", now("tv"))], inc="tt")
    E("vector", lambda e: e.tensor_copy(ccs[:], psD[0:1, 0:128]),
      waits=[("tt", now("tt"))], inc="tv")
    E("pool", lambda e: e.dma_start(out=cc2_in[0:1, :], in_=ccs[:]),
      waits=[("tv", now("tv"))], inc="tp", inc_by=16)
    E("pool", lambda e: e.collective_compute(
        "AllReduce", OP.max, replica_groups=[list(range(8))],
        ins=[cc2_in.ap().opt()], outs=[cc2_out.ap().opt()]), inc="cc")
    E("pool", lambda e: e.dma_start(out=ccs[:], in_=cc2_out[0:1, :]),
      waits=[("cc", now("cc"))], inc="tp", inc_by=16)
    E("vector", lambda e: e.tensor_reduce(scal[0:1, 1:2], ccs[:], AX.X, OP.max),
      waits=[("tp", now("tp"))], inc="tv")
    E("scalar", lambda e: e.activation(scal[0:1, 3:4], clip_s[0:1, 1:2], AF.Sigmoid),
      waits=[("tp", now("tp"))], inc="ta")
    E("vector", lambda e: e.tensor_tensor(scal[0:1, 6:7], scal[0:1, 1:2],
                                          scal[0:1, 3:4], OP.mult),
      waits=[("ta", now("ta"))])
    E("vector", lambda e: e.tensor_single_scalar(scal[0:1, 6:7], scal[0:1, 6:7],
                                                 1.0 / QMAX, OP.mult))
    E("vector", lambda e: e.reciprocal(scal[0:1, 7:8], scal[0:1, 6:7]), inc="tv")
    E("tensor", lambda e: e.matmul(psD[:, 2:4], ones1f[:], scal[0:1, 6:8],
                                   start=True, stop=True),
      waits=[("tv", now("tv"))], inc="tt")
    E("vector", lambda e: e.tensor_copy(scal_p[:, 2:4], psD[:, 2:4]),
      waits=[("tt", now("tt"))], inc="tv")
    E("vector", lambda e: e.tensor_tensor(wsc[:, 2:10], wsc[:, 2:10],
                                          vec_fm[:, 40:48], OP.mult), inc="tv")
    E("tensor", lambda e: e.transpose(psD[0:8, 0:128], wsc[:, 2:10], eye32_s[:]),
      waits=[("tv", now("tv"))], inc="tt")
    E("vector", lambda e: e.tensor_copy(rowst[0:8, :], psD[0:8, 0:128]),
      waits=[("tt", now("tt"))], inc="tv")
    E("sync", dma(rowb[0:1, 0:1024], rowst[0:8, :]), waits=[("tv", now("tv"))],
      inc="ld", inc_by=16)
    E("sync", dma(rowtmp[:], rowb[0:1, 0:C]), waits=[("ld", now("ld"))],
      inc="ld", inc_by=16)
    E("vector", lambda e: e.tensor_scalar(rowtmp[:], rowtmp[:],
                                          scal[0:1, 6:7], None, OP.mult),
      waits=[("ld", now("ld"))], inc="tv")
    E("tensor", lambda e: e.matmul(psA[:, 0:512], ones1f[:], rowtmp[0:1, 0:512],
                                   start=True, stop=True),
      waits=[("tv", now("tv"))])
    E("tensor", lambda e: e.matmul(psA[:, 512:1024], ones1f[:],
                                   rowtmp[0:1, 512:1024], start=True, stop=True),
      inc="tt")
    E("vector", lambda e: e.tensor_copy(bc0[:], psA[:]),
      waits=[("tt", now("tt"))], inc="tv")
    coef_tv = now("tv")

    y_ld = {}
    p6_tt = {}
    p6_tv = {}
    pqt3 = pqt.rearrange("p (s c) -> p s c", s=2)
    for t in range(NTO):
        bi = t % 2
        E("sync", dma(pstg.rearrange("p (c k) -> p c k", c=NI),
                      p_sp.ap().rearrange("c p k -> p c k")[:, :, t * 128:(t + 1) * 128]),
          waits=[("ld", p_done_ld), ("tv", p6_tv.get(t - 1, 0)),
                 ("tv", coef_tv if t == 0 else 0)],
          inc="ld", inc_by=16)
        ld_t = now("ld")
        E("vector", lambda e: e.tensor_scalar(
            pstg, pstg, scal_p[:, 3:4], None, OP.mult),
          waits=[("ld", ld_t), ("tt", p6_tt.get(t - 2, 0))])
        E("vector", lambda e: e.tensor_scalar(
            pstg, pstg, -QMAX - 1.0, QMAX, OP.max, OP.min))
        E("vector", lambda e: e.tensor_copy(wint6[:, 0:FI], pstg))
        E("vector", lambda e, bi=bi: e.tensor_copy(
            pqt3[:, bi, :], wint6[:, 0:FI]), inc="tv")
        p6_tv[t] = now("tv")
        ps = psA if bi == 0 else psB
        for sl in range(2):
            for cch in range(NI):
                w = []
                if sl == 0 and cch == 0:
                    w = [("tv", p6_tv[t]), ("tv", y_ld.get(("ev", t - 2), 0))]
                E("tensor", lambda e, ps=ps, sl=sl, cch=cch, bi=bi: e.matmul(
                    ps[:, sl * 512:(sl + 1) * 512],
                    pqt3[:, bi, cch * 128:(cch + 1) * 128],
                    wdT[:, cch * C + sl * 512:cch * C + (sl + 1) * 512],
                    start=(cch == 0), stop=(cch == NI - 1)),
                  waits=w, inc="tt" if cch == NI - 1 else None)
        p6_tt[t] = now("tt")
        E("sync", dma(xbuf[:, bi * C:(bi + 1) * C], x1_sp[t]),
          waits=[("tv", y_ld.get(("ev", t - 2), 0))], inc="ld", inc_by=16)
        xld = now("ld")
        E("vector", lambda e, ps=ps: e.tensor_tensor(dsc[:], ps[:], bc0[:], OP.mult),
          waits=[("tt", p6_tt[t]), ("ld", y_ld.get(t - 2, 0))])
        E("vector", lambda e, bi=bi: e.tensor_tensor(
            ysc[:, bi * C:(bi + 1) * C], dsc[:], xbuf[:, bi * C:(bi + 1) * C],
            OP.add), waits=[("ld", xld)], inc="tv")
        y_ld[("ev", t)] = now("tv")
        E("sync", dma(dy[t * 128:(t + 1) * 128, :], ysc[:, bi * C:(bi + 1) * C]),
          waits=[("tv", now("tv"))], inc="ld", inc_by=16)
        y_ld[t] = now("ld")

    E("sync", lambda e: e.nop(), waits=[("ld", now("ld"))])

    import contextlib
    stack = contextlib.ExitStack()
    sems = {s: stack.enter_context(nc.semaphore("sem_" + s)) for s in SEMS}
    block = stack.enter_context(nc.Block())

    def mk(engine_name):
        evs = [ev for ev in prog.events if ev[0] == engine_name]

        def body(eng):
            maxw = {k: 0 for k in SEMS}
            for _, fn, waits, inc, inc_by in evs:
                for sem, val in waits:
                    if val <= maxw[sem]:
                        continue
                    maxw[sem] = val
                    if engine_name == "tensor" and sem == "tt":
                        continue
                    eng.wait_ge(sems[sem], val)
                inst = fn(eng)
                if inc is not None:
                    assert inst is not None
                    inst.then_inc(sems[inc], inc_by)
        return body

    block.sync(mk("sync"))
    block.tensor(mk("tensor"))
    block.vector(mk("vector"))
    block.scalar(mk("scalar"))
    block.gpsimd(mk("pool"))
    stack.close()
    return nc


def _rope_tables():
    j32 = np.arange(0, D, 2, dtype=np.float32)
    theta32 = (np.float32(1.0) / (np.float32(10000.0) ** (j32 / np.float32(D)))
               ) * np.float32(3.0)
    pos32 = np.arange(L, dtype=np.float32)
    ang32 = (pos32[:, None] * theta32[None, :]).astype(np.float32)
    cos_hi = np.cos(ang32.astype(np.float64)).astype(np.float32)
    sin_hi = np.sin(ang32.astype(np.float64)).astype(np.float32)
    theta64 = (1.0 / (10000.0 ** (np.arange(0, D, 2, dtype=np.float64) / D))) * 3.0
    ang64 = np.arange(L, dtype=np.float64)[:, None] * theta64[None, :]
    cos_lo = (np.cos(ang64) - cos_hi).astype(np.float32)
    sin_lo = (np.sin(ang64) - sin_hi).astype(np.float32)
    return (np.tile(cos_hi, (1, 16)), np.tile(cos_lo, (1, 16)),
            np.tile(sin_hi, (1, 16)), np.tile(sin_lo, (1, 16)))


def _perm_qk():
    perm = np.arange(3 * C)
    for blk in range(2):
        for h in range(H):
            base = blk * C + h * D
            perm[base:base + 32] = base + np.arange(0, D, 2)
            perm[base + 32:base + D] = base + np.arange(1, D, 2)
    return perm


_NC_CACHE = {}


def kernel(L0_attn_x, L0_attn_w_qkv, L0_attn_w_proj, L0_attn_b_proj,
           L0_ffn_clip_x, L0_ffn_clip_down, L0_ffn_w_gate, L0_ffn_w_up,
           L0_ffn_w_down, L0_c, L0_wmod, _debug=False):
    x = np.ascontiguousarray(np.asarray(L0_attn_x, dtype=np.float32))
    wqkv = np.ascontiguousarray(np.asarray(L0_attn_w_qkv, dtype=np.float32))
    wproj = np.ascontiguousarray(np.asarray(L0_attn_w_proj, dtype=np.float32))
    bproj = np.asarray(L0_attn_b_proj, dtype=np.float32).reshape(1, C)
    wg = np.ascontiguousarray(np.asarray(L0_ffn_w_gate, dtype=np.float32))
    wu = np.ascontiguousarray(np.asarray(L0_ffn_w_up, dtype=np.float32))
    wd = np.ascontiguousarray(np.asarray(L0_ffn_w_down, dtype=np.float32))
    cv = np.asarray(L0_c, dtype=np.float32)
    wmod = np.ascontiguousarray(np.asarray(L0_wmod, dtype=np.float32))
    clips = np.array([[float(np.asarray(L0_ffn_clip_x).reshape(-1)[0]),
                       float(np.asarray(L0_ffn_clip_down).reshape(-1)[0])]],
                     dtype=np.float32)

    cos_hi, cos_lo, sin_hi, sin_lo = _rope_tables()
    tabs_base = np.ascontiguousarray(
        np.concatenate([cos_hi, cos_lo, sin_hi, sin_lo], axis=1))
    wqkv_p = np.ascontiguousarray(wqkv[:, _perm_qk()])
    eye16 = np.eye(128, dtype=BF)
    eye32 = np.eye(128, dtype=np.float32)
    sel = np.zeros((2, 128), np.float32)
    sel[0, 0:64] = 1.0
    sel[1, 64:128] = 1.0

    key = ("nc", bool(_debug))
    if key not in _NC_CACHE:
        _NC_CACHE[key] = build_nc(debug=_debug)
    ncobj = _NC_CACHE[key]

    in_maps = []
    for core in range(8):
        b, hf = core // 2, core % 2
        rot = hf * 2048
        xr = np.ascontiguousarray(np.roll(x[b], -rot, axis=0))
        tr = np.ascontiguousarray(np.roll(tabs_base, -rot, axis=0))
        in_maps.append({
            "x": xr, "wqkv": wqkv_p, "wproj": wproj, "bproj": bproj,
            "wmod": wmod, "cvec": np.ascontiguousarray(cv[b].reshape(NC, 128)),
            "clips": clips, "wg": wg, "wu": wu, "wd": wd,
            "tabs": tr, "eye16": eye16, "eye32": eye32, "sel": sel,
        })
    res = run_bass_kernel_spmd(ncobj, in_maps, core_ids=list(range(8)))
    out = np.empty((B, L, C), dtype=np.float32)
    for core in range(8):
        b, hf = core // 2, core % 2
        out[b, hf * 2048:(hf + 1) * 2048, :] = res.results[core]["y"]
    if _debug:
        return out, res
    return out


# version 05
# speedup vs baseline: 1.0826x; 1.0284x over previous
import numpy as np
import ml_dtypes
import concourse.bass as bass
import concourse.mybir as mybir
from concourse.bass_utils import run_bass_kernel_spmd

F32 = mybir.dt.float32
BF16 = mybir.dt.bfloat16
I32 = mybir.dt.int32
AF = mybir.ActivationFunctionType
OP = mybir.AluOpType
AX = mybir.AxisListType
BF = ml_dtypes.bfloat16

B, L, C = 4, 4096, 1024
H, D = 16, 64
NT = L // 128
NTO = 2048 // 128
FI = 2816
NI = FI // 128
NC = C // 128
QMAX = 127.0
EPS = 1e-5

ENGINES = ("sync", "tensor", "vector", "scalar", "pool")
_OWN = {"sync": "ld", "tensor": "tt", "vector": "tv", "scalar": "ta", "pool": "tp"}
SEMS = ("ld", "tt", "tv", "ta", "tp", "cc", "rd")


class Prog:
    def __init__(self):
        self.events = []
        self.cnt = {k: 0 for k in SEMS}
        self._gates = {}

    def now(self, sem):
        return self.cnt[sem]

    def _gate(self, owner, sem, v):
        key = (sem, v)
        if key in self._gates:
            return self._gates[key]
        full = self.cnt[sem]
        self.events.append((owner, lambda e: e.nop(), [(sem, full)], "rd", 1))
        self.cnt["rd"] += 1
        self._gates[key] = self.cnt["rd"]
        return self.cnt["rd"]

    def emit(self, eng, fn, waits=(), inc=None, inc_by=1):
        w = [(s, v) for (s, v) in waits if v > 0]
        w2 = []
        for s, v in w:
            if s == "ld" and eng != "sync":
                w2.append(("rd", self._gate("sync", "ld", v)))
            elif s == "tp" and eng != "pool":
                w2.append(("rd", self._gate("pool", "tp", v)))
            elif s == "ld" and eng == "sync":
                w2.append(("ld", self.cnt["ld"]))
            elif s == "tp" and eng == "pool":
                w2.append(("tp", self.cnt["tp"]))
            else:
                w2.append((s, v))
        w = w2
        if eng in ("vector", "scalar", "pool"):
            own = _OWN[eng]
            if self.cnt[own] > 0 and own != "tp":
                w.append((own, self.cnt[own]))
            elif own == "tp" and self.cnt[own] > 0:
                w.append((own, self.cnt[own]))
            if inc is None:
                inc = own
                inc_by = 1
        self.events.append((eng, fn, w, inc, inc_by))
        if inc is not None:
            self.cnt[inc] += inc_by


def build_nc(debug=False):
    nc = bass.Bass("TRN2", target_bir_lowering=False)

    dx = nc.dram_tensor("x", [L, C], F32, kind="ExternalInput").ap()
    dwqkv = nc.dram_tensor("wqkv", [C, 3 * C], F32, kind="ExternalInput").ap()
    dwproj = nc.dram_tensor("wproj", [C, C], F32, kind="ExternalInput").ap()
    dbproj = nc.dram_tensor("bproj", [1, C], F32, kind="ExternalInput").ap()
    dwmod = nc.dram_tensor("wmod", [C, 6 * C], F32, kind="ExternalInput").ap()
    dcvec = nc.dram_tensor("cvec", [NC, 128], F32, kind="ExternalInput").ap()
    dclip = nc.dram_tensor("clips", [1, 2], F32, kind="ExternalInput").ap()
    dwg = nc.dram_tensor("wg", [FI, C], F32, kind="ExternalInput").ap()
    dwu = nc.dram_tensor("wu", [FI, C], F32, kind="ExternalInput").ap()
    dwd = nc.dram_tensor("wd", [C, FI], F32, kind="ExternalInput").ap()
    dtab = nc.dram_tensor("tabs", [L, 2048], F32, kind="ExternalInput").ap()
    deye16 = nc.dram_tensor("eye16", [128, 128], BF16, kind="ExternalInput").ap()
    deye32 = nc.dram_tensor("eye32", [128, 128], F32, kind="ExternalInput").ap()
    dsel = nc.dram_tensor("sel", [2, 128], F32, kind="ExternalInput").ap()
    dy = nc.dram_tensor("y", [2048, C], F32, kind="ExternalOutput").ap()

    dbg = {}
    if debug:
        for nm, shp, dt_ in [("h1", [L, C], BF16), ("mod", [128, 48], F32),
                             ("kpost", [L, C], BF16), ("qpost", [2048, C], BF16),
                             ("attnout", [128, NC * 2048], BF16),
                             ("x1", [2048, C], F32), ("xq", [2048, C], BF16),
                             ("scal", [1, 16], F32)]:
            dbg[nm] = nc.dram_tensor("dbg_" + nm, shp, dt_, kind="ExternalOutput").ap()

    q_sp = nc.dram_tensor("q_sp", [NTO, 128, C], BF16, kind="Internal")
    k_sp = nc.dram_tensor("k_sp", [NT, 128, C], BF16, kind="Internal")
    v_sp = nc.dram_tensor("v_sp", [NT, 128, C], BF16, kind="Internal")
    p_sp = nc.dram_tensor("p_sp", [NI, 128, 2048], F32, kind="Internal")
    x1_sp = nc.dram_tensor("x1_sp", [NTO, 128, C], F32, kind="Internal")
    dl_sp = nc.dram_tensor("dl_sp", [NTO, 128, C], BF16, kind="Internal")
    wgT_sp = nc.dram_tensor("wgT_sp", [NI, 128, C], BF16, kind="Internal")
    wuT_sp = nc.dram_tensor("wuT_sp", [NI, 128, C], BF16, kind="Internal")
    wpj_sp = nc.dram_tensor("wpj_sp", [NC, 128, C], BF16, kind="Internal")
    rowb = nc.dram_tensor("rowb", [1, 4096], F32, kind="Internal")
    rowb_bf = nc.dram_tensor("rowb_bf", [1, 4096], BF16, kind="Internal")
    cc1_in = nc.dram_tensor("cc1_in", [1, 128], F32, kind="Internal")
    cc1_out = nc.dram_tensor("cc1_out", [1, 128], F32, kind="Internal")
    cc2_in = nc.dram_tensor("cc2_in", [1, 128], F32, kind="Internal")
    cc2_out = nc.dram_tensor("cc2_out", [1, 128], F32, kind="Internal")

    A = nc.alloc_sbuf_tensor
    AR1 = A("AR1", [128, 24576], BF16)
    AR2 = A("AR2", [128, 6144], F32)
    AR3 = A("AR3", [128, 16384], BF16)
    PSC = A("PSC", [128, 6656], F32)

    wqkv_s = AR1[:]
    kbuf = AR1[:, 0:8192]
    vbuf = AR1[:, 8192:16384]
    qbuf = AR1[:, 16384:20480]
    ptA = AR1[:, 20480:22528]
    ptB = AR1[:, 22528:24576]
    pqt = AR1[:, 0:5632]
    wgbuf = AR1[:, 11264:13312]
    wubuf = AR1[:, 13312:15360]
    wpjbuf = AR1[:, 15360:17408]
    wst = AR2[:, 0:6144]
    tbuf = AR2[:, 0:4096]
    ceff = AR2[:, 4096:6144]
    kfm = AR2.bitcast(BF16)[:, 0:4096]
    qfm = AR2.bitcast(BF16)[:, 4096:6144]
    drow = AR2[:, 3072:4096]
    rbt = AR2[:, 4096:4608]
    pstg_s = [AR2[:, 0:2816], AR2[:, 2816:5632]]
    wmb = AR3[:, 0:6144]
    attnout = AR3[:]
    xqfm = AR3[:]
    qkvbias_row = AR3[0:1, 8192:11264]
    rsc32 = PSC[:, 0:1024]
    rrow = PSC[0:1, 0:1024]
    h1t = PSC.bitcast(BF16)[:, 2048:4096]
    h1fm = PSC.bitcast(BF16)[:, 4096:6144]
    qsc = PSC.bitcast(BF16)[:, 6144:8192]
    ksc = PSC.bitcast(BF16)[:, 8192:10240]
    vsc = PSC.bitcast(BF16)[:, 10240:12288]
    wqi = PSC.bitcast(BF16)[:, 0:2816]
    wint = PSC.bitcast(I32)[:, 1536:4352]
    wint6 = PSC.bitcast(I32)[:, 2048:4864]
    qsc32 = PSC[:, 0:2048]
    h2sc = PSC[:, 2048:3072]
    qint = PSC.bitcast(I32)[:, 3072:4096]
    xqt = PSC.bitcast(BF16)[:, 8192:10240]
    silu_t = PSC[:, 5120:5632]
    p_t = PSC[:, 5632:6656]
    ysc = PSC[:, 0:2048]
    x1stg = PSC[:, 2048:4096]
    cv_s = A("cv_s", [NC, 128], F32)
    clip_s = A("clip_s", [1, 2], F32)
    sel_s = A("sel_s", [2, 128], F32)
    rrow2 = A("rrow2", [2, 512], F32)
    eye16_s = A("eye16_s", [128, 128], BF16)
    eye32_s = A("eye32_s", [128, 128], F32)
    ones1b = A("ones1b", [1, 128], BF16)
    ones1f = A("ones1f", [1, 128], F32)
    onesd = A("onesd", [128, 1], BF16)
    onesr = A("onesr", [128, 64], F32)
    silucf = A("silucf", [128, NC], BF16)
    silucf32 = A("silucf32", [128, NC], F32)
    mod_fm = A("mod_fm", [128, 48], F32)
    vec_fm = A("vec_fm", [128, 64], F32)
    sa_bf = A("sa_bf", [128, NC], BF16)
    bias_fm = A("bias_fm", [128, 24], F32)
    bias_fmb = A("bias_fmb", [128, 24], BF16)
    rowst = A("rowst", [32, 128], F32)
    rowtmp = A("rowtmp", [1, C], F32)
    bc0 = A("bc0", [128, C], F32)
    bc1 = A("bc1", [128, C], F32)
    wdT = A("wdT", [128, NI * C], BF16)
    xbuf = A("xbuf", [128, 2 * C], F32)
    stat_sum = A("stat_sum", [128, NT], F32)
    stat_sq = A("stat_sq", [128, NT], F32)
    stat_m = A("stat_m", [128, NT], F32)
    stat_r = A("stat_r", [128, NT], F32)
    dsc = A("dsc", [128, C], F32)
    wsc = A("wsc", [128, 16], F32)
    wgusc = A("wgusc", [128, 44], F32)
    scal = A("scal", [1, 16], F32)
    scal_p = A("scal_p", [128, 4], F32)
    pmax = A("pmax", [128, 2], F32)
    ccs = A("ccs", [1, 128], F32)

    psA = nc.alloc_psum_tensor("psA", [128, 1024], F32)
    psB = nc.alloc_psum_tensor("psB", [128, 1024], F32)
    psC = nc.alloc_psum_tensor("psC", [128, 1024], F32)
    psD = nc.alloc_psum_tensor("psD", [128, 512], F32)
    psF = nc.alloc_psum_tensor("psF", [128, 512], F32)
    psAb = psA[:].bitcast(BF16)

    prog = Prog()
    E = prog.emit
    now = prog.now

    def dma(out, in_):
        return lambda e: e.dma_start(out=out, in_=in_)

    E("sync", dma(cv_s[:], dcvec), inc="ld", inc_by=16)
    E("sync", dma(clip_s[:], dclip), inc="ld", inc_by=16)
    E("sync", dma(sel_s[:], dsel), inc="ld", inc_by=16)
    E("sync", dma(eye16_s[:], deye16), inc="ld", inc_by=16)
    E("sync", dma(eye32_s[:], deye32), inc="ld", inc_by=16)
    ld0 = now("ld")
    E("vector", lambda e: e.memset(ones1b[:], 1.0))
    E("vector", lambda e: e.memset(ones1f[:], 1.0))
    E("vector", lambda e: e.memset(onesd[:], 1.0))
    E("vector", lambda e: e.memset(onesr[:], 1.0))
    E("vector", lambda e: e.memset(stat_sum[:], 0.0))
    E("vector", lambda e: e.memset(stat_sq[:], 0.0))
    E("vector", lambda e: e.memset(pmax[:], 0.0))
    E("vector", lambda e: e.memset(scal[:], 0.0))
    E("vector", lambda e: e.memset(scal_p[:], 0.0), inc="tv")

    E("scalar", lambda e: e.activation(cv_s[:], cv_s[:], AF.Silu),
      waits=[("ld", ld0)], inc="ta")
    E("tensor", lambda e: e.transpose(psD[0:128, 0:NC], cv_s[:], eye32_s[0:NC, 0:NC]),
      waits=[("ta", now("ta"))], inc="tt")
    E("vector", lambda e: e.tensor_copy(silucf32[:], psD[0:128, 0:NC]),
      waits=[("tt", now("tt"))])
    E("vector", lambda e: e.tensor_copy(silucf[:], psD[0:128, 0:NC]),
      inc="tv")

    wmod_sl = [wst, AR3.bitcast(F32)[:, 0:6144]]
    mm_tt = {}
    for f in range(NC):
        wsl = wmod_sl[f % 2]
        E("sync", dma(wsl, dwmod[f * 128:(f + 1) * 128, :]),
          waits=[("tv", now("tv") if f == 0 else 0), ("tt", mm_tt.get(f - 2, 0))],
          inc="ld", inc_by=16)
        ld_f = now("ld")
        for j in range(48):
            E("tensor",
              lambda e, f=f, j=j, wsl=wsl: e.matmul(
                  psC[:, j:j + 1], wsl[:, j * 128:(j + 1) * 128],
                  silucf32[:, f:f + 1], start=True, stop=True),
              waits=[("tv", now("tv")), ("ld", ld_f)] if j == 0 else (),
              inc="tt" if j == 47 else None)
        mm_tt[f] = now("tt")
        if f == 0:
            E("vector", lambda e: e.tensor_copy(mod_fm[:], psC[:, 0:48]),
              waits=[("tt", now("tt"))], inc="tv")
        else:
            E("vector", lambda e: e.tensor_tensor(mod_fm[:], mod_fm[:],
                                                  psC[:, 0:48], OP.add),
              waits=[("tt", now("tt"))], inc="tv")
    if debug:
        E("sync", dma(dbg["mod"], mod_fm[:]), waits=[("tv", now("tv"))],
          inc="ld", inc_by=16)

    E("vector", lambda e: e.tensor_single_scalar(vec_fm[:, 0:8], mod_fm[:, 8:16], 1.0, OP.add))
    E("vector", lambda e: e.reciprocal(vec_fm[:, 8:16], vec_fm[:, 0:8]))
    E("vector", lambda e: e.tensor_tensor(vec_fm[:, 8:16], vec_fm[:, 8:16], mod_fm[:, 0:8], OP.mult))
    E("vector", lambda e: e.tensor_copy(sa_bf[:], vec_fm[:, 8:16]))
    for dst, src in [(16, 16), (40, 40)]:
        E("vector", lambda e, dst=dst, src=src: e.tensor_single_scalar(
            vec_fm[:, dst:dst + 8], mod_fm[:, src:src + 8], -1.0, OP.mult))
        E("vector", lambda e, dst=dst, src=src: e.tensor_tensor(
            vec_fm[:, dst:dst + 8], vec_fm[:, dst:dst + 8], mod_fm[:, src:src + 8], OP.max))
        E("vector", lambda e, dst=dst: e.tensor_single_scalar(
            vec_fm[:, dst:dst + 8], vec_fm[:, dst:dst + 8], 0.25, OP.max))
        E("vector", lambda e, src=src: e.tensor_scalar(
            vec_fm[:, 56:64], mod_fm[:, src:src + 8], 0.0, 2.0, OP.is_ge, OP.mult))
        E("vector", lambda e: e.tensor_single_scalar(
            vec_fm[:, 56:64], vec_fm[:, 56:64], -1.0, OP.add))
        E("vector", lambda e, dst=dst: e.tensor_tensor(
            vec_fm[:, dst:dst + 8], vec_fm[:, dst:dst + 8], vec_fm[:, 56:64], OP.mult))
    E("vector", lambda e: e.tensor_single_scalar(vec_fm[:, 24:32], mod_fm[:, 32:40], 1.0, OP.add))
    E("vector", lambda e: e.tensor_copy(vec_fm[:, 32:40], mod_fm[:, 24:32]), inc="tv")
    vec_tv = now("tv")

    E("tensor", lambda e: e.transpose(psD[0:32, 0:128], vec_fm[:, 16:48], eye32_s[:]),
      waits=[("tv", vec_tv)], inc="tt")
    E("vector", lambda e: e.tensor_copy(rowst[:], psD[0:32, 0:128]),
      waits=[("tt", now("tt"))], inc="tv")
    E("sync", dma(rowb[0:1, 0:4096], rowst[:]), waits=[("tv", now("tv"))],
      inc="ld", inc_by=16)
    rows_ld = now("ld")

    qk_tv = {}
    for f in range(NC):
        wsl = wst[:, (f % 2) * 3 * C:(f % 2 + 1) * 3 * C]
        E("sync", dma(wsl, dwqkv[f * 128:(f + 1) * 128, :]),
          waits=[("tv", qk_tv.get(f - 2, 0)),
                 ("tt", mm_tt[NC - 1] if f < 2 else 0)],
          inc="ld", inc_by=16)
        E("vector",
          lambda e, f=f, wsl=wsl: e.tensor_scalar(
              wqkv_s[:, f * 3 * C:(f + 1) * 3 * C], wsl, vec_fm[:, f:f + 1],
              None, OP.mult),
          waits=[("ld", now("ld"))], inc="tv")
        qk_tv[f] = now("tv")
    wqkv_tv = now("tv")

    for f in range(NC):
        for j in range(24):
            E("tensor",
              lambda e, j=j, f=f: e.matmul(
                  psD[:, 64 + j:65 + j],
                  wqkv_s[:, f * 3 * C + j * 128:f * 3 * C + (j + 1) * 128],
                  sa_bf[:, f:f + 1], start=True, stop=True),
              waits=[("tv", now("tv"))] if j == 0 else (),
              inc="tt" if j == 23 else None)
        if f == 0:
            E("vector", lambda e: e.tensor_copy(bias_fm[:], psD[:, 64:88]),
              waits=[("tt", now("tt"))], inc="tv")
        else:
            E("vector", lambda e: e.tensor_tensor(bias_fm[:], bias_fm[:],
                                                  psD[:, 64:88], OP.add),
              waits=[("tt", now("tt"))], inc="tv")
    E("vector", lambda e: e.tensor_copy(bias_fmb[:], bias_fm[:]), inc="tv")
    E("tensor", lambda e: e.transpose(psAb[0:24, 0:128], bias_fmb[:], eye16_s[:]),
      waits=[("tv", now("tv"))], inc="tt")
    E("vector", lambda e: e.tensor_copy(rowst.bitcast(BF16)[0:24, 0:128],
                                        psAb[0:24, 0:128]),
      waits=[("tt", now("tt"))], inc="tv")
    E("sync", dma(rowb_bf[0:1, 0:3072], rowst.bitcast(BF16)[0:24, 0:128]),
      waits=[("tv", now("tv"))], inc="ld", inc_by=16)
    E("sync", dma(qkvbias_row[:], rowb_bf[0:1, 0:3 * C]), waits=[("ld", now("ld"))],
      inc="ld", inc_by=16)

    pj_tv = {}
    for f in range(NC):
        wsl = wst[:, (f % 2) * 2816:(f % 2) * 2816 + C]
        E("sync", dma(wsl, dwproj[f * 128:(f + 1) * 128, :]),
          waits=[("tv", pj_tv.get(f - 2, wqkv_tv if f < 2 else 0))],
          inc="ld", inc_by=16)
        E("vector", lambda e, wsl=wsl: e.tensor_copy(wqi[:, 0:C], wsl),
          waits=[("ld", now("ld"))], inc="tv")
        pj_tv[f] = now("tv")
        E("sync", dma(wpj_sp[f], wqi[:, 0:C]), waits=[("tv", now("tv"))],
          inc="ld", inc_by=16)
    wpj_spill_ld = now("ld")
    pj_base_tv = now("tv")

    def quant_chunk(wsl, width, prev_tt, keep_scale_col=None, keep_tensor=None):
        E("vector", lambda e, wsl=wsl, width=width: e.tensor_reduce(
            wsc[:, 0:1], wsl[:, 0:width], AX.X, OP.max, apply_absolute_value=True),
          waits=[("ld", now("ld")), ("tt", prev_tt)])
        E("vector", lambda e: e.reciprocal(wsc[:, 1:2], wsc[:, 0:1]))
        E("vector", lambda e: e.tensor_single_scalar(wsc[:, 1:2], wsc[:, 1:2], QMAX, OP.mult))
        if keep_scale_col is not None:
            kt = wsc if keep_tensor is None else keep_tensor
            E("vector", lambda e, k=keep_scale_col, kt=kt: e.tensor_single_scalar(
                kt[:, k:k + 1], wsc[:, 0:1], 1.0 / QMAX, OP.mult))
        E("vector", lambda e, wsl=wsl, width=width: e.tensor_scalar(
            wint[:, 0:width], wsl[:, 0:width], wsc[:, 1:2], None, OP.mult))
        E("vector", lambda e, width=width: e.tensor_copy(wqi[:, 0:width],
                                                         wint[:, 0:width]), inc="tv")

    wgu_spill_ld = 0
    wsl2 = [wst[:, 0:2816], wst[:, 2816:5632]]
    q_tv = {}
    q_tt = {}
    sidx = 0
    for wi, (src, spT) in enumerate([(dwg, wgT_sp), (dwu, wuT_sp)]):
        for cch in range(NI):
            wsl = wsl2[sidx % 2]
            E("sync", dma(wsl[:, 0:C], src[cch * 128:(cch + 1) * 128, :]),
              waits=[("tv", q_tv.get(sidx - 2, pj_base_tv if sidx < 2 else 0))],
              inc="ld", inc_by=16)
            quant_chunk(wsl, C, q_tt.get(sidx - 1, 0),
                        keep_scale_col=wi * NI + cch, keep_tensor=wgusc)
            q_tv[sidx] = now("tv")
            for t in range(NC):
                E("tensor",
                  lambda e, t=t: e.transpose(psAb[:, t * 128:(t + 1) * 128],
                                             wqi[:, t * 128:(t + 1) * 128],
                                             eye16_s[:]),
                  waits=[("tv", now("tv"))] if t == 0 else (),
                  inc="tt" if t == NC - 1 else None)
            q_tt[sidx] = now("tt")
            E("vector", lambda e: e.tensor_copy(ksc[:, 0:C], psAb[:, 0:1024]),
              waits=[("tt", now("tt"))], inc="tv")
            E("sync", dma(spT[cch], ksc[:, 0:C]), waits=[("tv", now("tv"))],
              inc="ld", inc_by=16)
            wgu_spill_ld = now("ld")
            sidx += 1
    wdT3 = wdT[:].rearrange("p (cc hc) -> p cc hc", cc=NI)
    for hh in range(NC):
        wsl = wsl2[sidx % 2]
        E("sync", dma(wsl[:, 0:FI], dwd[hh * 128:(hh + 1) * 128, :]),
          waits=[("tv", q_tv.get(sidx - 2, 0))], inc="ld", inc_by=16)
        quant_chunk(wsl, FI, q_tt.get(sidx - 1, 0), keep_scale_col=2 + hh)
        q_tv[sidx] = now("tv")
        for grp in range(3):
            lo = grp * 8
            n = min(8, NI - lo)
            for u in range(n):
                E("tensor",
                  lambda e, lo=lo, u=u: e.transpose(
                      psAb[:, u * 128:(u + 1) * 128],
                      wqi[:, (lo + u) * 128:(lo + u + 1) * 128], eye16_s[:]),
                  waits=[("tv", now("tv"))] if (grp == 0 and u == 0) else
                        ([("tv", now("tv"))] if u == 0 else ()),
                  inc="tt" if u == n - 1 else None)
            E("vector",
              lambda e, lo=lo, n=n, hh=hh: e.tensor_copy(
                  wdT3[:, lo:lo + n, hh * 128:(hh + 1) * 128],
                  psAb[:, 0:n * 128].rearrange("p (a b) -> p a b", a=n)),
              waits=[("tt", now("tt"))], inc="tv")
        q_tt[sidx] = now("tt")
        sidx += 1
    setup_tv = now("tv")

    r_inv = 1.0 / C
    qkv_ps = {0: psA, 1: psA, 2: psB, 3: psB, 4: psC, 5: psC}
    ev_tv = {}
    sp_ld = {}
    xload_ld = {}
    apply_tv = {}
    tadd_tv = {}
    h1T_tt = {}
    h1ev_ta = {}
    qkv_tt = {}
    last_spill_ld = 0

    for t in range(NT):
        our = t < NTO
        bi = t % 2
        E("sync", dma(xbuf[:, bi * C:(bi + 1) * C], dx[t * 128:(t + 1) * 128, :]),
          waits=[("tv", apply_tv.get(t - 2, 0)), ("ta", h1ev_ta.get(t - 2, 0))],
          inc="ld", inc_by=16)
        E("sync", dma(tbuf[:, bi * 2048:(bi + 1) * 2048],
                      dtab[t * 128:(t + 1) * 128, :]),
          waits=[("tv", max(tadd_tv.get(t - 2, 0), setup_tv if t < 2 else 0))],
          inc="ld", inc_by=16)
        xload_ld[t] = now("ld")
        E("scalar", lambda e, bi=bi, t=t: e.activation(
            dsc[:], xbuf[:, bi * C:(bi + 1) * C], AF.Identity,
            accum_out=stat_sum[:, t:t + 1]),
          waits=[("ld", xload_ld[t]), ("tv", setup_tv if t == 0 else 0)])
        E("scalar", lambda e, bi=bi, t=t: e.activation(
            dsc[:], xbuf[:, bi * C:(bi + 1) * C], AF.Square,
            accum_out=stat_sq[:, t:t + 1]), inc="ta")
        E("vector", lambda e, t=t: e.tensor_single_scalar(
            stat_m[:, t:t + 1], stat_sum[:, t:t + 1], r_inv, OP.mult),
          waits=[("ta", now("ta"))])
        E("vector", lambda e, t=t: e.tensor_single_scalar(
            stat_sq[:, t:t + 1], stat_sq[:, t:t + 1], r_inv, OP.mult))
        E("vector", lambda e, t=t: e.tensor_tensor(
            stat_sum[:, t:t + 1], stat_m[:, t:t + 1], stat_m[:, t:t + 1], OP.mult))
        E("vector", lambda e, t=t: e.scalar_tensor_tensor(
            stat_sq[:, t:t + 1], stat_sq[:, t:t + 1], EPS, stat_sum[:, t:t + 1],
            OP.add, OP.subtract))
        E("vector", lambda e, t=t: e.reciprocal(stat_sum[:, t:t + 1],
                                                stat_sq[:, t:t + 1]), inc="tv")
        E("scalar", lambda e, t=t: e.activation(stat_r[:, t:t + 1],
                                                stat_sum[:, t:t + 1], AF.Sqrt),
          waits=[("tv", now("tv"))], inc="ta")
        E("vector", lambda e, bi=bi, t=t: e.tensor_scalar(
            h1t[:, bi * C:(bi + 1) * C], xbuf[:, bi * C:(bi + 1) * C],
            stat_m[:, t:t + 1], stat_r[:, t:t + 1], OP.subtract, OP.mult),
          waits=[("ta", now("ta")), ("tt", h1T_tt.get(t - 2, 0))], inc="tv")
        apply_tv[t] = now("tv")
        if debug:
            E("sync", dma(dbg["h1"][t * 128:(t + 1) * 128, :],
                          h1t[:, bi * C:(bi + 1) * C]),
              waits=[("tv", now("tv"))], inc="ld", inc_by=16)
        for f in range(NC):
            E("tensor", lambda e, f=f, bi=bi: e.transpose(
                psAb[:, f * 128:(f + 1) * 128],
                h1t[:, bi * C + f * 128:bi * C + (f + 1) * 128], eye16_s[:]),
              waits=([("tv", now("tv")), ("ta", h1ev_ta.get(t - 1, 0))]
                     if f == 0 else ()),
              inc="tt" if f == NC - 1 else None)
        h1T_tt[t] = now("tt")
        E("scalar", lambda e, bi=bi: e.activation(h1fm[:, bi * C:(bi + 1) * C],
                                                  psAb[:, 0:1024], AF.Identity),
          waits=[("tt", now("tt")), ("tt", qkv_tt.get((t - 2, 2), 0))], inc="ta")
        h1ev_ta[t] = now("ta")

        slices = list(range(6)) if our else [2, 3, 4, 5]
        for sl in slices:
            ps = qkv_ps[sl]
            col = (sl % 2) * 512
            grp = sl // 2
            w = [("ta", h1ev_ta[t])]
            if (t - 1, grp) in ev_tv:
                w.append(("tv", ev_tv[(t - 1, grp)]))
            elif grp == 0 and (NTO - 1, 0) in ev_tv:
                w.append(("tv", ev_tv[(NTO - 1, 0)]))
            E("tensor", lambda e, ps=ps, col=col, sl=sl: e.matmul(
                ps[:, col:col + 512], ones1b[:],
                qkvbias_row[0:1, sl * 512:(sl + 1) * 512], start=True, stop=False),
              waits=w)
            for f in range(NC):
                E("tensor", lambda e, ps=ps, col=col, sl=sl, f=f, bi=bi: e.matmul(
                    ps[:, col:col + 512],
                    h1fm[:, bi * C + f * 128:bi * C + (f + 1) * 128],
                    wqkv_s[:, f * 3 * C + sl * 512:f * 3 * C + (sl + 1) * 512],
                    start=False, stop=(f == NC - 1)),
                  inc="tt" if (f == NC - 1 and sl % 2 == 1) else None)
            if sl % 2 == 1:
                qkv_tt[(t, grp)] = now("tt")

        E("vector", lambda e, bi=bi: e.tensor_tensor(
            ceff[:, bi * 1024:bi * 1024 + 512],
            tbuf[:, bi * 2048:bi * 2048 + 512],
            tbuf[:, bi * 2048 + 512:bi * 2048 + 1024], OP.add))
        E("vector", lambda e, bi=bi: e.tensor_tensor(
            ceff[:, bi * 1024 + 512:bi * 1024 + 1024],
            tbuf[:, bi * 2048 + 1024:bi * 2048 + 1536],
            tbuf[:, bi * 2048 + 1536:bi * 2048 + 2048], OP.add), inc="tv")
        tadd_tv[t] = now("tv")

        def rope(psrc, dst, grp):
            co = bi * 1024
            do = bi * C
            re_v = psrc[:].rearrange("p (h s i) -> p h s i", h=16, s=2)[:, :, 0, :]
            im_v = psrc[:].rearrange("p (h s i) -> p h s i", h=16, s=2)[:, :, 1, :]
            cosv = ceff[:, co:co + 512].rearrange("p (h i) -> p h i", h=16)
            sinv = ceff[:, co + 512:co + 1024].rearrange("p (h i) -> p h i", h=16)
            o_re = dst[:, do:do + C].rearrange("p (h s i) -> p h s i", h=16, s=2)[:, :, 0, :]
            o_im = dst[:, do:do + C].rearrange("p (h s i) -> p h s i", h=16, s=2)[:, :, 1, :]
            sc_a = rsc32[:, 0:512].rearrange("p (h i) -> p h i", h=16)
            sc_b = rsc32[:, 512:1024].rearrange("p (h i) -> p h i", h=16)
            w0 = [("tt", qkv_tt[(t, grp)]), ("tv", now("tv")),
                  ("ld", sp_ld.get((t - 2, grp), 0))]
            E("vector", lambda e: e.tensor_tensor(sc_a, re_v, cosv, OP.mult), waits=w0)
            E("vector", lambda e: e.tensor_tensor(sc_b, im_v, sinv, OP.mult))
            E("vector", lambda e: e.tensor_tensor(o_re, sc_a, sc_b, OP.subtract))
            E("vector", lambda e: e.tensor_tensor(sc_a, re_v, sinv, OP.mult))
            E("vector", lambda e: e.tensor_tensor(sc_b, im_v, cosv, OP.mult))
            E("vector", lambda e: e.tensor_tensor(o_im, sc_a, sc_b, OP.add), inc="tv")

        if our:
            rope(psA, qsc, 0)
            ev_tv[(t, 0)] = now("tv")
            E("sync", dma(q_sp[t], qsc[:, bi * C:(bi + 1) * C]),
              waits=[("tv", now("tv"))], inc="ld", inc_by=16)
            sp_ld[(t, 0)] = now("ld")
            if debug:
                E("sync", dma(dbg["qpost"][t * 128:(t + 1) * 128, :],
                              qsc[:, bi * C:(bi + 1) * C]), inc="ld", inc_by=16)
        rope(psB, ksc, 1)
        ev_tv[(t, 1)] = now("tv")
        E("sync", dma(k_sp[t], ksc[:, bi * C:(bi + 1) * C]),
          waits=[("tv", now("tv"))], inc="ld", inc_by=16)
        sp_ld[(t, 1)] = now("ld")
        if debug:
            E("sync", dma(dbg["kpost"][t * 128:(t + 1) * 128, :],
                          ksc[:, bi * C:(bi + 1) * C]), inc="ld", inc_by=16)
        E("vector", lambda e, bi=bi: e.tensor_copy(vsc[:, bi * C:(bi + 1) * C], psC[:]),
          waits=[("tt", qkv_tt[(t, 2)]), ("ld", sp_ld.get((t - 2, 2), 0))], inc="tv")
        ev_tv[(t, 2)] = now("tv")
        E("sync", dma(v_sp[t], vsc[:, bi * C:(bi + 1) * C]),
          waits=[("tv", now("tv"))], inc="ld", inc_by=16)
        sp_ld[(t, 2)] = now("ld")
        last_spill_ld = now("ld")

    abuf_ld = {}
    hp_last_tv = {}
    tpose_ta = {}
    attn_tv = 0
    gate_drow_ld = 0
    gate_rrow_tt = 0
    gate_rbt_tv = 0
    rb_ta = 0
    for hp in range(8):
        bi = hp % 2
        wv = [("tv", hp_last_tv.get(hp - 2, 0)), ("ld", last_spill_ld)]
        E("sync", dma(
            kbuf[:, bi * 4096:(bi + 1) * 4096].rearrange("p (t c) -> p t c", t=NT),
            k_sp.ap().rearrange("t p c -> p t c")[:, :, hp * 128:(hp + 1) * 128]),
          waits=wv, inc="ld", inc_by=16)
        E("sync", dma(
            vbuf[:, bi * 4096:(bi + 1) * 4096].rearrange("p (t c) -> p t c", t=NT),
            v_sp.ap().rearrange("t p c -> p t c")[:, :, hp * 128:(hp + 1) * 128]),
          inc="ld", inc_by=16)
        E("sync", dma(
            qbuf[:, bi * 2048:(bi + 1) * 2048].rearrange("p (t c) -> p t c", t=NTO),
            q_sp.ap().rearrange("t p c -> p t c")[:, :, hp * 128:(hp + 1) * 128]),
          inc="ld", inc_by=16)
        abuf_ld[hp] = now("ld")

        for grp in range(6):
            if grp < 2:
                srcb, soff, dstt, g = qbuf, bi * 2048, qfm, grp
            else:
                srcb, soff, dstt, g = kbuf, bi * 4096, kfm, grp - 2
            for u in range(8):
                idx = g * 8 + u
                E("tensor", lambda e, srcb=srcb, soff=soff, idx=idx, u=u:
                  e.transpose(psAb[:, u * 128:(u + 1) * 128],
                              srcb[:, soff + idx * 128:soff + (idx + 1) * 128],
                              eye16_s[:]),
                  waits=([("ld", abuf_ld[hp]), ("ta", now("ta")),
                          ("tv", hp_last_tv.get(hp - 1, 0))] if u == 0 else ()),
                  inc="tt" if u == 7 else None)
            E("scalar", lambda e, dstt=dstt, g=g: e.activation(
                dstt[:, g * 1024:(g + 1) * 1024], psAb[:, 0:1024], AF.Identity),
              waits=[("tt", now("tt"))], inc="ta")
        tpose_ta[hp] = now("ta")

        for qb in range(4):
            st_tt = {}
            exp_ta = {}
            pv_tt = {}
            ev_done_tv = attn_tv
            for j in range(16):
                for g2 in range(2):
                    kc = j * 2 + g2
                    for hi, (ps, rl) in enumerate([(psA, 0), (psB, 64)]):
                        w = []
                        if g2 == 0 and hi == 0:
                            w = [("ta", tpose_ta[hp]),
                                 ("ta", exp_ta.get((0, j - 1), 0)),
                                 ("ta", exp_ta.get((1, j - 1), 0))]
                        E("tensor", lambda e, ps=ps, rl=rl, kc=kc, qb=qb, g2=g2:
                          e.matmul(ps[:, g2 * 512:(g2 + 1) * 512],
                                   kfm[rl:rl + 64, kc * 128:(kc + 1) * 128],
                                   qfm[rl:rl + 64, qb * 512:(qb + 1) * 512],
                                   start=True, stop=True, tile_position=(rl, 0)),
                          waits=w,
                          inc="tt" if g2 == 1 else None)
                        st_tt[(hi, kc)] = now("tt")
                for hi, (ps, pt) in enumerate([(psA, ptA), (psB, ptB)]):
                    w = [("tt", st_tt[(hi, j * 2 + 1)])]
                    if (hi, j - 2) in pv_tt:
                        w.append(("tt", pv_tt[(hi, j - 2)]))
                    E("scalar", lambda e, ps=ps, pt=pt, j=j: e.activation(
                        pt[:, (j % 2) * 1024:(j % 2 + 1) * 1024], ps[:],
                        AF.Exp, scale=0.125),
                      waits=w, inc="ta")
                    exp_ta[(hi, j)] = now("ta")
                for g2 in range(2):
                    kc = j * 2 + g2
                    for hi, pt in enumerate([ptA, ptB]):
                        cl = hi * 64
                        pv_dst = psC[0:64, 0:512] if hi == 0 else psC[64:128, 512:1024]
                        dn_dst = psD[0:1, 0:512] if hi == 0 else psF[64:65, 0:512]
                        ptap = pt[:, (j % 2) * 1024 + g2 * 512:
                                  (j % 2) * 1024 + (g2 + 1) * 512]
                        w = []
                        if g2 == 0:
                            w = [("ta", exp_ta[(hi, j)])]
                        if kc == 0:
                            w.append(("tv", ev_done_tv))
                            w.append(("ta", rb_ta))
                        E("tensor", lambda e, ptap=ptap, kc=kc, cl=cl, bi=bi, hi=hi,
                          pv_dst=pv_dst:
                          e.matmul(pv_dst,
                                   vbuf[:, bi * 4096 + kc * 128 + hi * 64:
                                        bi * 4096 + kc * 128 + hi * 64 + 64],
                                   ptap, start=(kc == 0), stop=(kc == NT - 1),
                                   tile_position=(0, cl)),
                          waits=w, inc=None)
                        E("tensor", lambda e, ptap=ptap, kc=kc, cl=cl, dn_dst=dn_dst:
                          e.matmul(dn_dst, onesd[:], ptap,
                                   start=(kc == 0), stop=(kc == NT - 1),
                                   tile_position=(0, cl)),
                          inc="tt" if g2 == 1 else None)
                        if g2 == 1:
                            pv_tt[(hi, j)] = now("tt")
            pv_done_tt = now("tt")
            E("scalar", lambda e: e.activation(drow[0:1, 0:512], psD[0:1, 0:512],
                                               AF.Identity),
              waits=[("tt", pv_done_tt), ("tt", gate_drow_ld)])
            E("scalar", lambda e: e.activation(drow[64:65, 512:1024],
                                               psF[64:65, 0:512], AF.Identity),
              inc="ta")
            dr_ta = now("ta")
            E("vector", lambda e: e.reciprocal(drow[0:1, 0:512], drow[0:1, 0:512]),
              waits=[("ta", dr_ta)])
            E("vector", lambda e: e.reciprocal(drow[64:65, 512:1024],
                                               drow[64:65, 512:1024]), inc="tv")
            E("tensor", lambda e: e.matmul(psD[0:64, 0:512], onesr[0:1, :],
                                           drow[0:1, 0:512],
                                           start=True, stop=True),
              waits=[("tv", now("tv"))])
            E("tensor", lambda e: e.matmul(psF[64:128, 0:512], onesr[64:65, :],
                                           drow[64:65, 512:1024],
                                           start=True, stop=True,
                                           tile_position=(64, 64)), inc="tt")
            gate_drow_ld = now("tt")
            E("scalar", lambda e: e.activation(rbt[0:64, :], psD[0:64, 0:512],
                                               AF.Identity),
              waits=[("tt", now("tt")), ("tv", gate_rbt_tv)])
            E("scalar", lambda e: e.activation(rbt[64:128, :], psF[64:128, 0:512],
                                               AF.Identity), inc="ta")
            rb_ta = now("ta")
            ao = hp * 2048 + qb * 512
            E("vector", lambda e, ao=ao: e.tensor_tensor(
                attnout[0:64, ao:ao + 512], psC[0:64, 0:512], rbt[0:64, :], OP.mult),
              waits=[("ta", now("ta"))])
            E("vector", lambda e, ao=ao: e.tensor_tensor(
                attnout[64:128, ao:ao + 512], psC[64:128, 512:1024],
                rbt[64:128, :], OP.mult), inc="tv")
            attn_tv = now("tv")
            gate_rbt_tv = now("tv")
        hp_last_tv[hp] = now("tv")
    if debug:
        for f in range(NC):
            E("sync", dma(dbg["attnout"][:, f * 2048:(f + 1) * 2048],
                          attnout[:, f * 2048:(f + 1) * 2048]),
              waits=[("tv", attn_tv)], inc="ld", inc_by=16)

    for tile, off in [(bc0, 0), (bc1, -1)]:
        rsrc = dbproj if off < 0 else rowb[0:1, off:off + C]
        E("sync", dma(rowtmp[:], rsrc),
          waits=[("ld", rows_ld), ("tt", now("tt"))], inc="ld", inc_by=16)
        E("tensor", lambda e: e.matmul(psA[:, 0:512], ones1f[:], rowtmp[0:1, 0:512],
                                       start=True, stop=True),
          waits=[("ld", now("ld")), ("ta", now("ta")),
                 ("tv", max(attn_tv, now("tv")))])
        E("tensor", lambda e: e.matmul(psA[:, 512:1024], ones1f[:],
                                       rowtmp[0:1, 512:1024], start=True, stop=True),
          inc="tt")
        E("vector", lambda e, tile=tile: e.tensor_copy(tile[:], psA[:]),
          waits=[("tt", now("tt"))], inc="tv")
    bc_tv = now("tv")

    pj_ld = {}
    pj_ev_ta = {}
    dl_sp_ld = {}
    for fo in range(NC):
        bi = fo % 2
        E("sync", dma(wpjbuf[:, bi * C:(bi + 1) * C].rearrange(
            "p (f c) -> p f c", f=NC),
            wpj_sp.ap().rearrange("f p c -> p f c")[:, :, fo * 128:(fo + 1) * 128]),
          waits=[("ld", wpj_spill_ld), ("tt", pj_ld.get(fo - 2, 0)),
                 ("tv", attn_tv if fo < 2 else 0)],
          inc="ld", inc_by=16)
        wld = now("ld")
        for sl in range(4):
            ps = (psA, psA, psB, psB)[sl]
            col = (sl % 2) * 512
            for f in range(NC):
                w = []
                if sl == 0 and f == 0:
                    w = [("ld", wld), ("tv", bc_tv), ("ta", pj_ev_ta.get(fo - 1, 0))]
                E("tensor", lambda e, ps=ps, col=col, sl=sl, f=f, fo=fo, bi=bi:
                  e.matmul(ps[:, col:col + 512],
                           wpjbuf[:, bi * C + f * 128:bi * C + (f + 1) * 128],
                           attnout[:, f * 2048 + sl * 512:f * 2048 + (sl + 1) * 512],
                           start=(f == 0), stop=(f == NC - 1)),
                  waits=w, inc="tt" if f == NC - 1 else None)
        pj_ld[fo] = now("tt")
        E("scalar", lambda e: e.activation(ksc[:, 0:C], psA[:], AF.Identity),
          waits=[("tt", now("tt"))])
        E("scalar", lambda e: e.activation(ksc[:, C:2 * C], psB[:], AF.Identity),
          inc="ta")
        ev_ta = now("ta")
        for grp in range(2):
            for u in range(8):
                tt_i = grp * 8 + u
                E("tensor", lambda e, tt_i=tt_i, u=u: e.transpose(
                    psAb[:, u * 128:(u + 1) * 128],
                    ksc[:, tt_i * 128:(tt_i + 1) * 128], eye16_s[:]),
                  waits=([("ta", ev_ta), ("ta", pj_ev_ta.get((fo, grp - 1),
                                                            pj_ev_ta.get((fo - 1, 1), 0)))]
                         if u == 0 else ()),
                  inc="tt" if u == 7 else None)
            E("scalar", lambda e, grp=grp: e.activation(qsc[:, grp * C:(grp + 1) * C],
                                                        psAb[:, 0:1024], AF.Identity),
              waits=[("tt", now("tt")), ("ld", dl_sp_ld.get((fo - 1, grp), 0))],
              inc="ta")
            pj_ev_ta[(fo, grp)] = now("ta")
            E("sync", dma(
                bass.AP(dl_sp, grp * 8 * 128 * 1024 + fo * 128,
                        [[1024, 128], [131072, 8], [1, 128]]),
                qsc[:, grp * C:(grp + 1) * C].rearrange("p (a b) -> p a b", a=8)),
              waits=[("ta", now("ta"))], inc="ld", inc_by=16)
            dl_sp_ld[(fo, grp)] = now("ld")
    dl_done_ld = now("ld")
    pj_ev_ta_last = now("ta")

    x1_tv = {}
    x1_ld = {}
    xq_tt = {}
    for t in range(NTO):
        bi = t % 2
        E("sync", dma(xbuf[:, bi * C:(bi + 1) * C], dx[t * 128:(t + 1) * 128, :]),
          waits=[("tv", x1_tv.get(t - 2, 0))], inc="ld", inc_by=16)
        E("sync", dma(ksc[:, bi * C:(bi + 1) * C], dl_sp[t]),
          waits=[("ld", dl_done_ld), ("tv", x1_tv.get(t - 2, 0)),
                 ("ta", pj_ev_ta_last if t < 2 else 0)],
          inc="ld", inc_by=16)
        ld_t = now("ld")
        E("vector", lambda e, bi=bi: e.tensor_tensor(
            dsc[:], ksc[:, bi * C:(bi + 1) * C], bc1[:], OP.add),
          waits=[("ld", ld_t)])
        E("vector", lambda e: e.tensor_tensor(dsc[:], dsc[:], bc0[:], OP.mult))
        E("vector", lambda e, t=t, bi=bi: e.tensor_tensor(
            x1stg[:, bi * C:(bi + 1) * C], dsc[:], xbuf[:, bi * C:(bi + 1) * C],
            OP.add), inc="tv")
        x1_tv[t] = now("tv")
        E("sync", dma(x1_sp[t], x1stg[:, bi * C:(bi + 1) * C]),
          waits=[("tv", now("tv"))], inc="ld", inc_by=16)
        x1_ld[t] = now("ld")
        if debug:
            E("sync", dma(dbg["x1"][t * 128:(t + 1) * 128, :],
                          x1stg[:, bi * C:(bi + 1) * C]), inc="ld", inc_by=16)
    x1_done_ld = now("ld")

    for tile, off in [(bc0, C), (bc1, 2 * C)]:
        E("sync", dma(rowtmp[:], rowb[0:1, off:off + C]),
          waits=[("tt", now("tt"))], inc="ld", inc_by=16)
        E("tensor", lambda e: e.matmul(psA[:, 0:512], ones1f[:], rowtmp[0:1, 0:512],
                                       start=True, stop=True),
          waits=[("ld", now("ld")), ("tv", now("tv")), ("ta", now("ta"))])
        E("tensor", lambda e: e.matmul(psA[:, 512:1024], ones1f[:],
                                       rowtmp[0:1, 512:1024], start=True, stop=True),
          inc="tt")
        E("vector", lambda e, tile=tile: e.tensor_copy(tile[:], psA[:]),
          waits=[("tt", now("tt"))], inc="tv")
    E("vector", lambda e: e.memset(stat_sum[:], 0.0))
    E("vector", lambda e: e.memset(stat_sq[:], 0.0), inc="tv")
    p4bc_tv = now("tv")

    st_ld = {}
    st_ta = {}
    st_tv = {}
    for t in range(NTO):
        bi = t % 2
        E("sync", dma(xbuf[:, bi * C:(bi + 1) * C], x1_sp[t]),
          waits=[("ld", x1_done_ld), ("ta", st_ta.get(t - 2, 0)),
                 ("tv", max(p4bc_tv if t < 2 else 0, st_tv.get(t - 2, 0)))],
          inc="ld", inc_by=16)
        st_ld[t] = now("ld")
        E("scalar", lambda e, bi=bi, t=t: e.activation(
            dsc[:], xbuf[:, bi * C:(bi + 1) * C], AF.Identity,
            accum_out=stat_sum[:, t:t + 1]), waits=[("ld", st_ld[t])])
        E("scalar", lambda e, bi=bi, t=t: e.activation(
            dsc[:], xbuf[:, bi * C:(bi + 1) * C], AF.Square,
            accum_out=stat_sq[:, t:t + 1]), inc="ta")
        st_ta[t] = now("ta")
        E("vector", lambda e, t=t: e.tensor_single_scalar(
            stat_m[:, t:t + 1], stat_sum[:, t:t + 1], r_inv, OP.mult),
          waits=[("ta", st_ta[t])])
        E("vector", lambda e, t=t: e.tensor_single_scalar(
            stat_sq[:, t:t + 1], stat_sq[:, t:t + 1], r_inv, OP.mult))
        E("vector", lambda e, t=t: e.tensor_tensor(
            stat_sum[:, t:t + 1], stat_m[:, t:t + 1], stat_m[:, t:t + 1], OP.mult))
        E("vector", lambda e, t=t: e.scalar_tensor_tensor(
            stat_sq[:, t:t + 1], stat_sq[:, t:t + 1], EPS, stat_sum[:, t:t + 1],
            OP.add, OP.subtract))
        E("vector", lambda e, t=t: e.reciprocal(stat_sum[:, t:t + 1],
                                                stat_sq[:, t:t + 1]), inc="tv")
        E("scalar", lambda e, t=t: e.activation(stat_r[:, t:t + 1],
                                                stat_sum[:, t:t + 1], AF.Sqrt),
          waits=[("tv", now("tv"))], inc="ta")
        E("vector", lambda e, t=t, bi=bi: e.tensor_scalar(
            h2sc[:], xbuf[:, bi * C:(bi + 1) * C], stat_m[:, t:t + 1],
            stat_r[:, t:t + 1], OP.subtract, OP.mult),
          waits=[("ta", now("ta"))])
        E("vector", lambda e: e.tensor_tensor(h2sc[:], h2sc[:], bc0[:], OP.mult))
        E("vector", lambda e: e.tensor_tensor(h2sc[:], h2sc[:], bc1[:], OP.add))
        E("vector", lambda e: e.tensor_reduce(wsc[:, 12:13], h2sc[:], AX.X, OP.max,
                                              apply_absolute_value=True))
        E("vector", lambda e: e.tensor_tensor(pmax[:, 0:1], pmax[:, 0:1],
                                              wsc[:, 12:13], OP.max), inc="tv")
        st_tv[t] = now("tv")
    E("tensor", lambda e: e.transpose(psD[0:1, 0:128], pmax[:, 0:1], eye32_s[:]),
      waits=[("tv", now("tv"))], inc="tt")
    E("vector", lambda e: e.tensor_copy(ccs[:], psD[0:1, 0:128]),
      waits=[("tt", now("tt"))], inc="tv")
    E("pool", lambda e: e.dma_start(out=cc1_in[0:1, :], in_=ccs[:]),
      waits=[("tv", now("tv"))], inc="tp", inc_by=16)
    E("pool", lambda e: e.collective_compute(
        "AllReduce", OP.max, replica_groups=[list(range(8))],
        ins=[cc1_in.ap().opt()], outs=[cc1_out.ap().opt()]), inc="cc")
    E("pool", lambda e: e.dma_start(out=ccs[:], in_=cc1_out[0:1, :]),
      waits=[("cc", now("cc"))], inc="tp", inc_by=16)
    E("vector", lambda e: e.tensor_reduce(scal[0:1, 0:1], ccs[:], AX.X, OP.max),
      waits=[("tp", now("tp"))], inc="tv")
    E("scalar", lambda e: e.activation(scal[0:1, 2:3], clip_s[0:1, 0:1], AF.Sigmoid),
      waits=[("tp", now("tp"))], inc="ta")
    E("vector", lambda e: e.tensor_tensor(scal[0:1, 4:5], scal[0:1, 0:1],
                                          scal[0:1, 2:3], OP.mult),
      waits=[("ta", now("ta"))])
    E("vector", lambda e: e.tensor_single_scalar(scal[0:1, 4:5], scal[0:1, 4:5],
                                                 1.0 / QMAX, OP.mult))
    E("vector", lambda e: e.reciprocal(scal[0:1, 5:6], scal[0:1, 4:5]), inc="tv")
    E("tensor", lambda e: e.matmul(psD[:, 0:2], ones1f[:], scal[0:1, 4:6],
                                   start=True, stop=True),
      waits=[("tv", now("tv"))], inc="tt")
    E("vector", lambda e: e.tensor_copy(scal_p[:, 0:2], psD[:, 0:2]),
      waits=[("tt", now("tt"))], inc="tv")
    if debug:
        E("sync", dma(dbg["scal"], scal[:]), waits=[("tv", now("tv"))],
          inc="ld", inc_by=16)

    xq_ev_ta = {}
    xqv_tv = {}
    xqfm3 = xqfm.rearrange("p (f tok) -> p f tok", f=NC)
    for t in range(NTO):
        bi = t % 2
        E("sync", dma(xbuf[:, bi * C:(bi + 1) * C], x1_sp[t]),
          waits=[("tv", max(now("tv") if t < 2 else 0, xqv_tv.get(t - 2, 0))),
                 ("tt", xq_tt.get(t - 2, 0))], inc="ld", inc_by=16)
        ld_t = now("ld")
        E("vector", lambda e, t=t, bi=bi: e.tensor_scalar(
            qsc32[:, bi * C:(bi + 1) * C], xbuf[:, bi * C:(bi + 1) * C],
            stat_m[:, t:t + 1], stat_r[:, t:t + 1], OP.subtract, OP.mult),
          waits=[("ld", ld_t)])
        E("vector", lambda e, bi=bi: e.tensor_tensor(
            qsc32[:, bi * C:(bi + 1) * C], qsc32[:, bi * C:(bi + 1) * C],
            bc0[:], OP.mult))
        E("vector", lambda e, bi=bi: e.tensor_tensor(
            qsc32[:, bi * C:(bi + 1) * C], qsc32[:, bi * C:(bi + 1) * C],
            bc1[:], OP.add))
        E("vector", lambda e, bi=bi: e.tensor_scalar(
            qsc32[:, bi * C:(bi + 1) * C], qsc32[:, bi * C:(bi + 1) * C],
            scal_p[:, 1:2], None, OP.mult))
        E("vector", lambda e, bi=bi: e.tensor_scalar(
            qsc32[:, bi * C:(bi + 1) * C], qsc32[:, bi * C:(bi + 1) * C],
            -QMAX - 1.0, QMAX, OP.max, OP.min))
        E("vector", lambda e, bi=bi: e.tensor_copy(qint[:], qsc32[:, bi * C:(bi + 1) * C]))
        E("vector", lambda e, bi=bi: e.tensor_copy(xqt[:, bi * C:(bi + 1) * C],
                                                   qint[:]), inc="tv")
        xqv_tv[t] = now("tv")
        if debug:
            E("sync", dma(dbg["xq"][t * 128:(t + 1) * 128, :],
                          xqt[:, bi * C:(bi + 1) * C]),
              waits=[("tv", now("tv"))], inc="ld", inc_by=16)
        for f in range(NC):
            E("tensor", lambda e, f=f, bi=bi: e.transpose(
                psAb[:, f * 128:(f + 1) * 128],
                xqt[:, bi * C + f * 128:bi * C + (f + 1) * 128], eye16_s[:]),
              waits=([("tv", now("tv")), ("ta", xq_ev_ta.get(t - 1, 0))]
                     if f == 0 else ()),
              inc="tt" if f == NC - 1 else None)
        xq_tt[t] = now("tt")
        E("scalar", lambda e, t=t: e.activation(
            xqfm3[:, :, t * 128:(t + 1) * 128],
            psAb[:, 0:1024].rearrange("p (a b) -> p a b", a=NC), AF.Identity),
          waits=[("tt", now("tt"))], inc="ta")
        xq_ev_ta[t] = now("ta")
    xq_ta = now("ta")

    g_tt = {}
    g_tv = {}
    pdma_ld = {}
    p5_ev_ta = {}
    p5_ev_tv = {}
    p5_stt_tv = {}
    for cch in range(NI):
        bi = cch % 2
        E("sync", dma(wgbuf[:, bi * C:(bi + 1) * C], wgT_sp[cch]),
          waits=[("ld", wgu_spill_ld), ("tt", g_tt.get(cch - 2, 0))],
          inc="ld", inc_by=16)
        E("sync", dma(wubuf[:, bi * C:(bi + 1) * C], wuT_sp[cch]), inc="ld", inc_by=16)
        wld = now("ld")
        E("vector", lambda e, cch=cch: e.tensor_tensor(
            wsc[:, 14:15], wgusc[:, cch:cch + 1], scal_p[:, 0:1], OP.mult),
          waits=[("ta", xq_ta)])
        E("vector", lambda e, cch=cch: e.tensor_tensor(
            wsc[:, 15:16], wgusc[:, NI + cch:NI + cch + 1], scal_p[:, 0:1],
            OP.mult), inc="tv")
        cgcu_tv = now("tv")
        for tb in range(4):
            lin = cch * 4 + tb
            ps = psA if lin % 2 == 0 else psB
            for gu in range(2):
                wb = wgbuf if gu == 0 else wubuf
                for hh in range(NC):
                    w = []
                    if gu == 0 and hh == 0:
                        w = [("ta", p5_ev_ta.get(lin - 2, 0)),
                             ("tv", p5_ev_tv.get(lin - 2, 0))]
                        if tb == 0:
                            w += [("ld", wld), ("ta", xq_ta),
                                  ("tv", g_tv.get(cch - 2, 0))]
                    E("tensor", lambda e, ps=ps, gu=gu, hh=hh, tb=tb, bi=bi, wb=wb:
                      e.matmul(ps[:, gu * 512:(gu + 1) * 512],
                               wb[:, bi * C + hh * 128:bi * C + (hh + 1) * 128],
                               xqfm[:, hh * 2048 + tb * 512:hh * 2048 + (tb + 1) * 512],
                               start=(hh == 0), stop=(hh == NC - 1)),
                      waits=w, inc="tt" if hh == NC - 1 else None)
            E("scalar", lambda e, ps=ps: e.activation(silu_t[:], ps[:, 0:512],
                                                      AF.Silu, scale=wsc[:, 14:15]),
              waits=[("tt", now("tt")), ("tv", max(cgcu_tv, p5_stt_tv.get(lin - 1, 0)))],
              inc="ta")
            p5_ev_ta[lin] = now("ta")
            E("vector", lambda e, ps=ps, lin=lin: e.scalar_tensor_tensor(
                p_t[:, (lin % 2) * 512:(lin % 2 + 1) * 512], ps[:, 512:1024],
                wsc[:, 15:16], silu_t[:], OP.mult, OP.mult),
              waits=[("ta", now("ta")), ("ld", pdma_ld.get(lin - 2, 0))])
            p5_stt_tv[lin] = now("tv")
            E("vector", lambda e, lin=lin: e.tensor_reduce(
                wsc[:, 13:14], p_t[:, (lin % 2) * 512:(lin % 2 + 1) * 512], AX.X,
                OP.max, apply_absolute_value=True))
            E("vector", lambda e: e.tensor_tensor(pmax[:, 1:2], pmax[:, 1:2],
                                                  wsc[:, 13:14], OP.max), inc="tv")
            p5_ev_tv[lin] = now("tv")
            E("sync", dma(p_sp.ap()[cch, :, tb * 512:(tb + 1) * 512],
                          p_t[:, (lin % 2) * 512:(lin % 2 + 1) * 512]),
              waits=[("tv", now("tv"))], inc="ld", inc_by=16)
            pdma_ld[lin] = now("ld")
        g_tt[cch] = now("tt")
        g_tv[cch] = now("tv")
    p_done_ld = now("ld")

    E("tensor", lambda e: e.transpose(psD[0:1, 0:128], pmax[:, 1:2], eye32_s[:]),
      waits=[("tv", now("tv"))], inc="tt")
    E("vector", lambda e: e.tensor_copy(ccs[:], psD[0:1, 0:128]),
      waits=[("tt", now("tt"))], inc="tv")
    E("pool", lambda e: e.dma_start(out=cc2_in[0:1, :], in_=ccs[:]),
      waits=[("tv", now("tv"))], inc="tp", inc_by=16)
    E("pool", lambda e: e.collective_compute(
        "AllReduce", OP.max, replica_groups=[list(range(8))],
        ins=[cc2_in.ap().opt()], outs=[cc2_out.ap().opt()]), inc="cc")
    E("pool", lambda e: e.dma_start(out=ccs[:], in_=cc2_out[0:1, :]),
      waits=[("cc", now("cc"))], inc="tp", inc_by=16)
    E("vector", lambda e: e.tensor_reduce(scal[0:1, 1:2], ccs[:], AX.X, OP.max),
      waits=[("tp", now("tp"))], inc="tv")
    E("scalar", lambda e: e.activation(scal[0:1, 3:4], clip_s[0:1, 1:2], AF.Sigmoid),
      waits=[("tp", now("tp"))], inc="ta")
    E("vector", lambda e: e.tensor_tensor(scal[0:1, 6:7], scal[0:1, 1:2],
                                          scal[0:1, 3:4], OP.mult),
      waits=[("ta", now("ta"))])
    E("vector", lambda e: e.tensor_single_scalar(scal[0:1, 6:7], scal[0:1, 6:7],
                                                 1.0 / QMAX, OP.mult))
    E("vector", lambda e: e.reciprocal(scal[0:1, 7:8], scal[0:1, 6:7]), inc="tv")
    E("tensor", lambda e: e.matmul(psD[:, 2:4], ones1f[:], scal[0:1, 6:8],
                                   start=True, stop=True),
      waits=[("tv", now("tv"))], inc="tt")
    E("vector", lambda e: e.tensor_copy(scal_p[:, 2:4], psD[:, 2:4]),
      waits=[("tt", now("tt"))], inc="tv")
    E("vector", lambda e: e.tensor_tensor(wsc[:, 2:10], wsc[:, 2:10],
                                          vec_fm[:, 40:48], OP.mult), inc="tv")
    E("tensor", lambda e: e.transpose(psD[0:8, 0:128], wsc[:, 2:10], eye32_s[:]),
      waits=[("tv", now("tv"))], inc="tt")
    E("vector", lambda e: e.tensor_copy(rowst[0:8, :], psD[0:8, 0:128]),
      waits=[("tt", now("tt"))], inc="tv")
    E("sync", dma(rowb[0:1, 0:1024], rowst[0:8, :]), waits=[("tv", now("tv"))],
      inc="ld", inc_by=16)
    E("sync", dma(rowtmp[:], rowb[0:1, 0:C]), waits=[("ld", now("ld"))],
      inc="ld", inc_by=16)
    E("vector", lambda e: e.tensor_scalar(rowtmp[:], rowtmp[:],
                                          scal[0:1, 6:7], None, OP.mult),
      waits=[("ld", now("ld"))], inc="tv")
    E("tensor", lambda e: e.matmul(psA[:, 0:512], ones1f[:], rowtmp[0:1, 0:512],
                                   start=True, stop=True),
      waits=[("tv", now("tv"))])
    E("tensor", lambda e: e.matmul(psA[:, 512:1024], ones1f[:],
                                   rowtmp[0:1, 512:1024], start=True, stop=True),
      inc="tt")
    E("vector", lambda e: e.tensor_copy(bc0[:], psA[:]),
      waits=[("tt", now("tt"))], inc="tv")
    coef_tv = now("tv")

    y_ld = {}
    p6_tt = {}
    p6_tv = {}
    pqt3 = pqt.rearrange("p (s c) -> p s c", s=2)
    for t in range(NTO):
        bi = t % 2
        pstg = pstg_s[bi]
        E("sync", dma(pstg.rearrange("p (c k) -> p c k", c=NI),
                      p_sp.ap().rearrange("c p k -> p c k")[:, :, t * 128:(t + 1) * 128]),
          waits=[("ld", p_done_ld), ("tv", p6_tv.get(t - 2, 0)),
                 ("tv", coef_tv if t == 0 else 0)],
          inc="ld", inc_by=16)
        ld_t = now("ld")
        E("vector", lambda e, pstg=pstg: e.tensor_scalar(
            pstg, pstg, scal_p[:, 3:4], None, OP.mult),
          waits=[("ld", ld_t), ("tt", p6_tt.get(t - 2, 0))])
        E("vector", lambda e, pstg=pstg: e.tensor_scalar(
            pstg, pstg, -QMAX - 1.0, QMAX, OP.max, OP.min))
        E("vector", lambda e, pstg=pstg: e.tensor_copy(wint6[:, 0:FI], pstg))
        E("vector", lambda e, bi=bi: e.tensor_copy(
            pqt3[:, bi, :], wint6[:, 0:FI]), inc="tv")
        p6_tv[t] = now("tv")
        ps = psA if bi == 0 else psB
        for sl in range(2):
            for cch in range(NI):
                w = []
                if sl == 0 and cch == 0:
                    w = [("tv", p6_tv[t]), ("tv", y_ld.get(("ev", t - 2), 0))]
                E("tensor", lambda e, ps=ps, sl=sl, cch=cch, bi=bi: e.matmul(
                    ps[:, sl * 512:(sl + 1) * 512],
                    pqt3[:, bi, cch * 128:(cch + 1) * 128],
                    wdT[:, cch * C + sl * 512:cch * C + (sl + 1) * 512],
                    start=(cch == 0), stop=(cch == NI - 1)),
                  waits=w, inc="tt" if cch == NI - 1 else None)
        p6_tt[t] = now("tt")
        E("sync", dma(xbuf[:, bi * C:(bi + 1) * C], x1_sp[t]),
          waits=[("tv", y_ld.get(("ev", t - 2), 0))], inc="ld", inc_by=16)
        xld = now("ld")
        E("vector", lambda e, ps=ps: e.tensor_tensor(dsc[:], ps[:], bc0[:], OP.mult),
          waits=[("tt", p6_tt[t]), ("ld", y_ld.get(t - 2, 0))])
        E("vector", lambda e, bi=bi: e.tensor_tensor(
            ysc[:, bi * C:(bi + 1) * C], dsc[:], xbuf[:, bi * C:(bi + 1) * C],
            OP.add), waits=[("ld", xld)], inc="tv")
        y_ld[("ev", t)] = now("tv")
        E("sync", dma(dy[t * 128:(t + 1) * 128, :], ysc[:, bi * C:(bi + 1) * C]),
          waits=[("tv", now("tv"))], inc="ld", inc_by=16)
        y_ld[t] = now("ld")

    E("sync", lambda e: e.nop(), waits=[("ld", now("ld"))])

    import contextlib
    stack = contextlib.ExitStack()
    sems = {s: stack.enter_context(nc.semaphore("sem_" + s)) for s in SEMS}
    block = stack.enter_context(nc.Block())

    def mk(engine_name):
        evs = [ev for ev in prog.events if ev[0] == engine_name]

        def body(eng):
            maxw = {k: 0 for k in SEMS}
            for _, fn, waits, inc, inc_by in evs:
                for sem, val in waits:
                    if val <= maxw[sem]:
                        continue
                    maxw[sem] = val
                    if engine_name == "tensor" and sem == "tt":
                        continue
                    eng.wait_ge(sems[sem], val)
                inst = fn(eng)
                if inc is not None:
                    assert inst is not None
                    inst.then_inc(sems[inc], inc_by)
        return body

    block.sync(mk("sync"))
    block.tensor(mk("tensor"))
    block.vector(mk("vector"))
    block.scalar(mk("scalar"))
    block.gpsimd(mk("pool"))
    stack.close()
    return nc


def _rope_tables():
    j32 = np.arange(0, D, 2, dtype=np.float32)
    theta32 = (np.float32(1.0) / (np.float32(10000.0) ** (j32 / np.float32(D)))
               ) * np.float32(3.0)
    pos32 = np.arange(L, dtype=np.float32)
    ang32 = (pos32[:, None] * theta32[None, :]).astype(np.float32)
    cos_hi = np.cos(ang32.astype(np.float64)).astype(np.float32)
    sin_hi = np.sin(ang32.astype(np.float64)).astype(np.float32)
    theta64 = (1.0 / (10000.0 ** (np.arange(0, D, 2, dtype=np.float64) / D))) * 3.0
    ang64 = np.arange(L, dtype=np.float64)[:, None] * theta64[None, :]
    cos_lo = (np.cos(ang64) - cos_hi).astype(np.float32)
    sin_lo = (np.sin(ang64) - sin_hi).astype(np.float32)
    return (np.tile(cos_hi, (1, 16)), np.tile(cos_lo, (1, 16)),
            np.tile(sin_hi, (1, 16)), np.tile(sin_lo, (1, 16)))


def _perm_qk():
    perm = np.arange(3 * C)
    for blk in range(2):
        for h in range(H):
            base = blk * C + h * D
            perm[base:base + 32] = base + np.arange(0, D, 2)
            perm[base + 32:base + D] = base + np.arange(1, D, 2)
    return perm


_NC_CACHE = {}


def kernel(L0_attn_x, L0_attn_w_qkv, L0_attn_w_proj, L0_attn_b_proj,
           L0_ffn_clip_x, L0_ffn_clip_down, L0_ffn_w_gate, L0_ffn_w_up,
           L0_ffn_w_down, L0_c, L0_wmod, _debug=False):
    x = np.ascontiguousarray(np.asarray(L0_attn_x, dtype=np.float32))
    wqkv = np.ascontiguousarray(np.asarray(L0_attn_w_qkv, dtype=np.float32))
    wproj = np.ascontiguousarray(np.asarray(L0_attn_w_proj, dtype=np.float32))
    bproj = np.asarray(L0_attn_b_proj, dtype=np.float32).reshape(1, C)
    wg = np.ascontiguousarray(np.asarray(L0_ffn_w_gate, dtype=np.float32))
    wu = np.ascontiguousarray(np.asarray(L0_ffn_w_up, dtype=np.float32))
    wd = np.ascontiguousarray(np.asarray(L0_ffn_w_down, dtype=np.float32))
    cv = np.asarray(L0_c, dtype=np.float32)
    wmod = np.ascontiguousarray(np.asarray(L0_wmod, dtype=np.float32))
    clips = np.array([[float(np.asarray(L0_ffn_clip_x).reshape(-1)[0]),
                       float(np.asarray(L0_ffn_clip_down).reshape(-1)[0])]],
                     dtype=np.float32)

    cos_hi, cos_lo, sin_hi, sin_lo = _rope_tables()
    tabs_base = np.ascontiguousarray(
        np.concatenate([cos_hi, cos_lo, sin_hi, sin_lo], axis=1))
    wqkv_p = np.ascontiguousarray(wqkv[:, _perm_qk()])
    eye16 = np.eye(128, dtype=BF)
    eye32 = np.eye(128, dtype=np.float32)
    sel = np.zeros((2, 128), np.float32)
    sel[0, 0:64] = 1.0
    sel[1, 64:128] = 1.0

    key = ("nc", bool(_debug))
    if key not in _NC_CACHE:
        _NC_CACHE[key] = build_nc(debug=_debug)
    ncobj = _NC_CACHE[key]

    in_maps = []
    for core in range(8):
        b, hf = core // 2, core % 2
        rot = hf * 2048
        xr = np.ascontiguousarray(np.roll(x[b], -rot, axis=0))
        tr = np.ascontiguousarray(np.roll(tabs_base, -rot, axis=0))
        in_maps.append({
            "x": xr, "wqkv": wqkv_p, "wproj": wproj, "bproj": bproj,
            "wmod": wmod, "cvec": np.ascontiguousarray(cv[b].reshape(NC, 128)),
            "clips": clips, "wg": wg, "wu": wu, "wd": wd,
            "tabs": tr, "eye16": eye16, "eye32": eye32, "sel": sel,
        })
    res = run_bass_kernel_spmd(ncobj, in_maps, core_ids=list(range(8)))
    out = np.empty((B, L, C), dtype=np.float32)
    for core in range(8):
        b, hf = core // 2, core % 2
        out[b, hf * 2048:(hf + 1) * 2048, :] = res.results[core]["y"]
    if _debug:
        return out, res
    return out


# version 06 (final)
# speedup vs baseline: 1.0976x; 1.0139x over previous
import numpy as np
import ml_dtypes
import concourse.bass as bass
import concourse.mybir as mybir
from concourse.bass_utils import run_bass_kernel_spmd

F32 = mybir.dt.float32
BF16 = mybir.dt.bfloat16
I32 = mybir.dt.int32
AF = mybir.ActivationFunctionType
OP = mybir.AluOpType
AX = mybir.AxisListType
BF = ml_dtypes.bfloat16

B, L, C = 4, 4096, 1024
H, D = 16, 64
NT = L // 128
NTO = 2048 // 128
FI = 2816
NI = FI // 128
NC = C // 128
QMAX = 127.0
EPS = 1e-5

ENGINES = ("sync", "tensor", "vector", "scalar", "pool")
_OWN = {"sync": "ld", "tensor": "tt", "vector": "tv", "scalar": "ta", "pool": "tp"}
SEMS = ("ld", "tt", "tv", "ta", "tp", "cc", "rd")


class Prog:
    def __init__(self):
        self.events = []
        self.cnt = {k: 0 for k in SEMS}
        self._gates = {}

    def now(self, sem):
        return self.cnt[sem]

    def _gate(self, owner, sem, v):
        key = (sem, v)
        if key in self._gates:
            return self._gates[key]
        full = self.cnt[sem]
        self.events.append((owner, lambda e: e.nop(), [(sem, full)], "rd", 1))
        self.cnt["rd"] += 1
        self._gates[key] = self.cnt["rd"]
        return self.cnt["rd"]

    def emit(self, eng, fn, waits=(), inc=None, inc_by=1):
        w = [(s, v) for (s, v) in waits if v > 0]
        w2 = []
        for s, v in w:
            if s == "ld" and eng != "sync":
                w2.append(("rd", self._gate("sync", "ld", v)))
            elif s == "tp" and eng != "pool":
                w2.append(("rd", self._gate("pool", "tp", v)))
            elif s == "ld" and eng == "sync":
                w2.append(("ld", self.cnt["ld"]))
            elif s == "tp" and eng == "pool":
                w2.append(("tp", self.cnt["tp"]))
            else:
                w2.append((s, v))
        w = w2
        if eng in ("vector", "scalar", "pool"):
            own = _OWN[eng]
            if self.cnt[own] > 0 and own != "tp":
                w.append((own, self.cnt[own]))
            elif own == "tp" and self.cnt[own] > 0:
                w.append((own, self.cnt[own]))
            if inc is None:
                inc = own
                inc_by = 1
        self.events.append((eng, fn, w, inc, inc_by))
        if inc is not None:
            self.cnt[inc] += inc_by


def build_nc(debug=False):
    nc = bass.Bass("TRN2", target_bir_lowering=False)

    dx = nc.dram_tensor("x", [L, C], F32, kind="ExternalInput").ap()
    dwqkv = nc.dram_tensor("wqkv", [C, 3 * C], F32, kind="ExternalInput").ap()
    dwproj = nc.dram_tensor("wproj", [C, C], F32, kind="ExternalInput").ap()
    dbproj = nc.dram_tensor("bproj", [1, C], F32, kind="ExternalInput").ap()
    dwmod = nc.dram_tensor("wmod", [C, 6 * C], F32, kind="ExternalInput").ap()
    dcvec = nc.dram_tensor("cvec", [NC, 128], F32, kind="ExternalInput").ap()
    dclip = nc.dram_tensor("clips", [1, 2], F32, kind="ExternalInput").ap()
    dwg = nc.dram_tensor("wg", [FI, C], F32, kind="ExternalInput").ap()
    dwu = nc.dram_tensor("wu", [FI, C], F32, kind="ExternalInput").ap()
    dwd = nc.dram_tensor("wd", [C, FI], F32, kind="ExternalInput").ap()
    dtab = nc.dram_tensor("tabs", [L, 2048], F32, kind="ExternalInput").ap()
    deye16 = nc.dram_tensor("eye16", [128, 128], BF16, kind="ExternalInput").ap()
    deye32 = nc.dram_tensor("eye32", [128, 128], F32, kind="ExternalInput").ap()
    dsel = nc.dram_tensor("sel", [2, 128], F32, kind="ExternalInput").ap()
    dy = nc.dram_tensor("y", [2048, C], F32, kind="ExternalOutput").ap()

    dbg = {}
    if debug:
        for nm, shp, dt_ in [("h1", [L, C], BF16), ("mod", [128, 48], F32),
                             ("kpost", [L, C], BF16), ("qpost", [2048, C], BF16),
                             ("attnout", [128, NC * 2048], BF16),
                             ("x1", [2048, C], F32), ("xq", [2048, C], BF16),
                             ("scal", [1, 16], F32)]:
            dbg[nm] = nc.dram_tensor("dbg_" + nm, shp, dt_, kind="ExternalOutput").ap()

    q_sp = nc.dram_tensor("q_sp", [NTO, 128, C], BF16, kind="Internal")
    k_sp = nc.dram_tensor("k_sp", [NT, 128, C], BF16, kind="Internal")
    v_sp = nc.dram_tensor("v_sp", [NT, 128, C], BF16, kind="Internal")
    p_sp = nc.dram_tensor("p_sp", [NI, 128, 2048], F32, kind="Internal")
    x1_sp = nc.dram_tensor("x1_sp", [NTO, 128, C], F32, kind="Internal")
    dl_sp = nc.dram_tensor("dl_sp", [NTO, 128, C], BF16, kind="Internal")
    wgT_sp = nc.dram_tensor("wgT_sp", [NI, 128, C], BF16, kind="Internal")
    wuT_sp = nc.dram_tensor("wuT_sp", [NI, 128, C], BF16, kind="Internal")
    wpj_sp = nc.dram_tensor("wpj_sp", [NC, 128, C], BF16, kind="Internal")
    rowb = nc.dram_tensor("rowb", [1, 4096], F32, kind="Internal")
    rowb_bf = nc.dram_tensor("rowb_bf", [1, 4096], BF16, kind="Internal")
    cc1_in = nc.dram_tensor("cc1_in", [1, 128], F32, kind="Internal")
    cc1_out = nc.dram_tensor("cc1_out", [1, 128], F32, kind="Internal")
    cc2_in = nc.dram_tensor("cc2_in", [1, 128], F32, kind="Internal")
    cc2_out = nc.dram_tensor("cc2_out", [1, 128], F32, kind="Internal")

    A = nc.alloc_sbuf_tensor
    AR1 = A("AR1", [128, 24576], BF16)
    AR2 = A("AR2", [128, 6144], F32)
    AR3 = A("AR3", [128, 16384], BF16)
    PSC = A("PSC", [128, 6656], F32)

    wqkv_s = AR1[:]
    kbuf = AR1[:, 0:8192]
    vbuf = AR1[:, 8192:16384]
    qbuf = AR1[:, 16384:20480]
    ptA = AR1[:, 20480:22528]
    ptB = AR1[:, 22528:24576]
    pqt = AR1[:, 0:5632]
    wgbuf = AR1[:, 11264:13312]
    wubuf = AR1[:, 13312:15360]
    wpjbuf = AR1[:, 15360:17408]
    wst = AR2[:, 0:6144]
    tbuf = AR2[:, 0:4096]
    ceff = AR2[:, 4096:6144]
    kfm = AR2.bitcast(BF16)[:, 0:4096]
    qfm = AR2.bitcast(BF16)[:, 4096:6144]
    drow = AR2[:, 3072:4096]
    rbt = AR2[:, 4096:4608]
    pstg_s = [AR2[:, 0:2816], AR2[:, 2816:5632]]
    wmb = AR3[:, 0:6144]
    attnout = AR3[:]
    xqfm = AR3[:]
    qkvbias_row = AR3[0:1, 8192:11264]
    rsc32 = PSC[:, 0:1024]
    rrow = PSC[0:1, 0:1024]
    h1t = PSC.bitcast(BF16)[:, 2048:4096]
    h1fm = PSC.bitcast(BF16)[:, 4096:6144]
    qsc = PSC.bitcast(BF16)[:, 6144:8192]
    ksc = PSC.bitcast(BF16)[:, 8192:10240]
    vsc = PSC.bitcast(BF16)[:, 10240:12288]
    wqi = PSC.bitcast(BF16)[:, 0:2816]
    wint = PSC.bitcast(I32)[:, 1536:4352]
    wint6 = PSC.bitcast(I32)[:, 2048:4864]
    qsc32 = PSC[:, 0:2048]
    h2sc = PSC[:, 2048:3072]
    qint = PSC.bitcast(I32)[:, 3072:4096]
    xqt = PSC.bitcast(BF16)[:, 8192:10240]
    silu_t = PSC[:, 5120:5632]
    p_t = PSC[:, 5632:6656]
    ysc = PSC[:, 0:2048]
    x1stg = PSC[:, 2048:4096]
    cv_s = A("cv_s", [NC, 128], F32)
    clip_s = A("clip_s", [1, 2], F32)
    sel_s = A("sel_s", [2, 128], F32)
    rrow2 = A("rrow2", [2, 512], F32)
    eye16_s = A("eye16_s", [128, 128], BF16)
    eye32_s = A("eye32_s", [128, 128], F32)
    ones1b = A("ones1b", [1, 128], BF16)
    ones1f = A("ones1f", [1, 128], F32)
    onesd = A("onesd", [128, 1], BF16)
    onesr = A("onesr", [128, 64], F32)
    silucf = A("silucf", [128, NC], BF16)
    silucf32 = A("silucf32", [128, NC], F32)
    mod_fm = A("mod_fm", [128, 48], F32)
    vec_fm = A("vec_fm", [128, 64], F32)
    sa_bf = A("sa_bf", [128, NC], BF16)
    bias_fm = A("bias_fm", [128, 24], F32)
    bias_fmb = A("bias_fmb", [128, 24], BF16)
    rowst = A("rowst", [32, 128], F32)
    rowtmp = A("rowtmp", [1, C], F32)
    bc0 = A("bc0", [128, C], F32)
    bc1 = A("bc1", [128, C], F32)
    wdT = A("wdT", [128, NI * C], BF16)
    xbuf = A("xbuf", [128, 2 * C], F32)
    stat_sum = A("stat_sum", [128, NT], F32)
    stat_sq = A("stat_sq", [128, NT], F32)
    stat_m = A("stat_m", [128, NT], F32)
    stat_r = A("stat_r", [128, NT], F32)
    dsc = A("dsc", [128, C], F32)
    wsc = A("wsc", [128, 16], F32)
    wgusc = A("wgusc", [128, 44], F32)
    scal = A("scal", [1, 16], F32)
    scal_p = A("scal_p", [128, 4], F32)
    pmax = A("pmax", [128, 2], F32)
    ccs = A("ccs", [1, 128], F32)

    psA = nc.alloc_psum_tensor("psA", [128, 1024], F32)
    psB = nc.alloc_psum_tensor("psB", [128, 1024], F32)
    psC = nc.alloc_psum_tensor("psC", [128, 1024], F32)
    psD = nc.alloc_psum_tensor("psD", [128, 512], F32)
    psF = nc.alloc_psum_tensor("psF", [128, 512], F32)
    psAb = psA[:].bitcast(BF16)

    prog = Prog()
    E = prog.emit
    now = prog.now

    def dma(out, in_):
        return lambda e: e.dma_start(out=out, in_=in_)

    E("sync", dma(cv_s[:], dcvec), inc="ld", inc_by=16)
    E("sync", dma(clip_s[:], dclip), inc="ld", inc_by=16)
    E("sync", dma(sel_s[:], dsel), inc="ld", inc_by=16)
    E("sync", dma(eye16_s[:], deye16), inc="ld", inc_by=16)
    E("sync", dma(eye32_s[:], deye32), inc="ld", inc_by=16)
    ld0 = now("ld")
    E("vector", lambda e: e.memset(ones1b[:], 1.0))
    E("vector", lambda e: e.memset(ones1f[:], 1.0))
    E("vector", lambda e: e.memset(onesd[:], 1.0))
    E("vector", lambda e: e.memset(onesr[:], 1.0))
    E("vector", lambda e: e.memset(stat_sum[:], 0.0))
    E("vector", lambda e: e.memset(stat_sq[:], 0.0))
    E("vector", lambda e: e.memset(pmax[:], 0.0))
    E("vector", lambda e: e.memset(scal[:], 0.0))
    E("vector", lambda e: e.memset(scal_p[:], 0.0), inc="tv")

    E("scalar", lambda e: e.activation(cv_s[:], cv_s[:], AF.Silu),
      waits=[("ld", ld0)], inc="ta")
    E("tensor", lambda e: e.transpose(psD[0:128, 0:NC], cv_s[:], eye32_s[0:NC, 0:NC]),
      waits=[("ta", now("ta"))], inc="tt")
    E("vector", lambda e: e.tensor_copy(silucf32[:], psD[0:128, 0:NC]),
      waits=[("tt", now("tt"))])
    E("vector", lambda e: e.tensor_copy(silucf[:], psD[0:128, 0:NC]),
      inc="tv")

    wmod_sl = [wst, AR3.bitcast(F32)[:, 0:6144]]
    mm_tt = {}
    for f in range(NC):
        wsl = wmod_sl[f % 2]
        E("sync", dma(wsl, dwmod[f * 128:(f + 1) * 128, :]),
          waits=[("tv", now("tv") if f == 0 else 0), ("tt", mm_tt.get(f - 2, 0))],
          inc="ld", inc_by=16)
        ld_f = now("ld")
        for j in range(48):
            E("tensor",
              lambda e, f=f, j=j, wsl=wsl: e.matmul(
                  psC[:, j:j + 1], wsl[:, j * 128:(j + 1) * 128],
                  silucf32[:, f:f + 1], start=True, stop=True),
              waits=[("tv", now("tv")), ("ld", ld_f)] if j == 0 else (),
              inc="tt" if j == 47 else None)
        mm_tt[f] = now("tt")
        if f == 0:
            E("vector", lambda e: e.tensor_copy(mod_fm[:], psC[:, 0:48]),
              waits=[("tt", now("tt"))], inc="tv")
        else:
            E("vector", lambda e: e.tensor_tensor(mod_fm[:], mod_fm[:],
                                                  psC[:, 0:48], OP.add),
              waits=[("tt", now("tt"))], inc="tv")
    if debug:
        E("sync", dma(dbg["mod"], mod_fm[:]), waits=[("tv", now("tv"))],
          inc="ld", inc_by=16)

    E("vector", lambda e: e.tensor_single_scalar(vec_fm[:, 0:8], mod_fm[:, 8:16], 1.0, OP.add))
    E("vector", lambda e: e.reciprocal(vec_fm[:, 8:16], vec_fm[:, 0:8]))
    E("vector", lambda e: e.tensor_tensor(vec_fm[:, 8:16], vec_fm[:, 8:16], mod_fm[:, 0:8], OP.mult))
    E("vector", lambda e: e.tensor_copy(sa_bf[:], vec_fm[:, 8:16]))
    for dst, src in [(16, 16), (40, 40)]:
        E("vector", lambda e, dst=dst, src=src: e.tensor_single_scalar(
            vec_fm[:, dst:dst + 8], mod_fm[:, src:src + 8], -1.0, OP.mult))
        E("vector", lambda e, dst=dst, src=src: e.tensor_tensor(
            vec_fm[:, dst:dst + 8], vec_fm[:, dst:dst + 8], mod_fm[:, src:src + 8], OP.max))
        E("vector", lambda e, dst=dst: e.tensor_single_scalar(
            vec_fm[:, dst:dst + 8], vec_fm[:, dst:dst + 8], 0.25, OP.max))
        E("vector", lambda e, src=src: e.tensor_scalar(
            vec_fm[:, 56:64], mod_fm[:, src:src + 8], 0.0, 2.0, OP.is_ge, OP.mult))
        E("vector", lambda e: e.tensor_single_scalar(
            vec_fm[:, 56:64], vec_fm[:, 56:64], -1.0, OP.add))
        E("vector", lambda e, dst=dst: e.tensor_tensor(
            vec_fm[:, dst:dst + 8], vec_fm[:, dst:dst + 8], vec_fm[:, 56:64], OP.mult))
    E("vector", lambda e: e.tensor_single_scalar(vec_fm[:, 24:32], mod_fm[:, 32:40], 1.0, OP.add))
    E("vector", lambda e: e.tensor_copy(vec_fm[:, 32:40], mod_fm[:, 24:32]), inc="tv")
    vec_tv = now("tv")

    E("tensor", lambda e: e.transpose(psD[0:32, 0:128], vec_fm[:, 16:48], eye32_s[:]),
      waits=[("tv", vec_tv)], inc="tt")
    E("vector", lambda e: e.tensor_copy(rowst[:], psD[0:32, 0:128]),
      waits=[("tt", now("tt"))], inc="tv")
    E("sync", dma(rowb[0:1, 0:4096], rowst[:]), waits=[("tv", now("tv"))],
      inc="ld", inc_by=16)
    rows_ld = now("ld")

    qk_tv = {}
    for f in range(NC):
        wsl = wst[:, (f % 2) * 3 * C:(f % 2 + 1) * 3 * C]
        E("sync", dma(wsl, dwqkv[f * 128:(f + 1) * 128, :]),
          waits=[("tv", qk_tv.get(f - 2, 0)),
                 ("tt", mm_tt[NC - 1] if f < 2 else 0)],
          inc="ld", inc_by=16)
        E("vector",
          lambda e, f=f, wsl=wsl: e.tensor_scalar(
              wqkv_s[:, f * 3 * C:(f + 1) * 3 * C], wsl, vec_fm[:, f:f + 1],
              None, OP.mult),
          waits=[("ld", now("ld"))], inc="tv")
        qk_tv[f] = now("tv")
    wqkv_tv = now("tv")

    for f in range(NC):
        for j in range(24):
            E("tensor",
              lambda e, j=j, f=f: e.matmul(
                  psD[:, 64 + j:65 + j],
                  wqkv_s[:, f * 3 * C + j * 128:f * 3 * C + (j + 1) * 128],
                  sa_bf[:, f:f + 1], start=True, stop=True),
              waits=[("tv", now("tv"))] if j == 0 else (),
              inc="tt" if j == 23 else None)
        if f == 0:
            E("vector", lambda e: e.tensor_copy(bias_fm[:], psD[:, 64:88]),
              waits=[("tt", now("tt"))], inc="tv")
        else:
            E("vector", lambda e: e.tensor_tensor(bias_fm[:], bias_fm[:],
                                                  psD[:, 64:88], OP.add),
              waits=[("tt", now("tt"))], inc="tv")
    E("vector", lambda e: e.tensor_copy(bias_fmb[:], bias_fm[:]), inc="tv")
    E("tensor", lambda e: e.transpose(psAb[0:24, 0:128], bias_fmb[:], eye16_s[:]),
      waits=[("tv", now("tv"))], inc="tt")
    E("vector", lambda e: e.tensor_copy(rowst.bitcast(BF16)[0:24, 0:128],
                                        psAb[0:24, 0:128]),
      waits=[("tt", now("tt"))], inc="tv")
    E("sync", dma(rowb_bf[0:1, 0:3072], rowst.bitcast(BF16)[0:24, 0:128]),
      waits=[("tv", now("tv"))], inc="ld", inc_by=16)
    E("sync", dma(qkvbias_row[:], rowb_bf[0:1, 0:3 * C]), waits=[("ld", now("ld"))],
      inc="ld", inc_by=16)

    pj_tv = {}
    for f in range(NC):
        wsl = wst[:, (f % 2) * 2816:(f % 2) * 2816 + C]
        E("sync", dma(wsl, dwproj[f * 128:(f + 1) * 128, :]),
          waits=[("tv", pj_tv.get(f - 2, wqkv_tv if f < 2 else 0))],
          inc="ld", inc_by=16)
        E("vector", lambda e, wsl=wsl: e.tensor_copy(wqi[:, 0:C], wsl),
          waits=[("ld", now("ld"))], inc="tv")
        pj_tv[f] = now("tv")
        E("sync", dma(wpj_sp[f], wqi[:, 0:C]), waits=[("tv", now("tv"))],
          inc="ld", inc_by=16)
    wpj_spill_ld = now("ld")
    pj_base_tv = now("tv")

    def quant_chunk(wsl, width, prev_tt, keep_scale_col=None, keep_tensor=None):
        E("vector", lambda e, wsl=wsl, width=width: e.tensor_reduce(
            wsc[:, 0:1], wsl[:, 0:width], AX.X, OP.max, apply_absolute_value=True),
          waits=[("ld", now("ld")), ("tt", prev_tt)])
        E("vector", lambda e: e.reciprocal(wsc[:, 1:2], wsc[:, 0:1]))
        E("vector", lambda e: e.tensor_single_scalar(wsc[:, 1:2], wsc[:, 1:2], QMAX, OP.mult))
        if keep_scale_col is not None:
            kt = wsc if keep_tensor is None else keep_tensor
            E("vector", lambda e, k=keep_scale_col, kt=kt: e.tensor_single_scalar(
                kt[:, k:k + 1], wsc[:, 0:1], 1.0 / QMAX, OP.mult))
        E("vector", lambda e, wsl=wsl, width=width: e.tensor_scalar(
            wint[:, 0:width], wsl[:, 0:width], wsc[:, 1:2], None, OP.mult))
        E("vector", lambda e, width=width: e.tensor_copy(wqi[:, 0:width],
                                                         wint[:, 0:width]), inc="tv")

    wgu_spill_ld = 0
    wsl2 = [wst[:, 0:2816], wst[:, 2816:5632]]
    q_tv = {}
    q_tt = {}
    sidx = 0
    pending_spill = None
    for wi, (src, spT) in enumerate([(dwg, wgT_sp), (dwu, wuT_sp)]):
        for cch in range(NI):
            wsl = wsl2[sidx % 2]
            E("sync", dma(wsl[:, 0:C], src[cch * 128:(cch + 1) * 128, :]),
              waits=[("tv", q_tv.get(sidx - 2, pj_base_tv if sidx < 2 else 0))],
              inc="ld", inc_by=16)
            if pending_spill is not None:
                pspT, pcch, ptv = pending_spill
                E("sync", dma(pspT[pcch], ksc[:, 0:C]), waits=[("tv", ptv)],
                  inc="ld", inc_by=16)
                wgu_spill_ld = now("ld")
            quant_chunk(wsl, C, q_tt.get(sidx - 1, 0),
                        keep_scale_col=wi * NI + cch, keep_tensor=wgusc)
            q_tv[sidx] = now("tv")
            for t in range(NC):
                E("tensor",
                  lambda e, t=t: e.transpose(psAb[:, t * 128:(t + 1) * 128],
                                             wqi[:, t * 128:(t + 1) * 128],
                                             eye16_s[:]),
                  waits=[("tv", now("tv"))] if t == 0 else (),
                  inc="tt" if t == NC - 1 else None)
            q_tt[sidx] = now("tt")
            E("vector", lambda e: e.tensor_copy(ksc[:, 0:C], psAb[:, 0:1024]),
              waits=[("tt", now("tt"))], inc="tv")
            pending_spill = (spT, cch, now("tv"))
            sidx += 1
    pspT, pcch, ptv = pending_spill
    E("sync", dma(pspT[pcch], ksc[:, 0:C]), waits=[("tv", ptv)],
      inc="ld", inc_by=16)
    wgu_spill_ld = now("ld")
    wdT3 = wdT[:].rearrange("p (cc hc) -> p cc hc", cc=NI)
    for hh in range(NC):
        wsl = wsl2[sidx % 2]
        E("sync", dma(wsl[:, 0:FI], dwd[hh * 128:(hh + 1) * 128, :]),
          waits=[("tv", q_tv.get(sidx - 2, 0))], inc="ld", inc_by=16)
        quant_chunk(wsl, FI, q_tt.get(sidx - 1, 0), keep_scale_col=2 + hh)
        q_tv[sidx] = now("tv")
        for grp in range(3):
            lo = grp * 8
            n = min(8, NI - lo)
            for u in range(n):
                E("tensor",
                  lambda e, lo=lo, u=u: e.transpose(
                      psAb[:, u * 128:(u + 1) * 128],
                      wqi[:, (lo + u) * 128:(lo + u + 1) * 128], eye16_s[:]),
                  waits=[("tv", now("tv"))] if (grp == 0 and u == 0) else
                        ([("tv", now("tv"))] if u == 0 else ()),
                  inc="tt" if u == n - 1 else None)
            E("vector",
              lambda e, lo=lo, n=n, hh=hh: e.tensor_copy(
                  wdT3[:, lo:lo + n, hh * 128:(hh + 1) * 128],
                  psAb[:, 0:n * 128].rearrange("p (a b) -> p a b", a=n)),
              waits=[("tt", now("tt"))], inc="tv")
        q_tt[sidx] = now("tt")
        sidx += 1
    setup_tv = now("tv")

    r_inv = 1.0 / C
    qkv_ps = {0: psA, 1: psA, 2: psB, 3: psB, 4: psC, 5: psC}
    ev_tv = {}
    sp_ld = {}
    xload_ld = {}
    apply_tv = {}
    tadd_tv = {}
    h1T_tt = {}
    h1ev_ta = {}
    qkv_tt = {}
    last_spill_ld = 0

    for t in range(NT):
        our = t < NTO
        bi = t % 2
        E("sync", dma(xbuf[:, bi * C:(bi + 1) * C], dx[t * 128:(t + 1) * 128, :]),
          waits=[("tv", apply_tv.get(t - 2, 0)), ("ta", h1ev_ta.get(t - 2, 0))],
          inc="ld", inc_by=16)
        E("sync", dma(tbuf[:, bi * 2048:(bi + 1) * 2048],
                      dtab[t * 128:(t + 1) * 128, :]),
          waits=[("tv", max(tadd_tv.get(t - 2, 0), setup_tv if t < 2 else 0))],
          inc="ld", inc_by=16)
        xload_ld[t] = now("ld")
        E("scalar", lambda e, bi=bi, t=t: e.activation(
            dsc[:], xbuf[:, bi * C:(bi + 1) * C], AF.Identity,
            accum_out=stat_sum[:, t:t + 1]),
          waits=[("ld", xload_ld[t]), ("tv", setup_tv if t == 0 else 0)])
        E("scalar", lambda e, bi=bi, t=t: e.activation(
            dsc[:], xbuf[:, bi * C:(bi + 1) * C], AF.Square,
            accum_out=stat_sq[:, t:t + 1]), inc="ta")
        E("vector", lambda e, t=t: e.tensor_single_scalar(
            stat_m[:, t:t + 1], stat_sum[:, t:t + 1], r_inv, OP.mult),
          waits=[("ta", now("ta"))])
        E("vector", lambda e, t=t: e.tensor_single_scalar(
            stat_sq[:, t:t + 1], stat_sq[:, t:t + 1], r_inv, OP.mult))
        E("vector", lambda e, t=t: e.tensor_tensor(
            stat_sum[:, t:t + 1], stat_m[:, t:t + 1], stat_m[:, t:t + 1], OP.mult))
        E("vector", lambda e, t=t: e.scalar_tensor_tensor(
            stat_sq[:, t:t + 1], stat_sq[:, t:t + 1], EPS, stat_sum[:, t:t + 1],
            OP.add, OP.subtract))
        E("vector", lambda e, t=t: e.reciprocal(stat_sum[:, t:t + 1],
                                                stat_sq[:, t:t + 1]), inc="tv")
        E("scalar", lambda e, t=t: e.activation(stat_r[:, t:t + 1],
                                                stat_sum[:, t:t + 1], AF.Sqrt),
          waits=[("tv", now("tv"))], inc="ta")
        E("vector", lambda e, bi=bi, t=t: e.tensor_scalar(
            h1t[:, bi * C:(bi + 1) * C], xbuf[:, bi * C:(bi + 1) * C],
            stat_m[:, t:t + 1], stat_r[:, t:t + 1], OP.subtract, OP.mult),
          waits=[("ta", now("ta")), ("tt", h1T_tt.get(t - 2, 0))], inc="tv")
        apply_tv[t] = now("tv")
        if debug:
            E("sync", dma(dbg["h1"][t * 128:(t + 1) * 128, :],
                          h1t[:, bi * C:(bi + 1) * C]),
              waits=[("tv", now("tv"))], inc="ld", inc_by=16)
        for f in range(NC):
            E("tensor", lambda e, f=f, bi=bi: e.transpose(
                psAb[:, f * 128:(f + 1) * 128],
                h1t[:, bi * C + f * 128:bi * C + (f + 1) * 128], eye16_s[:]),
              waits=([("tv", now("tv")), ("ta", h1ev_ta.get(t - 1, 0))]
                     if f == 0 else ()),
              inc="tt" if f == NC - 1 else None)
        h1T_tt[t] = now("tt")
        E("scalar", lambda e, bi=bi: e.activation(h1fm[:, bi * C:(bi + 1) * C],
                                                  psAb[:, 0:1024], AF.Identity),
          waits=[("tt", now("tt")), ("tt", qkv_tt.get((t - 2, 2), 0))], inc="ta")
        h1ev_ta[t] = now("ta")

        slices = list(range(6)) if our else [2, 3, 4, 5]
        for sl in slices:
            ps = qkv_ps[sl]
            col = (sl % 2) * 512
            grp = sl // 2
            w = [("ta", h1ev_ta[t])]
            if (t - 1, grp) in ev_tv:
                w.append(("tv", ev_tv[(t - 1, grp)]))
            elif grp == 0 and (NTO - 1, 0) in ev_tv:
                w.append(("tv", ev_tv[(NTO - 1, 0)]))
            E("tensor", lambda e, ps=ps, col=col, sl=sl: e.matmul(
                ps[:, col:col + 512], ones1b[:],
                qkvbias_row[0:1, sl * 512:(sl + 1) * 512], start=True, stop=False),
              waits=w)
            for f in range(NC):
                E("tensor", lambda e, ps=ps, col=col, sl=sl, f=f, bi=bi: e.matmul(
                    ps[:, col:col + 512],
                    h1fm[:, bi * C + f * 128:bi * C + (f + 1) * 128],
                    wqkv_s[:, f * 3 * C + sl * 512:f * 3 * C + (sl + 1) * 512],
                    start=False, stop=(f == NC - 1)),
                  inc="tt" if (f == NC - 1 and sl % 2 == 1) else None)
            if sl % 2 == 1:
                qkv_tt[(t, grp)] = now("tt")

        E("vector", lambda e, bi=bi: e.tensor_tensor(
            ceff[:, bi * 1024:bi * 1024 + 512],
            tbuf[:, bi * 2048:bi * 2048 + 512],
            tbuf[:, bi * 2048 + 512:bi * 2048 + 1024], OP.add))
        E("vector", lambda e, bi=bi: e.tensor_tensor(
            ceff[:, bi * 1024 + 512:bi * 1024 + 1024],
            tbuf[:, bi * 2048 + 1024:bi * 2048 + 1536],
            tbuf[:, bi * 2048 + 1536:bi * 2048 + 2048], OP.add), inc="tv")
        tadd_tv[t] = now("tv")

        def rope(psrc, dst, grp):
            co = bi * 1024
            do = bi * C
            re_v = psrc[:].rearrange("p (h s i) -> p h s i", h=16, s=2)[:, :, 0, :]
            im_v = psrc[:].rearrange("p (h s i) -> p h s i", h=16, s=2)[:, :, 1, :]
            cosv = ceff[:, co:co + 512].rearrange("p (h i) -> p h i", h=16)
            sinv = ceff[:, co + 512:co + 1024].rearrange("p (h i) -> p h i", h=16)
            o_re = dst[:, do:do + C].rearrange("p (h s i) -> p h s i", h=16, s=2)[:, :, 0, :]
            o_im = dst[:, do:do + C].rearrange("p (h s i) -> p h s i", h=16, s=2)[:, :, 1, :]
            sc_a = rsc32[:, 0:512].rearrange("p (h i) -> p h i", h=16)
            sc_b = rsc32[:, 512:1024].rearrange("p (h i) -> p h i", h=16)
            w0 = [("tt", qkv_tt[(t, grp)]), ("tv", now("tv")),
                  ("ld", sp_ld.get((t - 2, grp), 0))]
            E("vector", lambda e: e.tensor_tensor(sc_a, re_v, cosv, OP.mult), waits=w0)
            E("vector", lambda e: e.tensor_tensor(sc_b, im_v, sinv, OP.mult))
            E("vector", lambda e: e.tensor_tensor(o_re, sc_a, sc_b, OP.subtract))
            E("vector", lambda e: e.tensor_tensor(sc_a, re_v, sinv, OP.mult))
            E("vector", lambda e: e.tensor_tensor(sc_b, im_v, cosv, OP.mult))
            E("vector", lambda e: e.tensor_tensor(o_im, sc_a, sc_b, OP.add), inc="tv")

        if our:
            rope(psA, qsc, 0)
            ev_tv[(t, 0)] = now("tv")
            E("sync", dma(q_sp[t], qsc[:, bi * C:(bi + 1) * C]),
              waits=[("tv", now("tv"))], inc="ld", inc_by=16)
            sp_ld[(t, 0)] = now("ld")
            if debug:
                E("sync", dma(dbg["qpost"][t * 128:(t + 1) * 128, :],
                              qsc[:, bi * C:(bi + 1) * C]), inc="ld", inc_by=16)
        rope(psB, ksc, 1)
        ev_tv[(t, 1)] = now("tv")
        E("sync", dma(k_sp[t], ksc[:, bi * C:(bi + 1) * C]),
          waits=[("tv", now("tv"))], inc="ld", inc_by=16)
        sp_ld[(t, 1)] = now("ld")
        if debug:
            E("sync", dma(dbg["kpost"][t * 128:(t + 1) * 128, :],
                          ksc[:, bi * C:(bi + 1) * C]), inc="ld", inc_by=16)
        E("vector", lambda e, bi=bi: e.tensor_copy(vsc[:, bi * C:(bi + 1) * C], psC[:]),
          waits=[("tt", qkv_tt[(t, 2)]), ("ld", sp_ld.get((t - 2, 2), 0))], inc="tv")
        ev_tv[(t, 2)] = now("tv")
        E("sync", dma(v_sp[t], vsc[:, bi * C:(bi + 1) * C]),
          waits=[("tv", now("tv"))], inc="ld", inc_by=16)
        sp_ld[(t, 2)] = now("ld")
        last_spill_ld = now("ld")

    abuf_ld = {}
    hp_last_tv = {}
    tpose_ta = {}
    attn_tv = 0
    gate_drow_ld = 0
    gate_rrow_tt = 0
    gate_rbt_tv = 0
    rb_ta = 0
    for hp in range(8):
        bi = hp % 2
        wv = [("tv", hp_last_tv.get(hp - 2, 0)), ("ld", last_spill_ld)]
        E("sync", dma(
            kbuf[:, bi * 4096:(bi + 1) * 4096].rearrange("p (t c) -> p t c", t=NT),
            k_sp.ap().rearrange("t p c -> p t c")[:, :, hp * 128:(hp + 1) * 128]),
          waits=wv, inc="ld", inc_by=16)
        E("sync", dma(
            vbuf[:, bi * 4096:(bi + 1) * 4096].rearrange("p (t c) -> p t c", t=NT),
            v_sp.ap().rearrange("t p c -> p t c")[:, :, hp * 128:(hp + 1) * 128]),
          inc="ld", inc_by=16)
        E("sync", dma(
            qbuf[:, bi * 2048:(bi + 1) * 2048].rearrange("p (t c) -> p t c", t=NTO),
            q_sp.ap().rearrange("t p c -> p t c")[:, :, hp * 128:(hp + 1) * 128]),
          inc="ld", inc_by=16)
        abuf_ld[hp] = now("ld")

        for grp in range(6):
            if grp < 2:
                srcb, soff, dstt, g = qbuf, bi * 2048, qfm, grp
            else:
                srcb, soff, dstt, g = kbuf, bi * 4096, kfm, grp - 2
            for u in range(8):
                idx = g * 8 + u
                E("tensor", lambda e, srcb=srcb, soff=soff, idx=idx, u=u:
                  e.transpose(psAb[:, u * 128:(u + 1) * 128],
                              srcb[:, soff + idx * 128:soff + (idx + 1) * 128],
                              eye16_s[:]),
                  waits=([("ld", abuf_ld[hp]), ("ta", now("ta")),
                          ("tv", hp_last_tv.get(hp - 1, 0))] if u == 0 else ()),
                  inc="tt" if u == 7 else None)
            E("scalar", lambda e, dstt=dstt, g=g: e.activation(
                dstt[:, g * 1024:(g + 1) * 1024], psAb[:, 0:1024], AF.Identity),
              waits=[("tt", now("tt"))], inc="ta")
        tpose_ta[hp] = now("ta")

        for qb in range(4):
            st_tt = {}
            exp_ta = {}
            pv_tt = {}
            ev_done_tv = attn_tv
            for j in range(16):
                for g2 in range(2):
                    kc = j * 2 + g2
                    for hi, (ps, rl) in enumerate([(psA, 0), (psB, 64)]):
                        w = []
                        if g2 == 0 and hi == 0:
                            w = [("ta", tpose_ta[hp]),
                                 ("ta", exp_ta.get((0, j - 1), 0)),
                                 ("ta", exp_ta.get((1, j - 1), 0))]
                        E("tensor", lambda e, ps=ps, rl=rl, kc=kc, qb=qb, g2=g2:
                          e.matmul(ps[:, g2 * 512:(g2 + 1) * 512],
                                   kfm[rl:rl + 64, kc * 128:(kc + 1) * 128],
                                   qfm[rl:rl + 64, qb * 512:(qb + 1) * 512],
                                   start=True, stop=True, tile_position=(rl, 0)),
                          waits=w,
                          inc="tt" if g2 == 1 else None)
                        st_tt[(hi, kc)] = now("tt")
                for hi, (ps, pt) in enumerate([(psA, ptA), (psB, ptB)]):
                    w = [("tt", st_tt[(hi, j * 2 + 1)])]
                    if (hi, j - 2) in pv_tt:
                        w.append(("tt", pv_tt[(hi, j - 2)]))
                    E("scalar", lambda e, ps=ps, pt=pt, j=j: e.activation(
                        pt[:, (j % 2) * 1024:(j % 2 + 1) * 1024], ps[:],
                        AF.Exp, scale=0.125),
                      waits=w, inc="ta")
                    exp_ta[(hi, j)] = now("ta")
                for g2 in range(2):
                    kc = j * 2 + g2
                    for hi, pt in enumerate([ptA, ptB]):
                        cl = hi * 64
                        pv_dst = psC[0:64, 0:512] if hi == 0 else psC[64:128, 512:1024]
                        dn_dst = psD[0:1, 0:512] if hi == 0 else psF[64:65, 0:512]
                        ptap = pt[:, (j % 2) * 1024 + g2 * 512:
                                  (j % 2) * 1024 + (g2 + 1) * 512]
                        w = []
                        if g2 == 0:
                            w = [("ta", exp_ta[(hi, j)])]
                        if kc == 0:
                            w.append(("tv", ev_done_tv))
                            w.append(("ta", rb_ta))
                        E("tensor", lambda e, ptap=ptap, kc=kc, cl=cl, bi=bi, hi=hi,
                          pv_dst=pv_dst:
                          e.matmul(pv_dst,
                                   vbuf[:, bi * 4096 + kc * 128 + hi * 64:
                                        bi * 4096 + kc * 128 + hi * 64 + 64],
                                   ptap, start=(kc == 0), stop=(kc == NT - 1),
                                   tile_position=(0, cl)),
                          waits=w, inc=None)
                        E("tensor", lambda e, ptap=ptap, kc=kc, cl=cl, dn_dst=dn_dst:
                          e.matmul(dn_dst, onesd[:], ptap,
                                   start=(kc == 0), stop=(kc == NT - 1),
                                   tile_position=(0, cl)),
                          inc="tt" if g2 == 1 else None)
                        if g2 == 1:
                            pv_tt[(hi, j)] = now("tt")
            pv_done_tt = now("tt")
            E("scalar", lambda e: e.activation(drow[0:1, 0:512], psD[0:1, 0:512],
                                               AF.Identity),
              waits=[("tt", pv_done_tt), ("tt", gate_drow_ld)])
            E("scalar", lambda e: e.activation(drow[64:65, 512:1024],
                                               psF[64:65, 0:512], AF.Identity),
              inc="ta")
            dr_ta = now("ta")
            E("vector", lambda e: e.reciprocal(drow[0:1, 0:512], drow[0:1, 0:512]),
              waits=[("ta", dr_ta)])
            E("vector", lambda e: e.reciprocal(drow[64:65, 512:1024],
                                               drow[64:65, 512:1024]), inc="tv")
            E("tensor", lambda e: e.matmul(psD[0:64, 0:512], onesr[0:1, :],
                                           drow[0:1, 0:512],
                                           start=True, stop=True),
              waits=[("tv", now("tv"))])
            E("tensor", lambda e: e.matmul(psF[64:128, 0:512], onesr[64:65, :],
                                           drow[64:65, 512:1024],
                                           start=True, stop=True,
                                           tile_position=(64, 64)), inc="tt")
            gate_drow_ld = now("tt")
            E("scalar", lambda e: e.activation(rbt[0:64, :], psD[0:64, 0:512],
                                               AF.Identity),
              waits=[("tt", now("tt")), ("tv", gate_rbt_tv)])
            E("scalar", lambda e: e.activation(rbt[64:128, :], psF[64:128, 0:512],
                                               AF.Identity), inc="ta")
            rb_ta = now("ta")
            ao = hp * 2048 + qb * 512
            E("vector", lambda e, ao=ao: e.tensor_tensor(
                attnout[0:64, ao:ao + 512], psC[0:64, 0:512], rbt[0:64, :], OP.mult),
              waits=[("ta", now("ta"))])
            E("vector", lambda e, ao=ao: e.tensor_tensor(
                attnout[64:128, ao:ao + 512], psC[64:128, 512:1024],
                rbt[64:128, :], OP.mult), inc="tv")
            attn_tv = now("tv")
            gate_rbt_tv = now("tv")
        hp_last_tv[hp] = now("tv")
    if debug:
        for f in range(NC):
            E("sync", dma(dbg["attnout"][:, f * 2048:(f + 1) * 2048],
                          attnout[:, f * 2048:(f + 1) * 2048]),
              waits=[("tv", attn_tv)], inc="ld", inc_by=16)

    for tile, off in [(bc0, 0), (bc1, -1)]:
        rsrc = dbproj if off < 0 else rowb[0:1, off:off + C]
        E("sync", dma(rowtmp[:], rsrc),
          waits=[("ld", rows_ld), ("tt", now("tt"))], inc="ld", inc_by=16)
        E("tensor", lambda e: e.matmul(psA[:, 0:512], ones1f[:], rowtmp[0:1, 0:512],
                                       start=True, stop=True),
          waits=[("ld", now("ld")), ("ta", now("ta")),
                 ("tv", max(attn_tv, now("tv")))])
        E("tensor", lambda e: e.matmul(psA[:, 512:1024], ones1f[:],
                                       rowtmp[0:1, 512:1024], start=True, stop=True),
          inc="tt")
        E("vector", lambda e, tile=tile: e.tensor_copy(tile[:], psA[:]),
          waits=[("tt", now("tt"))], inc="tv")
    bc_tv = now("tv")

    pj_ld = {}
    pj_ev_ta = {}
    dl_sp_ld = {}
    for fo in range(NC):
        bi = fo % 2
        E("sync", dma(wpjbuf[:, bi * C:(bi + 1) * C].rearrange(
            "p (f c) -> p f c", f=NC),
            wpj_sp.ap().rearrange("f p c -> p f c")[:, :, fo * 128:(fo + 1) * 128]),
          waits=[("ld", wpj_spill_ld), ("tt", pj_ld.get(fo - 2, 0)),
                 ("tv", attn_tv if fo < 2 else 0)],
          inc="ld", inc_by=16)
        wld = now("ld")
        for sl in range(4):
            ps = (psA, psA, psB, psB)[sl]
            col = (sl % 2) * 512
            for f in range(NC):
                w = []
                if sl == 0 and f == 0:
                    w = [("ld", wld), ("tv", bc_tv), ("ta", pj_ev_ta.get(fo - 1, 0))]
                E("tensor", lambda e, ps=ps, col=col, sl=sl, f=f, fo=fo, bi=bi:
                  e.matmul(ps[:, col:col + 512],
                           wpjbuf[:, bi * C + f * 128:bi * C + (f + 1) * 128],
                           attnout[:, f * 2048 + sl * 512:f * 2048 + (sl + 1) * 512],
                           start=(f == 0), stop=(f == NC - 1)),
                  waits=w, inc="tt" if f == NC - 1 else None)
        pj_ld[fo] = now("tt")
        E("scalar", lambda e: e.activation(ksc[:, 0:C], psA[:], AF.Identity),
          waits=[("tt", now("tt"))])
        E("scalar", lambda e: e.activation(ksc[:, C:2 * C], psB[:], AF.Identity),
          inc="ta")
        ev_ta = now("ta")
        for grp in range(2):
            for u in range(8):
                tt_i = grp * 8 + u
                E("tensor", lambda e, tt_i=tt_i, u=u: e.transpose(
                    psAb[:, u * 128:(u + 1) * 128],
                    ksc[:, tt_i * 128:(tt_i + 1) * 128], eye16_s[:]),
                  waits=([("ta", ev_ta), ("ta", pj_ev_ta.get((fo, grp - 1),
                                                            pj_ev_ta.get((fo - 1, 1), 0)))]
                         if u == 0 else ()),
                  inc="tt" if u == 7 else None)
            E("scalar", lambda e, grp=grp: e.activation(qsc[:, grp * C:(grp + 1) * C],
                                                        psAb[:, 0:1024], AF.Identity),
              waits=[("tt", now("tt")), ("ld", dl_sp_ld.get((fo - 1, grp), 0))],
              inc="ta")
            pj_ev_ta[(fo, grp)] = now("ta")
            E("sync", dma(
                bass.AP(dl_sp, grp * 8 * 128 * 1024 + fo * 128,
                        [[1024, 128], [131072, 8], [1, 128]]),
                qsc[:, grp * C:(grp + 1) * C].rearrange("p (a b) -> p a b", a=8)),
              waits=[("ta", now("ta"))], inc="ld", inc_by=16)
            dl_sp_ld[(fo, grp)] = now("ld")
    dl_done_ld = now("ld")
    pj_ev_ta_last = now("ta")

    x1_tv = {}
    x1_ld = {}
    xq_tt = {}
    for t in range(NTO):
        bi = t % 2
        E("sync", dma(xbuf[:, bi * C:(bi + 1) * C], dx[t * 128:(t + 1) * 128, :]),
          waits=[("tv", x1_tv.get(t - 2, 0))], inc="ld", inc_by=16)
        E("sync", dma(ksc[:, bi * C:(bi + 1) * C], dl_sp[t]),
          waits=[("ld", dl_done_ld), ("tv", x1_tv.get(t - 2, 0)),
                 ("ta", pj_ev_ta_last if t < 2 else 0)],
          inc="ld", inc_by=16)
        ld_t = now("ld")
        E("vector", lambda e, bi=bi: e.tensor_tensor(
            dsc[:], ksc[:, bi * C:(bi + 1) * C], bc1[:], OP.add),
          waits=[("ld", ld_t)])
        E("vector", lambda e: e.tensor_tensor(dsc[:], dsc[:], bc0[:], OP.mult))
        E("vector", lambda e, t=t, bi=bi: e.tensor_tensor(
            x1stg[:, bi * C:(bi + 1) * C], dsc[:], xbuf[:, bi * C:(bi + 1) * C],
            OP.add), inc="tv")
        x1_tv[t] = now("tv")
        E("sync", dma(x1_sp[t], x1stg[:, bi * C:(bi + 1) * C]),
          waits=[("tv", now("tv"))], inc="ld", inc_by=16)
        x1_ld[t] = now("ld")
        if debug:
            E("sync", dma(dbg["x1"][t * 128:(t + 1) * 128, :],
                          x1stg[:, bi * C:(bi + 1) * C]), inc="ld", inc_by=16)
    x1_done_ld = now("ld")

    for tile, off in [(bc0, C), (bc1, 2 * C)]:
        E("sync", dma(rowtmp[:], rowb[0:1, off:off + C]),
          waits=[("tt", now("tt"))], inc="ld", inc_by=16)
        E("tensor", lambda e: e.matmul(psA[:, 0:512], ones1f[:], rowtmp[0:1, 0:512],
                                       start=True, stop=True),
          waits=[("ld", now("ld")), ("tv", now("tv")), ("ta", now("ta"))])
        E("tensor", lambda e: e.matmul(psA[:, 512:1024], ones1f[:],
                                       rowtmp[0:1, 512:1024], start=True, stop=True),
          inc="tt")
        E("vector", lambda e, tile=tile: e.tensor_copy(tile[:], psA[:]),
          waits=[("tt", now("tt"))], inc="tv")
    E("vector", lambda e: e.memset(stat_sum[:], 0.0))
    E("vector", lambda e: e.memset(stat_sq[:], 0.0), inc="tv")
    p4bc_tv = now("tv")

    st_ld = {}
    st_ta = {}
    st_tv = {}
    for t in range(NTO):
        bi = t % 2
        E("sync", dma(xbuf[:, bi * C:(bi + 1) * C], x1_sp[t]),
          waits=[("ld", x1_done_ld), ("ta", st_ta.get(t - 2, 0)),
                 ("tv", max(p4bc_tv if t < 2 else 0, st_tv.get(t - 2, 0)))],
          inc="ld", inc_by=16)
        st_ld[t] = now("ld")
        E("scalar", lambda e, bi=bi, t=t: e.activation(
            dsc[:], xbuf[:, bi * C:(bi + 1) * C], AF.Identity,
            accum_out=stat_sum[:, t:t + 1]), waits=[("ld", st_ld[t])])
        E("scalar", lambda e, bi=bi, t=t: e.activation(
            dsc[:], xbuf[:, bi * C:(bi + 1) * C], AF.Square,
            accum_out=stat_sq[:, t:t + 1]), inc="ta")
        st_ta[t] = now("ta")
        E("vector", lambda e, t=t: e.tensor_single_scalar(
            stat_m[:, t:t + 1], stat_sum[:, t:t + 1], r_inv, OP.mult),
          waits=[("ta", st_ta[t])])
        E("vector", lambda e, t=t: e.tensor_single_scalar(
            stat_sq[:, t:t + 1], stat_sq[:, t:t + 1], r_inv, OP.mult))
        E("vector", lambda e, t=t: e.tensor_tensor(
            stat_sum[:, t:t + 1], stat_m[:, t:t + 1], stat_m[:, t:t + 1], OP.mult))
        E("vector", lambda e, t=t: e.scalar_tensor_tensor(
            stat_sq[:, t:t + 1], stat_sq[:, t:t + 1], EPS, stat_sum[:, t:t + 1],
            OP.add, OP.subtract))
        E("vector", lambda e, t=t: e.reciprocal(stat_sum[:, t:t + 1],
                                                stat_sq[:, t:t + 1]), inc="tv")
        E("scalar", lambda e, t=t: e.activation(stat_r[:, t:t + 1],
                                                stat_sum[:, t:t + 1], AF.Sqrt),
          waits=[("tv", now("tv"))], inc="ta")
        E("vector", lambda e, t=t, bi=bi: e.tensor_scalar(
            h2sc[:], xbuf[:, bi * C:(bi + 1) * C], stat_m[:, t:t + 1],
            stat_r[:, t:t + 1], OP.subtract, OP.mult),
          waits=[("ta", now("ta"))])
        E("vector", lambda e: e.tensor_tensor(h2sc[:], h2sc[:], bc0[:], OP.mult))
        E("vector", lambda e: e.tensor_tensor(h2sc[:], h2sc[:], bc1[:], OP.add))
        E("vector", lambda e: e.tensor_reduce(wsc[:, 12:13], h2sc[:], AX.X, OP.max,
                                              apply_absolute_value=True))
        E("vector", lambda e: e.tensor_tensor(pmax[:, 0:1], pmax[:, 0:1],
                                              wsc[:, 12:13], OP.max), inc="tv")
        st_tv[t] = now("tv")
    E("tensor", lambda e: e.transpose(psD[0:1, 0:128], pmax[:, 0:1], eye32_s[:]),
      waits=[("tv", now("tv"))], inc="tt")
    E("vector", lambda e: e.tensor_copy(ccs[:], psD[0:1, 0:128]),
      waits=[("tt", now("tt"))], inc="tv")
    E("pool", lambda e: e.dma_start(out=cc1_in[0:1, :], in_=ccs[:]),
      waits=[("tv", now("tv"))], inc="tp", inc_by=16)
    E("pool", lambda e: e.collective_compute(
        "AllReduce", OP.max, replica_groups=[list(range(8))],
        ins=[cc1_in.ap().opt()], outs=[cc1_out.ap().opt()]), inc="cc")
    E("pool", lambda e: e.dma_start(out=ccs[:], in_=cc1_out[0:1, :]),
      waits=[("cc", now("cc"))], inc="tp", inc_by=16)
    E("vector", lambda e: e.tensor_reduce(scal[0:1, 0:1], ccs[:], AX.X, OP.max),
      waits=[("tp", now("tp"))], inc="tv")
    E("scalar", lambda e: e.activation(scal[0:1, 2:3], clip_s[0:1, 0:1], AF.Sigmoid),
      waits=[("tp", now("tp"))], inc="ta")
    E("vector", lambda e: e.tensor_tensor(scal[0:1, 4:5], scal[0:1, 0:1],
                                          scal[0:1, 2:3], OP.mult),
      waits=[("ta", now("ta"))])
    E("vector", lambda e: e.tensor_single_scalar(scal[0:1, 4:5], scal[0:1, 4:5],
                                                 1.0 / QMAX, OP.mult))
    E("vector", lambda e: e.reciprocal(scal[0:1, 5:6], scal[0:1, 4:5]), inc="tv")
    E("tensor", lambda e: e.matmul(psD[:, 0:2], ones1f[:], scal[0:1, 4:6],
                                   start=True, stop=True),
      waits=[("tv", now("tv"))], inc="tt")
    E("vector", lambda e: e.tensor_copy(scal_p[:, 0:2], psD[:, 0:2]),
      waits=[("tt", now("tt"))], inc="tv")
    if debug:
        E("sync", dma(dbg["scal"], scal[:]), waits=[("tv", now("tv"))],
          inc="ld", inc_by=16)

    xq_ev_ta = {}
    xqv_tv = {}
    xqfm3 = xqfm.rearrange("p (f tok) -> p f tok", f=NC)
    for t in range(NTO):
        bi = t % 2
        E("sync", dma(xbuf[:, bi * C:(bi + 1) * C], x1_sp[t]),
          waits=[("tv", max(now("tv") if t < 2 else 0, xqv_tv.get(t - 2, 0))),
                 ("tt", xq_tt.get(t - 2, 0))], inc="ld", inc_by=16)
        ld_t = now("ld")
        E("vector", lambda e, t=t, bi=bi: e.tensor_scalar(
            qsc32[:, bi * C:(bi + 1) * C], xbuf[:, bi * C:(bi + 1) * C],
            stat_m[:, t:t + 1], stat_r[:, t:t + 1], OP.subtract, OP.mult),
          waits=[("ld", ld_t)])
        E("vector", lambda e, bi=bi: e.tensor_tensor(
            qsc32[:, bi * C:(bi + 1) * C], qsc32[:, bi * C:(bi + 1) * C],
            bc0[:], OP.mult))
        E("vector", lambda e, bi=bi: e.tensor_tensor(
            qsc32[:, bi * C:(bi + 1) * C], qsc32[:, bi * C:(bi + 1) * C],
            bc1[:], OP.add))
        E("vector", lambda e, bi=bi: e.tensor_scalar(
            qsc32[:, bi * C:(bi + 1) * C], qsc32[:, bi * C:(bi + 1) * C],
            scal_p[:, 1:2], None, OP.mult))
        E("vector", lambda e, bi=bi: e.tensor_scalar(
            qsc32[:, bi * C:(bi + 1) * C], qsc32[:, bi * C:(bi + 1) * C],
            -QMAX - 1.0, QMAX, OP.max, OP.min))
        E("vector", lambda e, bi=bi: e.tensor_copy(qint[:], qsc32[:, bi * C:(bi + 1) * C]))
        E("vector", lambda e, bi=bi: e.tensor_copy(xqt[:, bi * C:(bi + 1) * C],
                                                   qint[:]), inc="tv")
        xqv_tv[t] = now("tv")
        if debug:
            E("sync", dma(dbg["xq"][t * 128:(t + 1) * 128, :],
                          xqt[:, bi * C:(bi + 1) * C]),
              waits=[("tv", now("tv"))], inc="ld", inc_by=16)
        for f in range(NC):
            E("tensor", lambda e, f=f, bi=bi: e.transpose(
                psAb[:, f * 128:(f + 1) * 128],
                xqt[:, bi * C + f * 128:bi * C + (f + 1) * 128], eye16_s[:]),
              waits=([("tv", now("tv")), ("ta", xq_ev_ta.get(t - 1, 0))]
                     if f == 0 else ()),
              inc="tt" if f == NC - 1 else None)
        xq_tt[t] = now("tt")
        E("scalar", lambda e, t=t: e.activation(
            xqfm3[:, :, t * 128:(t + 1) * 128],
            psAb[:, 0:1024].rearrange("p (a b) -> p a b", a=NC), AF.Identity),
          waits=[("tt", now("tt"))], inc="ta")
        xq_ev_ta[t] = now("ta")
    xq_ta = now("ta")

    g_tt = {}
    g_tv = {}
    pdma_ld = {}
    p5_ev_ta = {}
    p5_ev_tv = {}
    p5_stt_tv = {}
    for cch in range(NI):
        bi = cch % 2
        E("sync", dma(wgbuf[:, bi * C:(bi + 1) * C], wgT_sp[cch]),
          waits=[("ld", wgu_spill_ld), ("tt", g_tt.get(cch - 2, 0))],
          inc="ld", inc_by=16)
        E("sync", dma(wubuf[:, bi * C:(bi + 1) * C], wuT_sp[cch]), inc="ld", inc_by=16)
        wld = now("ld")
        E("vector", lambda e, cch=cch: e.tensor_tensor(
            wsc[:, 14:15], wgusc[:, cch:cch + 1], scal_p[:, 0:1], OP.mult),
          waits=[("ta", xq_ta)])
        E("vector", lambda e, cch=cch: e.tensor_tensor(
            wsc[:, 15:16], wgusc[:, NI + cch:NI + cch + 1], scal_p[:, 0:1],
            OP.mult), inc="tv")
        cgcu_tv = now("tv")
        for tb in range(4):
            lin = cch * 4 + tb
            ps = psA if lin % 2 == 0 else psB
            for gu in range(2):
                wb = wgbuf if gu == 0 else wubuf
                for hh in range(NC):
                    w = []
                    if gu == 0 and hh == 0:
                        w = [("ta", p5_ev_ta.get(lin - 2, 0)),
                             ("tv", p5_ev_tv.get(lin - 2, 0))]
                        if tb == 0:
                            w += [("ld", wld), ("ta", xq_ta),
                                  ("tv", g_tv.get(cch - 2, 0))]
                    E("tensor", lambda e, ps=ps, gu=gu, hh=hh, tb=tb, bi=bi, wb=wb:
                      e.matmul(ps[:, gu * 512:(gu + 1) * 512],
                               wb[:, bi * C + hh * 128:bi * C + (hh + 1) * 128],
                               xqfm[:, hh * 2048 + tb * 512:hh * 2048 + (tb + 1) * 512],
                               start=(hh == 0), stop=(hh == NC - 1)),
                      waits=w, inc="tt" if hh == NC - 1 else None)
            E("scalar", lambda e, ps=ps: e.activation(silu_t[:], ps[:, 0:512],
                                                      AF.Silu, scale=wsc[:, 14:15]),
              waits=[("tt", now("tt")), ("tv", max(cgcu_tv, p5_stt_tv.get(lin - 1, 0)))],
              inc="ta")
            p5_ev_ta[lin] = now("ta")
            E("vector", lambda e, ps=ps, lin=lin: e.scalar_tensor_tensor(
                p_t[:, (lin % 2) * 512:(lin % 2 + 1) * 512], ps[:, 512:1024],
                wsc[:, 15:16], silu_t[:], OP.mult, OP.mult),
              waits=[("ta", now("ta")), ("ld", pdma_ld.get(lin - 2, 0))])
            p5_stt_tv[lin] = now("tv")
            E("vector", lambda e, lin=lin: e.tensor_reduce(
                wsc[:, 13:14], p_t[:, (lin % 2) * 512:(lin % 2 + 1) * 512], AX.X,
                OP.max, apply_absolute_value=True))
            E("vector", lambda e: e.tensor_tensor(pmax[:, 1:2], pmax[:, 1:2],
                                                  wsc[:, 13:14], OP.max), inc="tv")
            p5_ev_tv[lin] = now("tv")
            E("sync", dma(p_sp.ap()[cch, :, tb * 512:(tb + 1) * 512],
                          p_t[:, (lin % 2) * 512:(lin % 2 + 1) * 512]),
              waits=[("tv", now("tv"))], inc="ld", inc_by=16)
            pdma_ld[lin] = now("ld")
        g_tt[cch] = now("tt")
        g_tv[cch] = now("tv")
    p_done_ld = now("ld")

    E("tensor", lambda e: e.transpose(psD[0:1, 0:128], pmax[:, 1:2], eye32_s[:]),
      waits=[("tv", now("tv"))], inc="tt")
    E("vector", lambda e: e.tensor_copy(ccs[:], psD[0:1, 0:128]),
      waits=[("tt", now("tt"))], inc="tv")
    E("pool", lambda e: e.dma_start(out=cc2_in[0:1, :], in_=ccs[:]),
      waits=[("tv", now("tv"))], inc="tp", inc_by=16)
    E("pool", lambda e: e.collective_compute(
        "AllReduce", OP.max, replica_groups=[list(range(8))],
        ins=[cc2_in.ap().opt()], outs=[cc2_out.ap().opt()]), inc="cc")
    E("pool", lambda e: e.dma_start(out=ccs[:], in_=cc2_out[0:1, :]),
      waits=[("cc", now("cc"))], inc="tp", inc_by=16)
    E("vector", lambda e: e.tensor_reduce(scal[0:1, 1:2], ccs[:], AX.X, OP.max),
      waits=[("tp", now("tp"))], inc="tv")
    E("scalar", lambda e: e.activation(scal[0:1, 3:4], clip_s[0:1, 1:2], AF.Sigmoid),
      waits=[("tp", now("tp"))], inc="ta")
    E("vector", lambda e: e.tensor_tensor(scal[0:1, 6:7], scal[0:1, 1:2],
                                          scal[0:1, 3:4], OP.mult),
      waits=[("ta", now("ta"))])
    E("vector", lambda e: e.tensor_single_scalar(scal[0:1, 6:7], scal[0:1, 6:7],
                                                 1.0 / QMAX, OP.mult))
    E("vector", lambda e: e.reciprocal(scal[0:1, 7:8], scal[0:1, 6:7]), inc="tv")
    E("tensor", lambda e: e.matmul(psD[:, 2:4], ones1f[:], scal[0:1, 6:8],
                                   start=True, stop=True),
      waits=[("tv", now("tv"))], inc="tt")
    E("vector", lambda e: e.tensor_copy(scal_p[:, 2:4], psD[:, 2:4]),
      waits=[("tt", now("tt"))], inc="tv")
    E("vector", lambda e: e.tensor_tensor(wsc[:, 2:10], wsc[:, 2:10],
                                          vec_fm[:, 40:48], OP.mult), inc="tv")
    E("tensor", lambda e: e.transpose(psD[0:8, 0:128], wsc[:, 2:10], eye32_s[:]),
      waits=[("tv", now("tv"))], inc="tt")
    E("vector", lambda e: e.tensor_copy(rowst[0:8, :], psD[0:8, 0:128]),
      waits=[("tt", now("tt"))], inc="tv")
    E("sync", dma(rowb[0:1, 0:1024], rowst[0:8, :]), waits=[("tv", now("tv"))],
      inc="ld", inc_by=16)
    E("sync", dma(rowtmp[:], rowb[0:1, 0:C]), waits=[("ld", now("ld"))],
      inc="ld", inc_by=16)
    E("vector", lambda e: e.tensor_scalar(rowtmp[:], rowtmp[:],
                                          scal[0:1, 6:7], None, OP.mult),
      waits=[("ld", now("ld"))], inc="tv")
    E("tensor", lambda e: e.matmul(psA[:, 0:512], ones1f[:], rowtmp[0:1, 0:512],
                                   start=True, stop=True),
      waits=[("tv", now("tv"))])
    E("tensor", lambda e: e.matmul(psA[:, 512:1024], ones1f[:],
                                   rowtmp[0:1, 512:1024], start=True, stop=True),
      inc="tt")
    E("vector", lambda e: e.tensor_copy(bc0[:], psA[:]),
      waits=[("tt", now("tt"))], inc="tv")
    coef_tv = now("tv")

    y_ld = {}
    p6_tt = {}
    p6_tv = {}
    pqt3 = pqt.rearrange("p (s c) -> p s c", s=2)
    for t in range(NTO):
        bi = t % 2
        pstg = pstg_s[bi]
        E("sync", dma(pstg.rearrange("p (c k) -> p c k", c=NI),
                      p_sp.ap().rearrange("c p k -> p c k")[:, :, t * 128:(t + 1) * 128]),
          waits=[("ld", p_done_ld), ("tv", p6_tv.get(t - 2, 0)),
                 ("tv", coef_tv if t == 0 else 0)],
          inc="ld", inc_by=16)
        ld_t = now("ld")
        E("vector", lambda e, pstg=pstg: e.tensor_scalar(
            pstg, pstg, scal_p[:, 3:4], None, OP.mult),
          waits=[("ld", ld_t), ("tt", p6_tt.get(t - 2, 0))])
        E("vector", lambda e, pstg=pstg: e.tensor_scalar(
            pstg, pstg, -QMAX - 1.0, QMAX, OP.max, OP.min))
        E("vector", lambda e, pstg=pstg: e.tensor_copy(wint6[:, 0:FI], pstg))
        E("vector", lambda e, bi=bi: e.tensor_copy(
            pqt3[:, bi, :], wint6[:, 0:FI]), inc="tv")
        p6_tv[t] = now("tv")
        ps = psA if bi == 0 else psB
        for sl in range(2):
            for cch in range(NI):
                w = []
                if sl == 0 and cch == 0:
                    w = [("tv", p6_tv[t]), ("tv", y_ld.get(("ev", t - 2), 0))]
                E("tensor", lambda e, ps=ps, sl=sl, cch=cch, bi=bi: e.matmul(
                    ps[:, sl * 512:(sl + 1) * 512],
                    pqt3[:, bi, cch * 128:(cch + 1) * 128],
                    wdT[:, cch * C + sl * 512:cch * C + (sl + 1) * 512],
                    start=(cch == 0), stop=(cch == NI - 1)),
                  waits=w, inc="tt" if cch == NI - 1 else None)
        p6_tt[t] = now("tt")
        E("sync", dma(xbuf[:, bi * C:(bi + 1) * C], x1_sp[t]),
          waits=[("tv", y_ld.get(("ev", t - 2), 0))], inc="ld", inc_by=16)
        xld = now("ld")
        E("vector", lambda e, ps=ps: e.tensor_tensor(dsc[:], ps[:], bc0[:], OP.mult),
          waits=[("tt", p6_tt[t]), ("ld", y_ld.get(t - 2, 0))])
        E("vector", lambda e, bi=bi: e.tensor_tensor(
            ysc[:, bi * C:(bi + 1) * C], dsc[:], xbuf[:, bi * C:(bi + 1) * C],
            OP.add), waits=[("ld", xld)], inc="tv")
        y_ld[("ev", t)] = now("tv")
        E("sync", dma(dy[t * 128:(t + 1) * 128, :], ysc[:, bi * C:(bi + 1) * C]),
          waits=[("tv", now("tv"))], inc="ld", inc_by=16)
        y_ld[t] = now("ld")

    E("sync", lambda e: e.nop(), waits=[("ld", now("ld"))])

    import contextlib
    stack = contextlib.ExitStack()
    sems = {s: stack.enter_context(nc.semaphore("sem_" + s)) for s in SEMS}
    block = stack.enter_context(nc.Block())

    def mk(engine_name):
        evs = [ev for ev in prog.events if ev[0] == engine_name]

        def body(eng):
            maxw = {k: 0 for k in SEMS}
            for _, fn, waits, inc, inc_by in evs:
                for sem, val in waits:
                    if val <= maxw[sem]:
                        continue
                    maxw[sem] = val
                    if engine_name == "tensor" and sem == "tt":
                        continue
                    eng.wait_ge(sems[sem], val)
                inst = fn(eng)
                if inc is not None:
                    assert inst is not None
                    inst.then_inc(sems[inc], inc_by)
        return body

    block.sync(mk("sync"))
    block.tensor(mk("tensor"))
    block.vector(mk("vector"))
    block.scalar(mk("scalar"))
    block.gpsimd(mk("pool"))
    stack.close()
    return nc


def _rope_tables():
    j32 = np.arange(0, D, 2, dtype=np.float32)
    theta32 = (np.float32(1.0) / (np.float32(10000.0) ** (j32 / np.float32(D)))
               ) * np.float32(3.0)
    pos32 = np.arange(L, dtype=np.float32)
    ang32 = (pos32[:, None] * theta32[None, :]).astype(np.float32)
    cos_hi = np.cos(ang32.astype(np.float64)).astype(np.float32)
    sin_hi = np.sin(ang32.astype(np.float64)).astype(np.float32)
    theta64 = (1.0 / (10000.0 ** (np.arange(0, D, 2, dtype=np.float64) / D))) * 3.0
    ang64 = np.arange(L, dtype=np.float64)[:, None] * theta64[None, :]
    cos_lo = (np.cos(ang64) - cos_hi).astype(np.float32)
    sin_lo = (np.sin(ang64) - sin_hi).astype(np.float32)
    return (np.tile(cos_hi, (1, 16)), np.tile(cos_lo, (1, 16)),
            np.tile(sin_hi, (1, 16)), np.tile(sin_lo, (1, 16)))


def _perm_qk():
    perm = np.arange(3 * C)
    for blk in range(2):
        for h in range(H):
            base = blk * C + h * D
            perm[base:base + 32] = base + np.arange(0, D, 2)
            perm[base + 32:base + D] = base + np.arange(1, D, 2)
    return perm


_NC_CACHE = {}


def kernel(L0_attn_x, L0_attn_w_qkv, L0_attn_w_proj, L0_attn_b_proj,
           L0_ffn_clip_x, L0_ffn_clip_down, L0_ffn_w_gate, L0_ffn_w_up,
           L0_ffn_w_down, L0_c, L0_wmod, _debug=False):
    x = np.ascontiguousarray(np.asarray(L0_attn_x, dtype=np.float32))
    wqkv = np.ascontiguousarray(np.asarray(L0_attn_w_qkv, dtype=np.float32))
    wproj = np.ascontiguousarray(np.asarray(L0_attn_w_proj, dtype=np.float32))
    bproj = np.asarray(L0_attn_b_proj, dtype=np.float32).reshape(1, C)
    wg = np.ascontiguousarray(np.asarray(L0_ffn_w_gate, dtype=np.float32))
    wu = np.ascontiguousarray(np.asarray(L0_ffn_w_up, dtype=np.float32))
    wd = np.ascontiguousarray(np.asarray(L0_ffn_w_down, dtype=np.float32))
    cv = np.asarray(L0_c, dtype=np.float32)
    wmod = np.ascontiguousarray(np.asarray(L0_wmod, dtype=np.float32))
    clips = np.array([[float(np.asarray(L0_ffn_clip_x).reshape(-1)[0]),
                       float(np.asarray(L0_ffn_clip_down).reshape(-1)[0])]],
                     dtype=np.float32)

    cos_hi, cos_lo, sin_hi, sin_lo = _rope_tables()
    tabs_base = np.ascontiguousarray(
        np.concatenate([cos_hi, cos_lo, sin_hi, sin_lo], axis=1))
    wqkv_p = np.ascontiguousarray(wqkv[:, _perm_qk()])
    eye16 = np.eye(128, dtype=BF)
    eye32 = np.eye(128, dtype=np.float32)
    sel = np.zeros((2, 128), np.float32)
    sel[0, 0:64] = 1.0
    sel[1, 64:128] = 1.0

    key = ("nc", bool(_debug))
    if key not in _NC_CACHE:
        _NC_CACHE[key] = build_nc(debug=_debug)
    ncobj = _NC_CACHE[key]

    in_maps = []
    for core in range(8):
        b, hf = core // 2, core % 2
        rot = hf * 2048
        xr = np.ascontiguousarray(np.roll(x[b], -rot, axis=0))
        tr = np.ascontiguousarray(np.roll(tabs_base, -rot, axis=0))
        in_maps.append({
            "x": xr, "wqkv": wqkv_p, "wproj": wproj, "bproj": bproj,
            "wmod": wmod, "cvec": np.ascontiguousarray(cv[b].reshape(NC, 128)),
            "clips": clips, "wg": wg, "wu": wu, "wd": wd,
            "tabs": tr, "eye16": eye16, "eye32": eye32, "sel": sel,
        })
    res = run_bass_kernel_spmd(ncobj, in_maps, core_ids=list(range(8)))
    out = np.empty((B, L, C), dtype=np.float32)
    for core in range(8):
        b, hf = core // 2, core % 2
        out[b, hf * 2048:(hf + 1) * 2048, :] = res.results[core]["y"]
    if _debug:
        return out, res
    return out
